# version 10
# baseline (speedup 1.0000x reference)
import math
from contextlib import ExitStack

import numpy as np
import ml_dtypes

import concourse.bass as bass
import concourse.mybir as mybir
from concourse.bass_utils import run_bass_kernel_spmd
from concourse.alu_op_type import AluOpType as OP

F32 = mybir.dt.float32
BF16 = mybir.dt.bfloat16
U32 = mybir.dt.uint32
AF = mybir.ActivationFunctionType
AX = mybir.AxisListType

NCORES = 8
DM = 1024
DI = 2048
DST = 128
NH = 32
HD = 64
CONVD = 2304
DIP = 4384
CH = 128
EPS = 1e-5
NEG = -1e30

BF = np.dtype(ml_dtypes.bfloat16)



class Sem:
    def __init__(self, handle):
        self.h = handle
        self.v = 0


class P:

    ENGS = ("sync", "vector", "scalar", "tensor", "gpsimd")

    def __init__(self, name="prog"):
        self.nc = bass.Bass("TRN2", target_bir_lowering=False)
        self.es = ExitStack()
        self.ops = []
        self.name = name
        self._uid = 0
        self.sems = {}
        self.scopes = []

    def uid(self):
        self._uid += 1
        return self._uid

    def dram(self, name, shape, dtype=F32, kind="ExternalInput"):
        return self.nc.dram_tensor(name, list(shape), dtype, kind=kind).ap()

    def _alloc_es(self):
        return self.scopes[-1] if self.scopes else self.es

    def push_scope(self):
        self.scopes.append(ExitStack())

    def pop_scope(self):
        self.scopes.pop().close()

    def sb(self, name, shape, dtype=F32):
        return self._alloc_es().enter_context(
            self.nc.sbuf_tensor(f"{name}_{self.uid()}" if self.scopes else name,
                                list(shape), dtype))

    def ps(self, name, shape, dtype=F32):
        return self._alloc_es().enter_context(
            self.nc.psum_tensor(f"{name}_{self.uid()}" if self.scopes else name,
                                list(shape), dtype))

    def sem(self, name=None):
        if name is not None and name in self.sems:
            return self.sems[name]
        s = Sem(self.es.enter_context(self.nc.semaphore(name or f"s{self.uid()}")))
        if name is not None:
            self.sems[name] = s
        return s

    def barrier(self):
        pairs = [(s.h, s.v) for n, s in self.sems.items()
                 if s.v > 0 and n != "cc"]

        def mk(e):
            for h, v in pairs:
                e.wait_ge(h, v)
        for en in self.ENGS:
            self.emit(en, mk)

    def emit(self, eng, fn):
        self.ops.append((eng, fn))

    def op(self, eng, method, *args, inc=None, incn=1, waits=(), **kw):
        if inc is not None:
            inc.v += incn
        wl = [(s.h, v) for (s, v) in waits]
        incs = None if inc is None else (inc.h, incn)

        def fn(e):
            for h, v in wl:
                e.wait_ge(h, v)
            try:
                ins = getattr(e, method)(*args, **kw)
            except Exception:
                import sys
                def nm(v):
                    t = getattr(v, "tensor", None)
                    return getattr(t, "name", None) or str(type(v).__name__)
                print(f"P.op FAILED: {method} kw={[(k, nm(v)) for k, v in kw.items()]}",
                      file=sys.stderr)
                raise
            if incs is not None:
                ins.then_inc(*incs)
        self.emit(eng, fn)
        return None if inc is None else inc.v

    def wait(self, eng, sem, val=None):
        v = sem.v if val is None else val
        self.emit(eng, lambda e: e.wait_ge(sem.h, v))

    def drain(self, eng):
        self.emit(eng, lambda e: e.drain())

    def build(self):
        block = self.es.enter_context(self.nc.Block())
        ops = self.ops

        def mk(engname):
            def _f(e):
                for en, fn in ops:
                    if en == engname:
                        fn(e)
            return _f
        for en in self.ENGS:
            getattr(block, en)(mk(en))
        self.es.close()
        return self.nc


def ap_v(t, offset, ap):
    return bass.AP(tensor=t, offset=offset, ap=[list(x) for x in ap])


RG8 = [list(range(NCORES))]


def emit_ag(p, in_ap, out_ap):
    cc = p.sem("cc")
    p.op("gpsimd", "collective_compute", "AllGather", OP.bypass, RG8,
         ins=[in_ap], outs=[out_ap], inc=cc)
    return cc.v


def wait_cc(p, val):
    cc = p.sem("cc")
    for en in P.ENGS:
        p.wait(en, cc, val)



def emit_row_bcast(p, row_ap, n, psum, out_sb, ones_r, mm, dv, waits=()):
    nb = (n + 511) // 512
    for i in range(nb):
        c0, c1 = i * 512, min(n, (i + 1) * 512)
        w = list(waits) if i == 0 else []
        if i > 0:
            w.append((dv, dv.v))
        p.op("tensor", "matmul", psum[:, 0:c1 - c0], ones_r[0:1, 0:128],
             row_ap[0:1, c0:c1], start=True, stop=True, waits=w, inc=mm)
        p.op("vector", "tensor_copy", out_sb[:, c0:c1], psum[:, 0:c1 - c0],
             waits=[(mm, mm.v)], inc=dv)
    p.drain("vector")


def emit_ln_tile(p, xt, nP, n, outt, sq_scr, c1, c2, dv, sc, epscol, w_bcast=None,
                 out2=None, c3=None, subtract_mean=True):
    if subtract_mean:
        p.op("vector", "tensor_reduce", c1[0:nP, :], xt[0:nP, 0:n], AX.X, OP.add, inc=dv)
        p.drain("vector")
        p.op("vector", "tensor_scalar", c1[0:nP, :], c1[0:nP, :], 1.0 / n, None, OP.mult, inc=dv)
        p.drain("vector")
        p.op("vector", "tensor_scalar", xt[0:nP, 0:n], xt[0:nP, 0:n], c1[0:nP, 0:1], None,
             OP.subtract, inc=dv)
    p.op("scalar", "activation", sq_scr[0:nP, 0:n], xt[0:nP, 0:n], AF.Square,
         accum_out=c2[0:nP, :], waits=[(dv, dv.v)], inc=sc)
    p.drain("scalar")
    p.op("scalar", "activation", c2[0:nP, :], c2[0:nP, :], AF.Ln, scale=1.0 / n, bias=epscol[0:nP, 0:1], inc=sc)
    p.drain("scalar")
    p.op("scalar", "activation", c2[0:nP, :], c2[0:nP, :], AF.Exp, scale=-0.5, inc=sc)
    p.op("vector", "tensor_scalar", outt[0:nP, 0:n], xt[0:nP, 0:n], c2[0:nP, 0:1], None,
         OP.mult, waits=[(sc, sc.v)], inc=dv)
    if out2 is not None:
        p.op("scalar", "activation", sq_scr[0:nP, 0:n], outt[0:nP, 0:n], AF.Square,
             accum_out=c3[0:nP, :], waits=[(dv, dv.v)], inc=sc)
        p.drain("scalar")
        p.op("scalar", "activation", c3[0:nP, :], c3[0:nP, :], AF.Ln, scale=1.0 / n, bias=epscol[0:nP, 0:1], inc=sc)
        p.drain("scalar")
        p.op("scalar", "activation", c3[0:nP, :], c3[0:nP, :], AF.Exp, scale=-0.5, inc=sc)
        p.op("vector", "scalar_tensor_tensor", out2[0:nP, 0:n], outt[0:nP, 0:n],
             c3[0:nP, 0:1], w_bcast[0:nP, 0:n], OP.mult, OP.mult,
             waits=[(sc, sc.v)], inc=dv)



def emit_l0_pre(p, io, T=2048, parts="all"):
    NT = T // 128
    x, xh, nw = io["x"], io["xh"], io["nw"]
    xn_o, xnT_o = io["xn"], io["xnT"]
    x2Tf_o, x2Tb_o = io["x2Tf"], io["x2Tb"]
    x2hTf_o, x2hTb_o = io["x2hTf"], io["x2hTb"]

    xbuf = p.sb("xbuf", [128, NT * DM])
    x2buf = p.sb("x2buf", [128, NT * DM])
    hbuf = p.sb("hbuf", [8, 2 * DM])
    wrow = p.sb("wrow", [1, DM])
    wb = p.sb("wb", [128, DM])
    c1 = p.sb("c1", [128, 1])
    c2 = p.sb("c2", [128, 1])
    c3 = p.sb("c3", [128, 1])
    epscol = p.sb("epscol", [128, 1])
    ones_r = p.sb("ones_r", [1, 128])
    identf = p.sb("identf", [128, 128])
    iotf = p.sb("iotf", [128, 128])
    stageN = p.sb("stageN", [128, T], BF16)
    stageF = p.sb("stageF", [128, T], BF16)
    stageB = p.sb("stageB", [128, T], BF16)
    hstg = p.sb("hstg", [128, 8], BF16)
    pw = p.ps("pw", [128, 1024])

    ld = p.sem("ld")
    dv = p.sem("dv")
    gp = p.sem("gp")
    sc = p.sem("sc")
    mm = p.sem("mm")
    st = p.sem("st")
    stN = p.sem("stN")
    stF = p.sem("stF")
    stB = p.sem("stB")
    stH = p.sem("stH")

    p.op("sync", "dma_start", out=xbuf[:, :].rearrange("p (t d) -> p t d", d=DM),
         in_=x.rearrange("(t p) d -> p t d", p=128), inc=ld, incn=16)
    p.op("sync", "dma_start", out=hbuf[:, 0:DM], in_=xh[:, :], inc=ld, incn=16)
    p.op("sync", "dma_start", out=wrow[:, :], in_=nw[:, :], inc=ld, incn=16)
    p.op("gpsimd", "memset", ones_r[:, :], 1.0, inc=gp)
    p.op("gpsimd", "memset", epscol[:, :], EPS, inc=gp)
    p.op("gpsimd", "iota", iotf[:, :], pattern=[[1, 128]], base=0,
         channel_multiplier=-1, allow_small_or_imprecise_dtypes=True, inc=gp)
    p.op("vector", "tensor_scalar", identf[:, :], iotf[:, :], 0.0, None, OP.is_equal,
         waits=[(gp, gp.v)], inc=dv)
    p.drain("vector")
    p.wait("scalar", gp, gp.v)
    emit_row_bcast(p, wrow, DM, pw, wb, ones_r, mm, dv,
                   waits=[(gp, gp.v), (ld, ld.v)])

    for t in range(NT):
        xt = ap_v(xbuf, t * DM, [[NT * DM, 128], [1, DM]])
        x2t = ap_v(x2buf, t * DM, [[NT * DM, 128], [1, DM]])
        emit_ln_tile(p, xt, 128, DM, xt, x2t, c1, c2, dv, sc, epscol,
                     w_bcast=wb, out2=x2t, c3=c3)
        p.op("sync", "dma_start",
             out=xn_o.rearrange("(t p) d -> p t d", p=128)[:, t, :],
             in_=xt, waits=[(dv, dv.v)], inc=st, incn=16)
    ht = ap_v(hbuf, 0, [[2 * DM, 8], [1, DM]])
    h2 = ap_v(hbuf, DM, [[2 * DM, 8], [1, DM]])
    emit_ln_tile(p, ht, 8, DM, ht, h2, c1, c2, dv, sc, epscol, w_bcast=wb, out2=h2, c3=c3)
    ln_done_dv = dv.v
    ln_done_sc = sc.v

    gt = [Guard(), Guard()]
    gsN = Guard()
    gsF = Guard()
    gsB = Guard()
    p.wait("tensor", dv, ln_done_dv)
    p.wait("tensor", sc, ln_done_sc)
    for k in range(0 if parts == "ln" else 8):
        for t in range(NT):
            ri = t % 2
            reg = ri * 512
            p.op("tensor", "transpose", pw[:, reg:reg + 128],
                 ap_v(x2buf, t * DM + k * 128, [[NT * DM, 128], [1, 128]]),
                 identf[:, :], waits=gt[ri].need_w(), inc=mm)
            gt[ri].did_w(mm)
            p.op("vector", "tensor_copy", stageF[:, t * 128:(t + 1) * 128],
                 pw[:, reg:reg + 128],
                 waits=[(mm, mm.v)] + (gsF.need_w() if t == 0 else []), inc=dv)
            p.drain("vector")
            if parts not in ("tf",):
                p.op("vector", "tensor_copy",
                     ap_v(stageB, (NT - 1 - t) * 128 + 127, [[T, 128], [-1, 128]]),
                     pw[:, reg:reg + 128],
                     waits=(gsB.need_w() if t == 0 else []), inc=dv)
                p.drain("vector")
            gt[ri].did_r(dv)
            if parts not in ("tf", "tfb"):
                p.op("tensor", "transpose", pw[:, reg:reg + 128],
                     ap_v(xbuf, t * DM + k * 128, [[NT * DM, 128], [1, 128]]),
                     identf[:, :], waits=gt[ri].need_w() + [(dv, dv.v)], inc=mm)
                gt[ri].did_w(mm)
                p.op("vector", "tensor_copy", stageN[:, t * 128:(t + 1) * 128],
                     pw[:, reg:reg + 128],
                     waits=[(mm, mm.v)] + (gsN.need_w() if t == 0 else []), inc=dv)
                p.drain("vector")
                gt[ri].did_r(dv)
        gsF.did_w(dv)
        gsB.did_w(dv)
        gsN.did_w(dv)
        if parts not in ("tf", "tfb"):
            p.op("sync", "dma_start", out=xnT_o[k * 128:(k + 1) * 128, :],
                 in_=stageN[:, :], waits=[(dv, dv.v)], inc=stN, incn=16)
            gsN.did_r(stN)
        p.op("sync", "dma_start", out=x2Tf_o[k * 128:(k + 1) * 128, :],
             in_=stageF[:, :], waits=[(dv, dv.v)], inc=stF, incn=16)
        gsF.did_r(stF)
        if parts not in ("tf",):
            p.op("sync", "dma_start", out=x2Tb_o[k * 128:(k + 1) * 128, :],
                 in_=stageB[:, :], inc=stB, incn=16)
            gsB.did_r(stB)
        if parts != "all":
            continue
        p.op("gpsimd", "memset", hstg[:, :], 0.0,
             waits=[(stH, stH.v)] if k > 0 else [], inc=gp)
        p.op("tensor", "transpose", pw[:, 0:8],
             ap_v(hbuf, DM + k * 128, [[2 * DM, 8], [1, 128]]),
             identf[0:8, 0:8], waits=gt[0].need_w() + [(gp, gp.v)], inc=mm)
        gt[0].did_w(mm)
        p.op("vector", "tensor_copy", hstg[:, 0:8], pw[:, 0:8],
             waits=[(mm, mm.v)], inc=dv)
        p.drain("vector")
        gt[0].did_r(dv)
        p.op("sync", "dma_start", out=x2hTf_o[k * 128:(k + 1) * 128, 0:3],
             in_=hstg[:, 0:3], waits=[(dv, dv.v)], inc=stH, incn=16)
        p.op("sync", "dma_start", out=x2hTf_o[k * 128:(k + 1) * 128, 3:4],
             in_=hstg[:, 3:4], allow_slow_non_contiguous=True, inc=stH, incn=16)
        p.op("sync", "dma_start", out=x2hTb_o[k * 128:(k + 1) * 128, 0:3],
             in_=hstg[:, 4:7], inc=stH, incn=16)
        p.op("sync", "dma_start", out=x2hTb_o[k * 128:(k + 1) * 128, 3:4],
             in_=hstg[:, 3:4], allow_slow_non_contiguous=True, inc=stH, incn=16)
    for s2 in (st, stN, stF, stB, stH, ld):
        p.wait("sync", s2, s2.v)


def build_l0_pre(T=2048):
    p = P("l0_pre")
    io = {
        "x": p.dram("x", [T, DM]),
        "xh": p.dram("xh", [8, DM]),
        "nw": p.dram("nw", [1, DM]),
        "xn": p.dram("xn", [T, DM], kind="ExternalOutput"),
        "xnT": p.dram("xnT", [DM, T], BF16, kind="ExternalOutput"),
        "x2Tf": p.dram("x2Tf", [DM, T], BF16, kind="ExternalOutput"),
        "x2Tb": p.dram("x2Tb", [DM, T], BF16, kind="ExternalOutput"),
        "x2hTf": p.dram("x2hTf", [DM, 4], BF16, kind="ExternalOutput"),
        "x2hTb": p.dram("x2hTb", [DM, 4], BF16, kind="ExternalOutput"),
    }
    emit_l0_pre(p, io, T)
    return p.build()




class Guard:
    def __init__(self):
        self.rd = {}
        self.wr = None

    def need_w(self):
        out = [(s, v) for s, v in self.rd.items()]
        if self.wr is not None:
            out.append(self.wr)
        return out

    def need_r(self):
        return [] if self.wr is None else [self.wr]

    def did_w(self, sem):
        self.wr = (sem, sem.v)
        self.rd = {}

    def did_r(self, sem):
        self.rd[sem] = sem.v


def emit_mamba_local(p, io, T=2048, tag="f", debug=False):
    NCH = T // CH
    NT1 = max(1, T // 512)
    TT1 = min(512, T)
    NTB = T // 128
    TS = T + 4

    x2T = io["x2T"]
    x2hT = io["x2hT"]
    wInT = io["wInT"]
    cw = io["cw"]
    cbr = io["cbr"]
    dtb = io["dtb"]
    alog = io["alog"]
    dexpr = io["dexpr"]
    yloc = io["yloc"]
    ztm = io["ztm"]
    cfm_o = io["cfm"]
    agT_o = io["agT"]
    sout_o = io["sout"]
    xbcpre = p.dram("xbcpre_" + tag, [CONVD, TS], BF16, kind="Internal")
    abscr = p.dram("abscr_" + tag, [2, NH, T], kind="Internal")

    A1 = max(8 * DIP, 18 * T)
    arena1 = p.sb("arena1", [128, A1 + 2 * DI + 1024], BF16)
    A1s = arena1.shape[1]

    def a1v(off, n):
        return bass.AP(tensor=arena1, offset=off, ap=[[A1s, 128], [1, n]])
    W16o = 0
    CV16o = 0
    A2 = max(8 * 2052, 2 * TS + NH * CH * 2 + 1024 + DI)
    arena2 = p.sb("arena2", [128, A2], BF16)

    def a2v(off, n):
        return bass.AP(tensor=arena2, offset=off, ap=[[A2, 128], [1, n]])
    CV16 = a1v(0, 18 * T)
    zst = a1v(A1, 2 * DI)
    xst = a1v(A1 + 2 * DI, 1024)
    X16o = 0
    cvino = 0
    eDo = 2 * TS
    scoresTo = eDo
    tmp2o = eDo + NH * CH
    Dbco = tmp2o + 1024
    eD = a2v(eDo, NH * CH)
    scoresT = a2v(scoresTo, NH * CH)
    tmp2 = a2v(tmp2o, 1024)
    Dbc = a2v(Dbco, DI)
    cacc = p.sb("cacc", [128, 1024])
    tmp1 = bass.AP(tensor=cacc, offset=0, ap=[[1024, 128], [1, 512]])
    iot = bass.AP(tensor=cacc, offset=512, ap=[[1024, 128], [1, 128]])
    wch = bass.AP(tensor=cacc, offset=640, ap=[[1024, NH], [1, CH]])
    onesNH = bass.AP(tensor=cacc, offset=768, ap=[[1024, NH], [1, CH]])
    dtraw = p.sb("dtraw", [NH, T])
    lndt = p.sb("lndt", [NH, T])
    arel = p.sb("arel", [NH, T])
    scr32 = p.sb("scr32", [NH, T])
    btile = scr32
    aoff = p.sb("aoff", [NH, 1])
    dtbc = p.sb("dtbc", [NH, 1])
    acol = p.sb("acol", [NH, 1])

    S16 = p.sb("S16", [128, DI], BF16)
    xs_tm = p.sb("xs_tm", [128, DI], BF16)
    btm = p.sb("btm", [128, DST], BF16)
    wx = p.sb("wx", [128, DI], BF16)
    cbt = p.sb("cbt", [128, CH])
    aTt = p.sb("aTt", [128, NH])
    waT = p.sb("waT", [128, NH])
    expaT = p.sb("expaT", [128, NH])
    Pb = p.sb("Pb", [128, NH])
    prow = p.sb("prow", [1, NH])
    brow = p.sb("brow", [1, NH * CH])
    arow = p.sb("arow", [1, NH * CH])

    ybuf = p.sb("ybuf", [128, DI])
    onesr = p.sb("onesr", [1, 128])

    onec = p.sb("onec", [NH, 1])
    identf = p.sb("identf", [128, 128])
    ident = p.sb("identb", [128, 128], BF16)
    vneg = p.sb("vneg", [128, 128], BF16)

    cbcol = p.sb("cbcol", [128, 18])
    cwt = p.sb("cwt", [128, 4 * 18])

    pd = p.ps("pd", [128, 2048])
    py = p.ps("py", [128, 1024])
    pt = p.ps("pt", [128, 1024], BF16)
    pt2 = p.ps("pt2", [128, 1024], BF16)

    ld = p.sem("ld")
    lg = p.sem("lg")
    ab0 = p.sem("ab0")
    ab1 = p.sem("ab1")
    shf = p.sem("shf")
    dv = p.sem("dv")
    sc = p.sem("sc")
    gp = p.sem("gp")
    mm = p.sem("mm")
    so = p.sem("so")
    sx0 = p.sem("sx0")
    sx1 = p.sem("sx1")
    sz0 = p.sem("sz0")
    sz1 = p.sem("sz1")
    s2s = p.sem("s2s")
    sy = p.sem("sy")
    cvl0 = p.sem("cvl0")
    cvl1 = p.sem("cvl1")

    gA = Guard()
    gB = Guard()
    gy0 = Guard()
    gy1 = Guard()
    gt0 = Guard()
    gt1 = Guard()
    gz = [Guard(), Guard()]
    gx = [Guard(), Guard()]
    gcv = [Guard(), Guard()]

    p.op("gpsimd", "memset", onesr[:, :], 1.0, inc=gp)
    p.op("gpsimd", "memset", onec[:, :], 1.0, inc=gp)
    p.op("gpsimd", "memset", onesNH[:, :], 1.0, inc=gp)

    p.op("gpsimd", "memset", S16[:, :], 0.0, inc=gp)
    p.op("gpsimd", "memset", a2v(X16o, 8 * 2052), 0.0, inc=gp)
    p.op("gpsimd", "iota", iot[:, :], pattern=[[1, 128]], base=0,
         channel_multiplier=-1, allow_small_or_imprecise_dtypes=True, inc=gp)
    p.op("vector", "tensor_scalar", ident[:, :], iot[:, :], 0.0, None, OP.is_equal,
         waits=[(gp, gp.v)], inc=dv)
    p.drain("vector")
    p.op("vector", "tensor_scalar", identf[:, :], iot[:, :], 0.0, None, OP.is_equal, inc=dv)
    p.drain("vector")
    p.op("vector", "tensor_scalar", vneg[:, :], iot[:, :], 0.0, None, OP.is_lt, inc=dv)
    p.drain("vector")
    p.op("vector", "tensor_scalar", vneg[:, :], vneg[:, :], float(NEG), None, OP.mult, inc=dv)
    p.drain("vector")
    v_const = dv.v

    p.op("gpsimd", "dma_start",
         out=a1v(W16o, 8 * DIP).rearrange("p (k o) -> p k o", k=8),
         in_=wInT.rearrange("(k p) o -> p k o", p=128), inc=lg, incn=16)
    p.op("gpsimd", "dma_start",
         out=a2v(X16o, 8 * 2052).rearrange("p (k c) -> p k c", k=8)[:, :, 4:4 + T],
         in_=x2T.rearrange("(k p) t -> p k t", p=128),
         waits=[(gp, gp.v)], inc=lg, incn=16)
    p.op("gpsimd", "dma_start",
         out=a2v(X16o, 8 * 2052).rearrange("p (k c) -> p k c", k=8)[:, :, 0:4],
         in_=x2hT.rearrange("(k p) t -> p k t", p=128), inc=lg, incn=16)
    p.op("sync", "dma_start", out=cbcol[:, :], in_=cbr.rearrange("o p -> p o"),
         allow_slow_non_contiguous=True, inc=ld, incn=16)
    p.op("sync", "dma_start", out=cwt[:, :].rearrange("p (k o) -> p k o", k=4),
         in_=cw.rearrange("(o p) k -> p k o", p=128), allow_slow_non_contiguous=True,
         inc=ld, incn=16)
    p.op("sync", "dma_start", out=acol[:, :], in_=alog[:, :], inc=ld, incn=16)
    p.op("sync", "dma_start", out=dtbc[:, :], in_=dtb[:, :], inc=ld, incn=16)
    p.op("sync", "dma_start", out=ybuf[0:1, :], in_=dexpr[:, :], inc=ld, incn=16)
    ld_all = ld.v

    p.wait("tensor", lg, lg.v)
    p.wait("tensor", dv, v_const)

    def xw(kb, c0, c1):
        return a2v(X16o + kb * 2052 + c0, c1 - c0)

    def wsl(kb, o0, o1):
        return a1v(W16o + kb * DIP + o0, o1 - o0)

    regs = [(0, gA), (512, gA)]
    gR = [Guard(), Guard()]
    for oi in range(16, 35):
        o0 = oi * 128
        nP = 128 if oi < 34 else 32
        for tb in range(NT1):
            t0 = tb * TT1
            ri = tb % 2
            reg = ri * 512
            for kb in range(8):
                p.op("tensor", "matmul", pd[0:nP, reg:reg + TT1],
                     wsl(kb, o0, o0 + nP), xw(kb, 4 + t0, 4 + t0 + TT1),
                     start=(kb == 0), stop=(kb == 7),
                     waits=gR[ri].need_w() if kb == 0 else (), inc=mm)
            gR[ri].did_w(mm)
            if oi < 34:
                xi = tb % 2
                p.op("vector", "tensor_copy", xst[:, xi * 512:xi * 512 + TT1],
                     pd[:, reg:reg + TT1],
                     waits=[(mm, mm.v)] + gx[xi].need_w(), inc=dv)
                gR[ri].did_r(dv)
                gx[xi].did_w(dv)
                sxs = sx0 if xi == 0 else sx1
                p.op("sync", "dma_start",
                     out=xbcpre[(oi - 16) * 128:(oi - 15) * 128, 3 + t0:3 + t0 + TT1],
                     in_=xst[:, xi * 512:xi * 512 + TT1], waits=[(dv, dv.v)],
                     inc=sxs, incn=16)
                gx[xi].did_r(sxs)
            else:
                p.op("vector", "tensor_copy", dtraw[:, t0:t0 + TT1],
                     pd[0:32, reg:reg + TT1], waits=[(mm, mm.v)], inc=dv)
                gR[ri].did_r(dv)
    gH = Guard()
    for oi in range(16, 34):
        o0 = oi * 128
        for kb in range(8):
            p.op("tensor", "matmul", pd[:, 1024:1028],
                 wsl(kb, o0, o0 + 128), xw(kb, 0, 4),
                 start=(kb == 0), stop=(kb == 7),
                 waits=gH.need_w() if kb == 0 else (), inc=mm)
        gH.did_w(mm)
        xi = oi % 2
        p.op("vector", "tensor_copy", xst[:, xi * 512:xi * 512 + 4], pd[:, 1024:1028],
             waits=[(mm, mm.v)] + gx[xi].need_w(), inc=dv)
        gH.did_r(dv)
        gx[xi].did_w(dv)
        sxs = sx0 if xi == 0 else sx1
        p.op("sync", "dma_start", out=xbcpre[(oi - 16) * 128:(oi - 15) * 128, 0:3],
             in_=xst[:, xi * 512:xi * 512 + 3], waits=[(dv, dv.v)], inc=sxs, incn=16)
        gx[xi].did_r(sxs)
    gY = [gy0, gy1]
    for tb in range(NTB):
        zi = tb % 2
        for ob in range(4):
            o0 = ob * 512
            ri = ob % 2
            reg = ri * 512
            for kb in range(8):
                p.op("tensor", "matmul", py[:, reg:reg + 512],
                     xw(kb, 4 + tb * 128, 4 + tb * 128 + 128),
                     wsl(kb, o0, o0 + 512),
                     start=(kb == 0), stop=(kb == 7),
                     waits=gY[ri].need_w() if kb == 0 else (), inc=mm)
            gY[ri].did_w(mm)
            p.op("scalar", "activation", zst[:, zi * DI + o0:zi * DI + o0 + 512],
                 py[:, reg:reg + 512], AF.Silu,
                 waits=[(mm, mm.v)] + (gz[zi].need_w() if ob == 0 else []), inc=sc)
            gY[ri].did_r(sc)
            gz[zi].did_w(sc)
        szs = sz0 if zi == 0 else sz1
        p.op("sync", "dma_start", out=ztm[tb * 128:(tb + 1) * 128, :],
             in_=zst[:, zi * DI:(zi + 1) * DI], waits=[(sc, sc.v)], inc=szs, incn=16)
        gz[zi].did_r(szs)
    st1_mm = mm.v
    st1_sx = (sx0.v, sx1.v)

    p.drain("vector")
    p.op("vector", "tensor_scalar", dtraw[:, :], dtraw[:, :], dtbc[0:NH, 0:1], None,
         OP.add, waits=[(ld, ld_all)], inc=dv)
    p.drain("vector")
    p.op("scalar", "activation", scr32[:, :], dtraw[:, :], AF.Abs,
         waits=[(dv, dv.v)], inc=sc)
    p.drain("scalar")
    p.op("scalar", "activation", scr32[:, :], scr32[:, :], AF.Exp, scale=-1.0, inc=sc)
    p.drain("scalar")
    p.op("scalar", "activation", scr32[:, :], scr32[:, :], AF.Ln, bias=onec[0:NH, 0:1], inc=sc)
    p.drain("scalar")
    p.op("vector", "tensor_scalar", lndt[:, :], dtraw[:, :], 0.0, None, OP.max, inc=dv)
    p.drain("vector")
    p.op("vector", "tensor_tensor", dtraw[:, :], lndt[:, :], scr32[:, :], OP.add,
         waits=[(sc, sc.v)], inc=dv)
    p.drain("vector")
    p.op("scalar", "activation", lndt[:, :], dtraw[:, :], AF.Ln, waits=[(dv, dv.v)], inc=sc)
    p.drain("scalar")
    p.op("scalar", "activation", acol[:, :], acol[:, :], AF.Exp, inc=sc)
    p.drain("scalar")
    p.op("scalar", "mul", acol[:, :], acol[:, :], -1.0, inc=sc)
    p.drain("scalar")
    p.op("vector", "tensor_scalar", scr32[:, :], dtraw[:, :], acol[0:NH, 0:1], None,
         OP.mult, waits=[(sc, sc.v)], inc=dv)
    p.drain("vector")
    for c in range(NCH):
        s0 = c * CH
        p.op("vector", "tensor_tensor_scan", arel[:, s0:s0 + CH],
             onesNH[:, :], scr32[:, s0:s0 + CH], 0.0, OP.mult, OP.add, inc=dv)
        p.drain("vector")
    p.op("gpsimd", "memset", aoff[:, :], 0.0, waits=[(ld, ld_all)], inc=gp)
    for c in range(NCH):
        s0 = c * CH
        p.op("vector", "tensor_scalar", dtraw[:, s0:s0 + CH], arel[:, s0:s0 + CH],
             aoff[0:NH, 0:1], None, OP.add, waits=[(gp, gp.v)], inc=dv)
        p.drain("vector")
        p.op("vector", "tensor_copy", aoff[:, :], dtraw[:, s0 + CH - 1:s0 + CH], inc=dv)
        p.drain("vector")
    p.op("vector", "tensor_tensor", btile[:, :], lndt[:, :], arel[:, :], OP.subtract,
         inc=dv)
    p.drain("vector")
    p.op("sync", "dma_start", out=abscr[0, :, :], in_=arel[:, :], waits=[(dv, dv.v)],
         inc=s2s, incn=16)
    p.op("sync", "dma_start", out=abscr[1, :, :], in_=btile[:, :], waits=[(dv, dv.v)],
         inc=s2s, incn=16)
    st2_so = s2s.v
    st2_dv = dv.v

    p.wait("tensor", mm, st1_mm)
    for i in range(4):
        p.op("tensor", "matmul", pd[:, i * 512:(i + 1) * 512], onesr[0:1, 0:128],
             ybuf[0:1, i * 512:(i + 1) * 512], start=True, stop=True,
             waits=(gR[0].need_w() + gR[1].need_w() + gH.need_w() + [(ld, ld_all)])
             if i == 0 else (), inc=mm)
    gR[0].did_w(mm)
    gR[1].did_w(mm)
    gH.did_w(mm)
    p.op("vector", "tensor_copy", Dbc[:, :], pd[:, 0:2048],
         waits=[(mm, mm.v), (lg, lg.v)], inc=dv)
    p.drain("vector")
    gR[0].did_r(dv)
    gR[1].did_r(dv)
    gH.did_r(dv)
    CB = min(1024, T)
    NB = T // CB
    Hh = CB // 2
    gcc = [Guard(), Guard()]
    for oi in range(18):
        ci = oi % 2
        cv = cvl0 if ci == 0 else cvl1
        p.op("sync", "dma_start", out=a2v(cvino + ci * TS, T + 3),
             in_=xbcpre[oi * 128:(oi + 1) * 128, 0:T + 3],
             waits=[(sx0, st1_sx[0]), (sx1, st1_sx[1]), (dv, dv.v)] + gcv[ci].need_w(),
             inc=cv, incn=16)
        gcv[ci].did_w(cv)
        for blk in range(NB):
            b0 = blk * CB
            h0 = 0
            p.op("vector", "tensor_scalar", cacc[:, h0:h0 + Hh],
                 a2v(cvino + ci * TS + b0 + h0, Hh),
                 cwt[:, 0 * 18 + oi:0 * 18 + oi + 1], None, OP.mult,
                 waits=[(cv, cv.v), (ld, ld_all)] + gcc[0].need_w(), inc=dv)
            p.drain("vector")
            for k in range(1, 4):
                p.op("vector", "scalar_tensor_tensor", cacc[:, h0:h0 + Hh],
                     a2v(cvino + ci * TS + b0 + h0 + k, Hh),
                     cwt[:, k * 18 + oi:k * 18 + oi + 1], cacc[:, h0:h0 + Hh],
                     OP.mult, OP.add, inc=dv)
                p.drain("vector")
            gcv[ci].did_r(dv)
            gcc[0].did_w(dv)
            h0 = Hh

            def wcv(k):
                return bass.AP(tensor=cwt, offset=k * 18 + oi,
                               ap=[[72, 128], [0, Hh]])
            p.op("gpsimd", "tensor_tensor", cacc[:, h0:h0 + Hh],
                 a2v(cvino + ci * TS + b0 + h0, Hh), wcv(0), OP.mult,
                 waits=[(cv, cv.v), (ld, ld_all)] + gcc[1].need_w(), inc=gp)
            p.drain("gpsimd")
            for k in range(1, 4):
                p.op("gpsimd", "tensor_tensor", tmp2[:, 0:Hh],
                     a2v(cvino + ci * TS + b0 + h0 + k, Hh), wcv(k), OP.mult, inc=gp)
                p.drain("gpsimd")
                p.op("gpsimd", "tensor_tensor", cacc[:, h0:h0 + Hh],
                     cacc[:, h0:h0 + Hh], tmp2[:, 0:Hh], OP.add, inc=gp)
                p.drain("gpsimd")
            gcv[ci].did_r(gp)
            gcc[1].did_w(gp)
            p.op("scalar", "activation", CV16[:, oi * T + b0:oi * T + b0 + CB],
                 cacc[:, 0:CB], AF.Silu, bias=cbcol[:, oi:oi + 1],
                 waits=[(dv, dv.v), (gp, gp.v)], inc=sc)
            p.drain("scalar")
            gcc[0].did_r(sc)
            gcc[1].did_r(sc)
    p.op("sync", "dma_start", out=cfm_o[:, :], in_=CV16[:, 17 * T:18 * T],
         waits=[(sc, sc.v)], inc=so, incn=16)
    st25_sc = sc.v

    p.wait("tensor", sc, st25_sc)
    p.wait("tensor", dv, st2_dv)
    gab = [Guard(), Guard()]
    gpd0 = Guard()
    gpd1 = Guard()
    gsm = Guard()
    gyq = [Guard(), Guard()]
    gtp = [Guard(), Guard()]
    gyb = Guard()
    gS = Guard()
    gpd0.rd = dict(gR[0].rd)
    gpd0.wr = gR[0].wr
    gpd1.rd = dict(gH.rd)
    gpd1.wr = gH.wr
    gyq[0].rd = dict(gY[0].rd)
    gyq[0].wr = gY[0].wr
    gyq[1].rd = dict(gY[1].rd)
    gyq[1].wr = gY[1].wr

    def abload(c):
        ci = c % 2
        s0 = c * CH
        sem = ab0 if ci == 0 else ab1
        w = [(s2s, st2_so)] + gab[0].need_w()
        p.op("sync", "dma_start",
             out=arow[0:1, :].rearrange("p (h t) -> p h t", t=CH),
             in_=abscr[0, :, s0:s0 + CH].unsqueeze(0), waits=w, inc=sem, incn=16)
        p.op("sync", "dma_start",
             out=brow[0:1, :].rearrange("p (h t) -> p h t", t=CH),
             in_=abscr[1, :, s0:s0 + CH].unsqueeze(0), inc=sem, incn=16)
        gab[0].did_w(sem)

    abload(0)
    for c in range(NCH):
        ci = c % 2
        s0 = c * CH
        absem = ab0 if ci == 0 else ab1
        abv = absem.v


        Bch = CV16[:, 16 * T + s0:16 * T + s0 + CH]
        Cch = CV16[:, 17 * T + s0:17 * T + s0 + CH]

        p.op("tensor", "matmul", pd[:, 0:CH], Bch, Cch, start=True, stop=True,
             waits=gsm.need_w() + gpd0.need_w(), inc=mm)
        gsm.did_w(mm)
        p.op("vector", "tensor_copy", cbt[:, :], pd[:, 0:CH], waits=[(mm, mm.v)], inc=dv)
        p.drain("vector")
        gsm.did_r(dv)
        p.op("tensor", "transpose", pd[:, 128:128 + NH], arel[:, s0:s0 + CH],
             identf[0:NH, 0:NH], waits=gsm.need_w(), inc=mm)
        gsm.did_w(mm)
        p.op("vector", "tensor_copy", aTt[:, :], pd[:, 128:128 + NH], waits=[(mm, mm.v)], inc=dv)
        p.drain("vector")
        gsm.did_r(dv)
        p.op("scalar", "activation", expaT[:, :], aTt[:, :], AF.Exp, waits=[(dv, dv.v)], inc=sc)
        p.drain("scalar")
        p.op("scalar", "activation", wch[:, :], btile[:, s0:s0 + CH], AF.Exp,
             bias=arel[0:NH, s0 + CH - 1:s0 + CH], inc=sc)
        p.drain("scalar")
        p.op("tensor", "transpose", pd[:, 160:160 + NH], wch[:, :],
             identf[0:NH, 0:NH], waits=gsm.need_w() + [(sc, sc.v)], inc=mm)
        gsm.did_w(mm)
        p.op("vector", "tensor_copy", waT[:, :], pd[:, 160:160 + NH], waits=[(mm, mm.v)], inc=dv)
        p.drain("vector")
        gsm.did_r(dv)
        p.op("scalar", "activation", prow[:, :],
             arow[0:1, :].rearrange("p (h t) -> p h t", t=CH)[:, :, CH - 1],
             AF.Exp, waits=[(absem, abv)], inc=sc)
        gab[0].did_r(sc)
        p.drain("scalar")
        p.op("tensor", "matmul", pd[:, 192:192 + NH], onesr[0:1, 0:128], prow[0:1, :],
             start=True, stop=True, waits=gsm.need_w() + [(sc, sc.v)], inc=mm)
        gsm.did_w(mm)
        p.op("vector", "tensor_copy", Pb[:, :], pd[:, 192:192 + NH], waits=[(mm, mm.v)], inc=dv)
        p.drain("vector")
        gsm.did_r(dv)

        for j in range(17):
            src = CV16[:, j * T + s0:j * T + s0 + CH] if j < 16 else Bch
            dstp = pt if j % 2 == 0 else pt2
            gtj = gtp[j % 2]
            p.op("tensor", "transpose", dstp[:, 0:128], src, ident[:, :],
                 waits=gtj.need_w(), inc=mm)
            gtj.did_w(mm)
            dst = xs_tm[:, j * 128:(j + 1) * 128] if j < 16 else btm[:, 0:128]
            p.op("vector", "tensor_copy", dst, dstp[:, 0:128], waits=[(mm, mm.v)], inc=dv)
            p.drain("vector")
            gtj.did_r(dv)
        xsdone = dv.v

        for H in range(2):
            wv = gpd0.need_w() + gpd1.need_w() + [(absem, abv)] + (gsm.need_w() if H == 0 else [])
            for b4 in range(4):
                h4 = H * 16 + b4 * 4
                p.op("tensor", "matmul", pd[:, b4 * 512:(b4 + 1) * 512],
                     onesr[0:1, 0:128], arow[0:1, h4 * CH:(h4 + 4) * CH],
                     start=True, stop=False,
                     waits=wv if b4 == 0 else (), inc=mm)
            for hh in range(16):
                h = H * 16 + hh
                p.op("tensor", "matmul", pd[:, hh * 128:(hh + 1) * 128],
                     brow[0:1, h * CH:(h + 1) * CH], onesr[0:1, 0:128],
                     start=False, stop=False, inc=mm)
            for b in range(4):
                p.op("tensor", "matmul", pd[:, b * 512:(b + 1) * 512], ident[:, :],
                     ap_v(vneg, 0, [[128, 128], [0, 4], [1, 128]]),
                     start=False, stop=True, inc=mm)
            gpd0.did_w(mm)
            gpd1.did_w(mm)
            for b in range(4):
                p.op("scalar", "activation",
                     eD[:, H * 2048 + b * 512:H * 2048 + (b + 1) * 512],
                     pd[:, b * 512:(b + 1) * 512], AF.Exp, waits=[(mm, mm.v)], inc=sc)
                p.drain("scalar")
            gpd0.did_r(sc)
            gpd1.did_r(sc)
        gab[0].did_r(mm)
        if c + 1 < NCH:
            abload(c + 1)
        p.op("gpsimd", "tensor_tensor",
             scoresT[:, :].rearrange("p (h t) -> p h t", h=NH),
             eD[:, :].rearrange("p (h t) -> p h t", h=NH),
             ap_v(cbt, 0, [[CH, 128], [0, NH], [1, CH]]),
             OP.mult, waits=[(sc, sc.v), (dv, dv.v)], inc=gp)
        p.drain("gpsimd")
        scdone = gp.v

        for q in range(4):
            yr = (q % 2) * 512
            gq = gyq[q % 2]
            gpdq = gpd0 if q % 2 == 0 else gpd1
            for hh in range(8):
                h = q * 8 + hh
                p.op("tensor", "matmul", py[:, yr + hh * 64:yr + (hh + 1) * 64],
                     scoresT[:, h * CH:(h + 1) * CH], xs_tm[:, h * 64:(h + 1) * 64],
                     start=(hh == 0), stop=(hh == 7),
                     waits=(gq.need_w() + [(gp, scdone), (dv, xsdone)]) if hh == 0 else (),
                     inc=mm)
            gq.did_w(mm)
            p.op("tensor", "matmul", pd[:, (q % 2) * 512:(q % 2) * 512 + 512],
                 Cch, S16[:, q * 512:(q + 1) * 512], start=True, stop=True,
                 waits=gpdq.need_w() + gS.need_r(), inc=mm)
            gpdq.did_w(mm)
            gS.did_r(mm)
            p.op("vector", "tensor_tensor",
                 tmp1[:, 0:512].rearrange("p (h q) -> p h q", h=8),
                 pd[:, (q % 2) * 512:(q % 2) * 512 + 512].rearrange("p (h q) -> p h q", h=8),
                 ap_v(expaT, q * 8, [[NH, 128], [1, 8], [0, 64]]),
                 OP.mult, waits=[(mm, mm.v)], inc=dv)
            p.drain("vector")
            gpdq.did_r(dv)
            p.op("vector", "tensor_tensor", ybuf[:, q * 512:(q + 1) * 512],
                 py[:, yr:yr + 512], tmp1[:, 0:512], OP.add,
                 waits=(gyb.need_w() if q == 0 else []), inc=dv)
            p.drain("vector")
            gq.did_r(dv)
            p.op("gpsimd", "tensor_tensor", tmp2[:, 0:512], xs_tm[:, q * 512:(q + 1) * 512],
                 Dbc[:, q * 512:(q + 1) * 512], OP.mult, waits=[(dv, xsdone)], inc=gp)
            p.drain("gpsimd")
            p.op("gpsimd", "tensor_tensor", ybuf[:, q * 512:(q + 1) * 512],
                 ybuf[:, q * 512:(q + 1) * 512], tmp2[:, 0:512], OP.add,
                 waits=[(dv, dv.v)], inc=gp)
            p.drain("gpsimd")
            gyb.did_w(gp)
        p.op("sync", "dma_start", out=yloc[s0:s0 + CH, :], in_=ybuf[:, :],
             waits=[(gp, gp.v)], inc=sy, incn=16)
        gyb.did_r(sy)

        p.op("gpsimd", "tensor_tensor",
             wx[:, :].rearrange("p (h q) -> p h q", h=32),
             xs_tm[:, :].rearrange("p (h q) -> p h q", h=32),
             ap_v(waT, 0, [[NH, 128], [1, 32], [0, 64]]), OP.mult,
             waits=[(dv, xsdone)], inc=gp)
        p.drain("gpsimd")
        for qq in range(4):
            gpdq = gpd0 if qq < 2 else gpd1
            p.op("tensor", "matmul", pd[:, qq * 512:(qq + 1) * 512],
                 btm[:, 0:128], wx[:, qq * 512:(qq + 1) * 512],
                 start=True, stop=True,
                 waits=(gpdq.need_w() + [(gp, gp.v), (dv, xsdone)]) if qq % 2 == 0 else (),
                 inc=mm)
            if qq % 2 == 1:
                gpdq.did_w(mm)
            else:
                gpdq.did_w(mm)
        for half in range(2):
            hof = half * 1024
            p.op("vector", "tensor_tensor",
                 tmp2[:, :].rearrange("p (h q) -> p h q", h=16),
                 S16[:, hof:hof + 1024].rearrange("p (h q) -> p h q", h=16),
                 ap_v(Pb, half * 16, [[NH, 128], [1, 16], [0, 64]]), OP.mult,
                 waits=(gS.need_r() + gS.need_w() + [(gp, gp.v)]) if half == 0 else (), inc=dv)
            p.drain("vector")
            gS.did_r(dv)
            p.op("vector", "tensor_tensor", S16[:, hof:hof + 1024], tmp2[:, :],
                 pd[:, hof:hof + 1024], OP.add, waits=[(mm, mm.v)], inc=dv)
            p.drain("vector")
            (gpd0 if half == 0 else gpd1).did_r(dv)
            gS.did_w(dv)
    p.op("sync", "dma_start", out=sout_o[:, :], in_=S16[:, :], waits=[(dv, dv.v)],
         inc=so, incn=16)
    agst = p.sb("agst", [128, 2 * NH])
    gag = [Guard(), Guard()]
    gagS = [Guard(), Guard()]
    for tt in range(NTB):
        t0 = tt * 128
        ri = tt % 2
        reg = ri * 512
        w0 = [(dv, dv.v), (gp, gp.v), (sc, sc.v)] if tt < 2 else []
        p.op("tensor", "transpose", pd[:, reg:reg + NH], dtraw[:, t0:t0 + 128],
             identf[0:NH, 0:NH], waits=gag[ri].need_w() + w0, inc=mm)
        gag[ri].did_w(mm)
        p.op("vector", "tensor_copy", agst[:, ri * NH:(ri + 1) * NH],
             pd[:, reg:reg + NH],
             waits=[(mm, mm.v)] + gagS[ri].need_w(), inc=dv)
        p.drain("vector")
        gag[ri].did_r(dv)
        gagS[ri].did_w(dv)
        p.op("sync", "dma_start", out=agT_o[t0:t0 + 128, :],
             in_=agst[:, ri * NH:(ri + 1) * NH], waits=[(dv, dv.v)], inc=so, incn=16)
        gagS[ri].did_r(so)
    if debug:
        dbg = {}
        for nm, t, n in (("d_scores", scoresT, NH * CH), ("d_xs", xs_tm, DI),
                         ("d_cbt", cbt, CH), ("d_expaT", expaT, NH),
                         ("d_waT", waT, NH), ("d_Pb", Pb, NH), ("d_btm", btm, DST),
                         ("d_arel", arel, T), ("d_brow", brow, NH * CH),
                         ("d_wx", wx, DI)):
            npart = 32 if nm in ("d_arel",) else (1 if nm == "d_brow" else 128)
            dt_ = BF16 if nm in ("d_scores", "d_xs", "d_btm", "d_wx") else F32
            o = p.dram(nm, [npart, n], dt_, kind="ExternalOutput")
            p.op("sync", "dma_start", out=o[:, :], in_=t[0:npart, 0:n],
                 waits=[(dv, dv.v), (gp, gp.v), (sc, sc.v)], inc=so, incn=16)
    for s in (so, sx0, sx1, sz0, sz1, s2s, sy, ab0, ab1, cvl0, cvl1, lg, ld):
        p.wait("sync", s, s.v)


def build_mamba_local(T=2048, debug=False):
    p = P("mamba_local")
    io = {
        "x2T": p.dram("x2T", [DM, T], BF16),
        "x2hT": p.dram("x2hT", [DM, 4], BF16),
        "wInT": p.dram("wInT", [DM, DIP]),
        "cw": p.dram("cw", [CONVD, 4]),
        "cbr": p.dram("cbr", [18, 128]),
        "dtb": p.dram("dtb", [NH, 1]),
        "alog": p.dram("alog", [NH, 1]),
        "dexpr": p.dram("dexpr", [1, DI]),
        "yloc": p.dram("yloc", [T, DI], kind="ExternalOutput"),
        "ztm": p.dram("ztm", [T, DI], BF16, kind="ExternalOutput"),
        "cfm": p.dram("cfm", [DST, T], BF16, kind="ExternalOutput"),
        "agT": p.dram("agT", [T, NH], kind="ExternalOutput"),
        "sout": p.dram("sout", [DST, DI], BF16, kind="ExternalOutput"),
    }
    emit_mamba_local(p, io, T, tag="x", debug=debug)
    return p.build()



def emit_mamba_fin(p, io, T=2048):
    NTB = T // 128
    NB5 = max(1, T // 512)
    TB5 = min(512, T)
    projT = io["projT"]
    pbias = io["pbias"]
    xn = io["xn"]
    s_o = io["s"]
    mscr = p.dram("mscr", [2, DM, T], BF16, kind="Internal")

    WO = p.sb("WO", [128, 16 * DM], BF16)
    SO = p.sb("SO", [128, 8 * DI], BF16)
    PJ = p.sb("PJ", [128, 16 * DM], BF16)
    ynT = p.sb("ynT", [128, 16 * TB5], BF16)
    ytile = p.sb("ytile", [128, DI])
    tmpf = p.sb("tmpf", [128, DI])
    ygt = p.sb("ygt", [128, DI])
    ygb = p.sb("ygb", [128, DI], BF16)
    ztile = p.sb("ztile", [128, DI], BF16)
    cft = p.sb("cft", [128, T], BF16)
    agt = p.sb("agt", [128, NH])
    expag = p.sb("expag", [128, NH])
    Sin = p.sb("Sin", [128, DI], BF16)
    Tacc = p.sb("Tacc", [128, DI])
    Tb = p.sb("Tb", [128, DI])
    Prow = p.sb("Prow", [1, NH])
    Pbc = p.sb("Pbc", [128, NH])
    aer = p.sb("aer", [1, 8 * NH])
    mrow = p.sb("mrow", [1, 8])
    irow = p.sb("irow", [1, 8])
    mtil = p.sb("mtil", [128, 8])
    nwbc = p.sb("nwbc", [128, DI], BF16)
    nwrow = p.sb("nwrow", [1, DI])
    xnt = p.sb("xnt", [128, DM])
    sout = p.sb("sout", [128, DM])
    mstage = p.sb("mstage", [128, 2 * 512], BF16)
    mtv = p.sb("mtv", [128, 2 * 128], BF16)
    pbrow = p.sb("pbrow", [1, DM])
    onesr = p.sb("onesr", [1, 128])
    ident = p.sb("identb2", [128, 128], BF16)
    iot = p.sb("iot2", [128, 128])
    rcol = p.sb("rcol", [128, 1])
    epscol = p.sb("epscol2", [128, 1])

    pc = p.ps("pc", [128, 2048])
    pt = p.ps("ptA", [128, 1024], BF16)
    pt2 = p.ps("ptB", [128, 1024], BF16)
    pw = p.ps("pw", [128, 1024])

    ld = p.sem("ld")
    l2 = p.sem("l2")
    dv = p.sem("dv")
    sc = p.sem("sc")
    gp = p.sem("gp")
    mm = p.sem("mm")
    so = p.sem("so")
    sm = p.sem("sm")
    sm1 = p.sem("sm1")
    lv0 = p.sem("lv0")
    lv1 = p.sem("lv1")

    gpc = Guard()
    gpt = [Guard(), Guard()]
    gpw = Guard()
    gyn = Guard()
    gms = [Guard(), Guard()]
    gmtv = [Guard(), Guard()]
    gTb = Guard()

    p.op("gpsimd", "memset", onesr[:, :], 1.0, inc=gp)
    p.op("gpsimd", "memset", epscol[:, :], EPS, inc=gp)
    p.op("gpsimd", "iota", iot[:, :], pattern=[[1, 128]], base=0,
         channel_multiplier=-1, allow_small_or_imprecise_dtypes=True, inc=gp)
    p.op("vector", "tensor_scalar", ident[:, :], iot[:, :], 0.0, None, OP.is_equal,
         waits=[(gp, gp.v)], inc=dv)
    p.drain("vector")
    p.op("sync", "dma_start", out=pbrow[:, :], in_=pbias[:, :], inc=ld, incn=16)
    p.op("gpsimd", "dma_start", out=PJ[:, :].rearrange("p (k o) -> p k o", k=16),
         in_=projT.rearrange("(k p) o -> p k o", p=128), inc=l2, incn=16)

    for di, d in enumerate(("f", "b")):
        p.op("gpsimd", "dma_start", out=SO[:, :].rearrange("p (j n) -> p j n", j=8),
             in_=io["so" + d].rearrange("(j p) n -> p j n", p=128),
             waits=[(dv, dv.v), (gp, gp.v)], inc=l2, incn=16)
        p.op("gpsimd", "dma_start", out=WO[:, :].rearrange("p (k o) -> p k o", k=16),
             in_=io["wo" + d].rearrange("(k p) o -> p k o", p=128), inc=l2, incn=16)
        p.op("gpsimd", "dma_start", out=cft[:, :], in_=io["cf" + d][:, :], inc=l2, incn=16)
        p.op("sync", "dma_start", out=aer[:, :], in_=io["ae" + d][:, :], inc=ld, incn=16)
        p.op("sync", "dma_start", out=mrow[:, :], in_=io["mrow" + d][:, :], inc=ld, incn=16)
        p.op("sync", "dma_start", out=irow[:, :], in_=io["irow" + d][:, :], inc=ld, incn=16)
        p.op("sync", "dma_start", out=mtil[:, :], in_=io["mtil" + d][:, :], inc=ld, incn=16)
        p.op("sync", "dma_start", out=nwrow[:, :], in_=io["nw" + d][:, :], inc=ld, incn=16)
        ldv = ld.v
        l2v = l2.v
        for i in range(4):
            p.op("tensor", "matmul", pw[:, (i % 2) * 512:(i % 2) * 512 + 512],
                 onesr[0:1, 0:128], nwrow[0:1, i * 512:(i + 1) * 512],
                 start=True, stop=True,
                 waits=(gpw.need_w() + [(ld, ldv), (dv, dv.v)]) if i < 2 else [(dv, dv.v)],
                 inc=mm)
            p.op("vector", "tensor_copy", nwbc[:, i * 512:(i + 1) * 512],
                 pw[:, (i % 2) * 512:(i % 2) * 512 + 512], waits=[(mm, mm.v)], inc=dv)
            p.drain("vector")
        gpw.did_w(mm)
        gpw.did_r(dv)

        p.op("gpsimd", "memset", Tacc[:, :], 0.0, waits=[(dv, dv.v)], inc=gp)
        order = range(8) if d == "f" else range(7, -1, -1)
        gpr = Guard()
        for j in order:
            p.op("scalar", "activation", Prow[:, :], aer[0:1, j * NH:(j + 1) * NH],
                 AF.Exp, waits=[(ld, ldv), (gp, gp.v)] + gpr.need_w(), inc=sc)
            p.drain("scalar")
            gpr.did_w(sc)
            p.op("vector", "tensor_scalar", Prow[:, :], Prow[:, :],
                 mrow[0:1, j:j + 1], irow[0:1, j:j + 1], OP.mult, op1=OP.add,
                 waits=[(sc, sc.v)], inc=dv)
            p.drain("vector")
            gpr.did_w(dv)
            p.op("tensor", "matmul", pw[:, 0:NH], onesr[0:1, 0:128], Prow[0:1, :],
                 start=True, stop=True, waits=gpw.need_w() + [(dv, dv.v)], inc=mm)
            gpw.did_w(mm)
            gpr.did_r(mm)
            p.op("vector", "tensor_copy", Pbc[:, :], pw[:, 0:NH], waits=[(mm, mm.v)], inc=dv)
            p.drain("vector")
            gpw.did_r(dv)
            p.op("vector", "tensor_tensor",
                 Tb[:, :].rearrange("p (h q) -> p h q", h=NH),
                 Tacc[:, :].rearrange("p (h q) -> p h q", h=NH),
                 ap_v(Pbc, 0, [[NH, 128], [1, NH], [0, HD]]), OP.mult,
                 waits=gTb.need_w() + [(gp, gp.v)], inc=dv)
            p.drain("vector")
            p.op("vector", "scalar_tensor_tensor", Tacc[:, :],
                 ap_v(SO, j * DI, [[8 * DI, 128], [1, DI]]),
                 mtil[0:128, j:j + 1], Tb[:, :], OP.mult, OP.add,
                 waits=[(l2, l2v)], inc=dv)
            p.drain("vector")
            gTb.did_r(dv)
        p.op("vector", "tensor_copy", Sin[:, :], Tacc[:, :], inc=dv)
        p.drain("vector")

        for bb in range(NB5):
            for i in range(TB5 // 128):
                tt = bb * (TB5 // 128) + i
                t0 = tt * 128
                p.op("sync", "dma_start", out=ytile[:, :], in_=io["y" + d][t0:t0 + 128, :],
                     waits=gyn.need_w() + [(dv, dv.v)], inc=ld, incn=16)
                p.op("sync", "dma_start", out=ztile[:, :], in_=io["z" + d][t0:t0 + 128, :],
                     inc=ld, incn=16)
                p.op("sync", "dma_start", out=agt[:, :], in_=io["ag" + d][t0:t0 + 128, :],
                     inc=ld, incn=16)
                ldv2 = ld.v
                p.op("scalar", "activation", expag[:, :], agt[:, :], AF.Exp,
                     waits=[(ld, ldv2)], inc=sc)
                p.drain("scalar")
                for q in range(4):
                    p.op("tensor", "matmul", pc[:, q * 512:(q + 1) * 512],
                         cft[:, t0:t0 + 128], Sin[:, q * 512:(q + 1) * 512],
                         start=True, stop=True,
                         waits=(gpc.need_w() + [(dv, dv.v), (l2, l2v)]) if q == 0 else (),
                         inc=mm)
                gpc.did_w(mm)
                p.op("vector", "tensor_tensor",
                     tmpf[:, :].rearrange("p (h q) -> p h q", h=NH),
                     pc[:, :].rearrange("p (h q) -> p h q", h=NH),
                     ap_v(expag, 0, [[NH, 128], [1, NH], [0, HD]]), OP.mult,
                     waits=[(mm, mm.v), (sc, sc.v)], inc=dv)
                p.drain("vector")
                gpc.did_r(dv)
                p.op("gpsimd", "tensor_tensor", ygt[:, :], ytile[:, :], tmpf[:, :],
                     OP.add, waits=[(dv, dv.v), (ld, ldv2)], inc=gp)
                p.drain("gpsimd")
                p.op("gpsimd", "tensor_tensor", ygb[:, :], ygt[:, :], ztile[:, :],
                     OP.mult, inc=gp)
                p.drain("gpsimd")
                p.op("scalar", "activation", tmpf[:, :], ygb[:, :], AF.Square,
                     accum_out=rcol[:, :], waits=[(gp, gp.v)], inc=sc)
                p.drain("scalar")
                p.op("scalar", "activation", rcol[:, :], rcol[:, :], AF.Ln,
                     scale=1.0 / DI, bias=epscol[0:128, 0:1], inc=sc)
                p.drain("scalar")
                p.op("scalar", "activation", rcol[:, :], rcol[:, :], AF.Exp, scale=-0.5,
                     inc=sc)
                p.drain("scalar")
                p.op("vector", "scalar_tensor_tensor", ztile[:, :], ygb[:, :],
                     rcol[0:128, 0:1], nwbc[:, :], OP.mult, OP.mult,
                     waits=[(sc, sc.v)], inc=dv)
                p.drain("vector")
                for kb in range(16):
                    dstp = pt if kb % 2 == 0 else pt2
                    gtj = gpt[kb % 2]
                    p.op("tensor", "transpose", dstp[:, 0:128],
                         ztile[:, kb * 128:(kb + 1) * 128], ident[:, :],
                         waits=gtj.need_w() + gyn.need_w() + [(dv, dv.v)], inc=mm)
                    gtj.did_w(mm)
                    p.op("vector", "tensor_copy", ynT[:, kb * TB5 + i * 128:kb * TB5 + i * 128 + 128],
                         dstp[:, 0:128], waits=[(mm, mm.v)], inc=dv)
                    p.drain("vector")
                    gtj.did_r(dv)
                gyn.did_w(dv)
            for ob in range(8):
                ri = ob % 2
                for kb in range(16):
                    if di == 0:
                        rhsv = ynT[:, kb * TB5:(kb + 1) * TB5]
                    else:
                        rhsv = ap_v(ynT, kb * TB5 + TB5 - 1,
                                    [[16 * TB5, 128], [-1, TB5]])
                    p.op("tensor", "matmul", pw[:, ri * 512:ri * 512 + TB5],
                         WO[:, kb * DM + ob * 128:kb * DM + (ob + 1) * 128],
                         rhsv,
                         start=(kb == 0), stop=(kb == 15),
                         waits=(gpw.need_w() + [(dv, dv.v), (l2, l2v)]) if kb == 0 else (),
                         inc=mm)
                gpw.did_w(mm)
                si = ob % 2
                p.op("vector", "tensor_copy", mstage[:, si * 512:si * 512 + TB5],
                     pw[:, ri * 512:ri * 512 + TB5],
                     waits=[(mm, mm.v)] + gms[si].need_w(), inc=dv)
                p.drain("vector")
                gpw.did_r(dv)
                gms[si].did_w(dv)
                gyn.did_r(mm)
                sms = sm if si == 0 else sm1
                c0s = bb * TB5 if di == 0 else T - (bb + 1) * TB5
                p.op("sync", "dma_start",
                     out=mscr[di, ob * 128:(ob + 1) * 128, c0s:c0s + TB5],
                     in_=mstage[:, si * 512:si * 512 + TB5], waits=[(dv, dv.v)],
                     inc=sms, incn=16)
                gms[si].did_r(sms)

    smv = sm.v
    smv1 = sm1.v
    ldC = ld.v
    gxn = Guard()
    gso = Guard()
    for tt in range(NTB):
        t0 = tt * 128
        p.op("sync", "dma_start", out=xnt[:, :], in_=xn[t0:t0 + 128, :],
             waits=gxn.need_w(), inc=ld, incn=16)
        ldv3 = ld.v
        for half in range(2):
            for di in range(2):
                for kb in range(8):
                    msl = mscr[di, kb * 128:(kb + 1) * 128, t0:t0 + 128]
                    li = (di * 8 + kb) % 2
                    lsem = lv0 if li == 0 else lv1
                    p.op("gpsimd", "dma_start", out=mtv[:, li * 128:(li + 1) * 128],
                         in_=msl, waits=[(sm, smv), (sm1, smv1)] + gmtv[li].need_w(),
                         inc=lsem, incn=16)
                    gmtv[li].did_w(lsem)
                    p.op("tensor", "matmul", pw[:, half * 512:(half + 1) * 512],
                         mtv[:, li * 128:(li + 1) * 128],
                         PJ[:, (di * 8 + kb) * DM + half * 512:(di * 8 + kb) * DM + half * 512 + 512],
                         start=(di == 0 and kb == 0), stop=False,
                         waits=[(lsem, lsem.v)] + (gpw.need_w() if (di == 0 and kb == 0) else []),
                         inc=mm)
                    gmtv[li].did_r(mm)
            p.op("tensor", "matmul", pw[:, half * 512:(half + 1) * 512],
                 onesr[0:1, 0:128], pbrow[0:1, half * 512:(half + 1) * 512],
                 start=False, stop=True, waits=[(ld, ldC)], inc=mm)
        gpw.did_w(mm)
        p.op("vector", "tensor_tensor", sout[:, :], pw[:, 0:1024], xnt[:, :], OP.add,
             waits=[(mm, mm.v), (ld, ldv3)], inc=dv)
        p.drain("vector")
        gpw.did_r(dv)
        gxn.did_r(dv)
        p.op("sync", "dma_start", out=s_o[t0:t0 + 128, :], in_=sout[:, :],
             waits=[(dv, dv.v)], inc=so, incn=16)
        gso.did_w(so)
        p.wait("vector", so, so.v)
    for s2 in (so, sm, sm1, ld, l2, lv0, lv1):
        p.wait("sync", s2, s2.v)


def fin_io_dram(p, T=2048, kinds=None):
    kinds = kinds or {}
    io = {}
    for d in ("f", "b"):
        io["y" + d] = p.dram("y" + d, [T, DI])
        io["z" + d] = p.dram("z" + d, [T, DI], BF16)
        io["cf" + d] = p.dram("cf" + d, [DST, T], BF16)
        io["ag" + d] = p.dram("ag" + d, [T, NH])
        io["ae" + d] = p.dram("ae" + d, [1, 8 * NH])
        io["so" + d] = p.dram("so" + d, [8 * DST, DI], BF16)
        io["mrow" + d] = p.dram("mrow" + d, [1, 8])
        io["irow" + d] = p.dram("irow" + d, [1, 8])
        io["mtil" + d] = p.dram("mtil" + d, [128, 8])
        io["wo" + d] = p.dram("wo" + d, [DI, DM])
        io["nw" + d] = p.dram("nw" + d, [1, DI])
    io["projT"] = p.dram("projT", [2 * DM, DM])
    io["pbias"] = p.dram("pbias", [1, DM])
    io["xn"] = p.dram("xn_fin", [T, DM]) if kinds.get("xn_shared") is None else kinds["xn_shared"]
    io["s"] = p.dram("s", [T, DM], kind="ExternalOutput")
    return io


def build_mamba_fin(T=2048):
    p = P("mamba_fin")
    io = fin_io_dram(p, T)
    emit_mamba_fin(p, io, T)
    return p.build()



def emit_moe_route(p, io, T=2048):
    RH = 4096
    NTB = T // 128
    NT5 = max(1, T // 512)
    T5 = min(512, T)
    SUB = T5 // 128
    N2 = 2 * T
    xnT = io["xnT"]
    xn = io["xn"]
    rw1 = io["rw1"]
    rb1c = io["rb1c"]
    rw2 = io["rw2"]
    rb2 = io["rb2"]
    send_o = io["send"]
    counts_o = io["counts"]
    ids_o = io["ids"]
    lpos_o = io["lpos"]
    probs_o = io["probs"]
    rscr = p.dram("rscr", [2, N2], kind="Internal")

    W1 = p.sb("W1", [128, 8 * RH], BF16)
    XT = p.sb("XT", [128, 8 * T], BF16)
    XR = p.sb("XR", [128, 2 * DM], BF16)
    HB = p.sb("HB", [128, 2 * T5], BF16)
    W2 = p.sb("W2", [128, 32 * 8], BF16)
    rb1b = p.sb("rb1b", [128, 32])
    logi = p.sb("logi", [128, NTB * 8])
    idsb = p.sb("idsb", [128, NTB * 2])
    lposb = p.sb("lposb", [128, NTB * 2])
    probsb = p.sb("probsb", [128, NTB * 2])
    m1c = p.sb("m1c", [128, 1])
    m2c = p.sb("m2c", [128, 1])
    lgm = p.sb("lgm", [128, 8])
    msk = p.sb("msk", [128, 8])
    tsc = p.sb("tsc", [128, 8])
    iob8 = p.sb("iob8", [128, 8])
    maskT = p.sb("maskT", [8, N2])
    cumT = p.sb("cumT", [8, N2])
    rowA = p.sb("rowA", [1, N2])
    rowB = p.sb("rowB", [1, N2])
    ones2T = p.sb("ones2T", [8, N2 // 8 if False else 512])
    ccol = p.sb("ccol", [8, 1])
    crow = p.sb("crow", [1, 8])
    startrow = p.sb("startrow", [1, 8])
    startcol = p.sb("startcol", [8, 1])
    iotc8 = p.sb("iotc8", [8, 1])
    ones8 = p.sb("ones8", [8, 1])
    onesr = p.sb("onesr", [1, 128])
    rb2r = p.sb("rb2r", [1, 8])
    id8 = p.sb("id8", [8, 8])
    sidx = p.sb("sidx", [128, 1], U32)
    sidxf = p.sb("sidxf", [128, 1])

    ph = p.ps("ph", [128, 1024])
    pl = p.ps("pl", [128, 2048])
    ps8 = p.ps("ps8", [8, 512])

    ld = p.sem("ld")
    lg = p.sem("lg")
    lgx0 = p.sem("lgx0")
    lgx1 = p.sem("lgx1")
    dv = p.sem("dv")
    sc = p.sem("sc")
    gp = p.sem("gp")
    mm = p.sem("mm")
    sr = p.sem("sr")
    sidl = p.sem("sidl")
    sst = p.sem("sst")
    soo = p.sem("soo")

    gph = [Guard(), Guard()]
    gpl = Guard()
    gp8 = Guard()
    ghb = [Guard(), Guard()]
    gxr = [Guard(), Guard()]
    gsx = Guard()
    gsi = Guard()

    p.op("gpsimd", "memset", onesr[:, :], 1.0, inc=gp)
    p.op("gpsimd", "memset", ones8[:, :], 1.0, inc=gp)
    p.op("gpsimd", "memset", ones2T[:, :], 1.0, inc=gp)
    p.op("gpsimd", "iota", iob8[:, :], pattern=[[1, 8]], base=1000000,
         channel_multiplier=0, allow_small_or_imprecise_dtypes=True, inc=gp)
    p.op("gpsimd", "iota", iotc8[:, :], pattern=[[0, 1]], base=0,
         channel_multiplier=1, allow_small_or_imprecise_dtypes=True, inc=gp)
    p.op("gpsimd", "iota", id8[:, :], pattern=[[1, 8]], base=0,
         channel_multiplier=-1, allow_small_or_imprecise_dtypes=True, inc=gp)
    p.op("vector", "tensor_scalar", id8[:, :], id8[:, :], 0.0, None, OP.is_equal,
         waits=[(gp, gp.v)], inc=dv)
    p.drain("vector")

    p.op("gpsimd", "dma_start", out=W1[:, :].rearrange("p (k o) -> p k o", k=8),
         in_=rw1.rearrange("(k p) o -> p k o", p=128), inc=lg, incn=16)
    p.op("gpsimd", "dma_start", out=XT[:, :].rearrange("p (k t) -> p k t", k=8),
         in_=xnT.rearrange("(k p) t -> p k t", p=128), inc=lg, incn=16)
    p.op("gpsimd", "dma_start", out=W2[:, :].rearrange("p (k o) -> p k o", k=32),
         in_=rw2.rearrange("(k p) o -> p k o", p=128), inc=lg, incn=16)
    p.op("sync", "dma_start", out=rb1b[:, :], in_=rb1c[:, :], inc=ld, incn=16)
    p.op("sync", "dma_start", out=rb2r[:, :], in_=rb2[:, :], inc=ld, incn=16)
    ldv = ld.v

    for t5 in range(NT5):
        t0 = t5 * T5
        for hb in range(32):
            ri = hb % 2
            for kb in range(8):
                p.op("tensor", "matmul", ph[:, ri * 512:ri * 512 + T5],
                     ap_v(W1, kb * RH + hb * 128, [[8 * RH, 128], [1, 128]]),
                     ap_v(XT, kb * T + t0, [[8 * T, 128], [1, T5]]),
                     start=(kb == 0), stop=(kb == 7),
                     waits=(gph[ri].need_w() + [(lg, lg.v)]) if kb == 0 else (), inc=mm)
            gph[ri].did_w(mm)
            p.op("scalar", "activation", HB[:, ri * T5:(ri + 1) * T5],
                 ph[:, ri * 512:ri * 512 + T5], AF.Relu,
                 bias=rb1b[:, hb:hb + 1],
                 waits=[(mm, mm.v), (ld, ldv)] + ghb[ri].need_w(), inc=sc)
            p.drain("scalar")
            gph[ri].did_r(sc)
            ghb[ri].did_w(sc)
            for tt in range(SUB):
                p.op("tensor", "matmul", pl[:, tt * 512:tt * 512 + 8],
                     HB[:, ri * T5 + tt * 128:ri * T5 + (tt + 1) * 128],
                     W2[:, hb * 8:(hb + 1) * 8],
                     start=(hb == 0), stop=False,
                     waits=[(sc, sc.v)] + (gpl.need_w() if (hb == 0 and tt == 0) else []),
                     inc=mm)
            ghb[ri].did_r(mm)
        for tt in range(SUB):
            p.op("tensor", "matmul", pl[:, tt * 512:tt * 512 + 8], onesr[0:1, 0:128],
                 rb2r[0:1, :], start=False, stop=True, inc=mm)
        gpl.did_w(mm)
        p.op("vector", "tensor_copy",
             logi[:, t5 * SUB * 8:(t5 + 1) * SUB * 8].rearrange("p (s e) -> p s e", e=8),
             ap_v(pl, 0, [[2048, 128], [512, SUB], [1, 8]]), waits=[(mm, mm.v)], inc=dv)
        p.drain("vector")
        gpl.did_r(dv)

    for tb in range(NTB):
        lt = logi[:, tb * 8:(tb + 1) * 8]
        p.op("vector", "tensor_reduce", m1c[:, :], lt, AX.X, OP.max, inc=dv)
        p.drain("vector")
        p.op("vector", "tensor_scalar", msk[:, :], lt, m1c[0:128, 0:1], None,
             OP.is_ge, inc=dv)
        p.drain("vector")
        p.op("vector", "scalar_tensor_tensor", tsc[:, :], msk[:, :], -1000000.0,
             iob8[:, :], OP.mult, OP.add, inc=dv)
        p.drain("vector")
        p.op("vector", "tensor_reduce", idsb[:, tb * 2:tb * 2 + 1], tsc[:, :],
             AX.X, OP.min, inc=dv)
        p.drain("vector")
        p.op("vector", "scalar_tensor_tensor", lgm[:, :], msk[:, :], -2000000.0,
             lt, OP.mult, OP.add, inc=dv)
        p.drain("vector")
        p.op("vector", "tensor_reduce", m2c[:, :], lgm[:, :], AX.X, OP.max, inc=dv)
        p.drain("vector")
        p.op("vector", "tensor_scalar", msk[:, :], lgm[:, :], m2c[0:128, 0:1], None,
             OP.is_ge, inc=dv)
        p.drain("vector")
        p.op("vector", "scalar_tensor_tensor", tsc[:, :], msk[:, :], -1000000.0,
             iob8[:, :], OP.mult, OP.add, inc=dv)
        p.drain("vector")
        p.op("vector", "tensor_reduce", idsb[:, tb * 2 + 1:tb * 2 + 2], tsc[:, :],
             AX.X, OP.min, inc=dv)
        p.drain("vector")
        p.op("vector", "tensor_tensor", m2c[:, :], m1c[:, :], m2c[:, :], OP.subtract,
             inc=dv)
        p.drain("vector")
        p.op("scalar", "activation", probsb[:, tb * 2:tb * 2 + 1], m2c[:, :],
             AF.Sigmoid, waits=[(dv, dv.v)], inc=sc)
        p.drain("scalar")
        p.op("vector", "tensor_scalar", probsb[:, tb * 2 + 1:tb * 2 + 2],
             probsb[:, tb * 2:tb * 2 + 1], -1.0, 1.0, OP.mult, op1=OP.add,
             waits=[(sc, sc.v)], inc=dv)
        p.drain("vector")
    p.op("sync", "dma_start", out=ids_o.rearrange("(b p) k -> p b k", p=128),
         in_=idsb[:, :].rearrange("p (b k) -> p b k", k=2),
         waits=[(dv, dv.v)], inc=soo, incn=16)
    p.op("sync", "dma_start", out=probs_o.rearrange("(b p) k -> p b k", p=128),
         in_=probsb[:, :].rearrange("p (b k) -> p b k", k=2),
         waits=[(dv, dv.v)], inc=soo, incn=16)
    p.op("sync", "dma_start",
         out=rscr[0:1, :].rearrange("q (b p k) -> (q p) b k", b=NTB, k=2),
         in_=idsb[:, :].rearrange("p (b k) -> p b k", k=2),
         waits=[(dv, dv.v)], inc=sr, incn=16)
    srv = sr.v

    p.wait("sync", sr, srv)
    for cblk in range(N2 // 512):
        p.op("sync", "dma_start", out=rowA[0:1, cblk * 512:(cblk + 1) * 512],
             in_=rscr[0:1, cblk * 512:(cblk + 1) * 512],
             waits=[(sr, srv)], inc=ld, incn=16)
        p.op("tensor", "matmul", ps8[:, :], ones2T[0:1, 0:8],
             rowA[0:1, cblk * 512:(cblk + 1) * 512], start=True, stop=True,
             waits=[(ld, ld.v)] + gp8.need_w() + [(gp, gp.v)], inc=mm)
        gp8.did_w(mm)
        p.op("vector", "tensor_scalar", maskT[:, cblk * 512:(cblk + 1) * 512],
             ps8[0:8, :], iotc8[0:8, 0:1], None, OP.is_equal,
             waits=[(mm, mm.v)], inc=dv)
        p.drain("vector")
        gp8.did_r(dv)
    p.op("vector", "tensor_tensor_scan", cumT[:, :], ap_v(ones2T, 0, [[512, 8], [0, N2]]),
         maskT[:, :], 0.0, OP.mult, OP.add, inc=dv)
    p.drain("vector")
    p.op("vector", "tensor_copy", ccol[:, :], cumT[0:8, N2 - 1:N2], inc=dv)
    p.drain("vector")
    p.op("sync", "dma_start", out=counts_o[:, :], in_=ccol[0:8, :],
         waits=[(dv, dv.v)], inc=soo, incn=16)
    p.op("tensor", "transpose", pl[0:1, 0:8], ccol[0:8, 0:1], id8[0:8, 0:8],
         waits=gpl.need_w() + [(dv, dv.v)], inc=mm)
    gpl.did_w(mm)
    p.op("vector", "tensor_copy", crow[:, :], pl[0:1, 0:8], waits=[(mm, mm.v)], inc=dv)
    p.drain("vector")
    gpl.did_r(dv)
    p.op("vector", "tensor_tensor_scan", startrow[:, :], ones2T[0:1, 0:8], crow[:, :],
         0.0, OP.mult, OP.add, inc=dv)
    p.drain("vector")
    p.op("vector", "tensor_tensor", startrow[:, :], startrow[:, :], crow[:, :],
         OP.subtract, inc=dv)
    p.drain("vector")
    p.op("tensor", "transpose", pl[0:8, 0:1], startrow[0:1, 0:8], id8[0:1, 0:1],
         waits=gpl.need_w() + [(dv, dv.v)], inc=mm)
    gpl.did_w(mm)
    p.op("vector", "tensor_copy", startcol[0:8, :], pl[0:8, 0:1], waits=[(mm, mm.v)], inc=dv)
    p.drain("vector")
    gpl.did_r(dv)
    p.op("vector", "tensor_scalar", cumT[:, :], cumT[:, :], -1.0, None, OP.add, inc=dv)
    p.drain("vector")
    p.op("vector", "tensor_scalar", cumT[:, :], cumT[:, :], startcol[0:8, 0:1], None,
         OP.add, inc=dv)
    p.drain("vector")
    p.op("vector", "tensor_tensor", cumT[:, :], cumT[:, :], maskT[:, :], OP.mult,
         inc=dv)
    p.drain("vector")
    for cblk in range(N2 // 512):
        p.op("tensor", "matmul", pl[0:1, 0:512], ones2T[0:8, 0:1],
             cumT[0:8, cblk * 512:(cblk + 1) * 512],
             start=True, stop=True, waits=(gpl.need_w() + [(dv, dv.v)]), inc=mm)
        gpl.did_w(mm)
        p.op("vector", "tensor_copy", rowB[0:1, cblk * 512:(cblk + 1) * 512],
             pl[0:1, 0:512], waits=[(mm, mm.v)], inc=dv)
        p.drain("vector")
        gpl.did_r(dv)
    p.op("sync", "dma_start", out=rscr[1:2, :], in_=rowB[0:1, :],
         waits=[(dv, dv.v)], inc=sr, incn=16)
    srv2 = sr.v

    p.wait("sync", sr, srv2)
    for tb in range(NTB):
        t0 = tb * 128
        xi = tb % 2
        lgx = lgx0 if xi == 0 else lgx1
        p.op("gpsimd", "dma_start", out=XR[:, xi * DM:(xi + 1) * DM],
             in_=xn[t0:t0 + 128, :], waits=gxr[xi].need_w(), inc=lgx, incn=16)
        gxr[xi].did_w(lgx)
        for k in range(2):
            p.op("sync", "dma_start", out=sidxf[:, :],
                 in_=ap_v_dram(rscr, N2 + t0 * 2 + k, [[1, 1], [2, 128]]),
                 allow_slow_non_contiguous=True,
                 waits=[(sr, srv2)] + gsx.need_w(), inc=sidl, incn=16)
            gsx.did_w(sidl)
            p.op("vector", "tensor_copy", sidx[:, :], sidxf[:, :],
                 waits=[(sidl, sidl.v)] + gsi.need_w(), inc=dv)
            p.drain("vector")
            gsx.did_r(dv)
            gsi.did_w(dv)
            p.op("gpsimd", "indirect_dma_start", out=send_o[:, :],
                 out_offset=bass.IndirectOffsetOnAxis(ap=sidx[:, 0:1], axis=0),
                 in_=XR[:, xi * DM:(xi + 1) * DM], in_offset=None,
                 waits=[(dv, dv.v), (lgx, lgx.v)], inc=sst, incn=16)
            gxr[xi].did_r(sst)
            gsi.did_r(sst)
    p.op("vector", "tensor_scalar", maskT[:, :], maskT[:, :], startcol[0:8, 0:1],
         None, OP.mult, inc=dv)
    p.drain("vector")
    for cblk in range(N2 // 512):
        p.op("tensor", "matmul", pl[0:1, 0:512], ones2T[0:8, 0:1],
             maskT[0:8, cblk * 512:(cblk + 1) * 512],
             start=True, stop=True, waits=(gpl.need_w() + [(dv, dv.v)]), inc=mm)
        gpl.did_w(mm)
        p.op("vector", "tensor_copy", rowA[0:1, cblk * 512:(cblk + 1) * 512],
             pl[0:1, 0:512], waits=[(mm, mm.v)], inc=dv)
        p.drain("vector")
        gpl.did_r(dv)
    p.op("vector", "tensor_tensor", rowB[:, :], rowB[:, :], rowA[:, :], OP.subtract,
         inc=dv)
    p.drain("vector")
    p.op("sync", "dma_start", out=lpos_o.rearrange("t k -> (t k)").unsqueeze(0),
         in_=rowB[0:1, :], waits=[(dv, dv.v)], inc=soo, incn=16)
    for s2 in (soo, sst, sr, sidl, ld, lg, lgx0, lgx1):
        p.wait("sync", s2, s2.v)


def build_moe_route(T=2048):
    RH = 4096
    N2 = 2 * T
    p = P("moe_route")
    io = {
        "xnT": p.dram("xnT", [DM, T], BF16),
        "xn": p.dram("xn", [T, DM]),
        "rw1": p.dram("rw1", [DM, RH]),
        "rb1c": p.dram("rb1c", [128, 32]),
        "rw2": p.dram("rw2", [RH, 8]),
        "rb2": p.dram("rb2", [1, 8]),
        "send": p.dram("send", [N2, DM], BF16, kind="ExternalOutput"),
        "counts": p.dram("counts", [8, 1], kind="ExternalOutput"),
        "ids": p.dram("ids", [T, 2], kind="ExternalOutput"),
        "lpos": p.dram("lpos", [T, 2], kind="ExternalOutput"),
        "probs": p.dram("probs", [T, 2], kind="ExternalOutput"),
    }
    emit_moe_route(p, io, T)
    return p.build()


def ap_v_dram(t, offset, ap):
    return bass.AP(tensor=t.tensor, offset=offset, ap=[list(x) for x in ap])



def emit_moe_expert(p, io):
    NS = 4096
    H = 4096
    sendall = io["sendall"]
    countsA = io["countsA"]
    onehotE = io["onehotE"]
    ccol_i = io["ccol"]
    ew1 = io["ew1"]
    eb1c = io["eb1c"]
    ew2 = io["ew2"]
    eb2 = io["eb2"]
    eo_o = io["eo"]
    ew2b = p.dram("ew2b", [H, DM], BF16, kind="Internal")

    W1 = p.sb("W1e", [128, 8 * H], BF16)
    HBUF = p.sb("HBUF", [128, 32 * 512], BF16)
    DT = p.sb("DT", [128, 8 * 512], BF16)
    GR = p.sb("GR", [128, 2 * DM], BF16)
    W2s = p.sb("W2s", [128, 2 * DM], BF16)
    OS = p.sb("OS", [128, 2 * DM], BF16)
    cA = p.sb("cA", [8, 8])
    ohE = p.sb("ohE", [8, 8])
    ccol = p.sb("ccol_s", [8, 1])
    crow = p.sb("crow2", [1, 8])
    cumrow = p.sb("cumrow", [1, 8])
    strow = p.sb("strow", [1, 9])
    ddrow = p.sb("ddrow", [1, 8])
    stcol = p.sb("stcol", [8, 1])
    cumb = p.sb("cumb", [128, 8])
    ddb = p.sb("ddb", [128, 8])
    c0b = p.sb("c0b", [128, 1])
    Ptile = p.sb("Ptile", [128, 32])
    idxf = p.sb("idxf", [128, 32])
    idxu = p.sb("idxu", [128, 32], U32)
    msk32 = p.sb("msk32", [128, 32])
    eb1b = p.sb("eb1b", [128, 32])
    eb2r = p.sb("eb2r", [1, DM])
    ones8b = p.sb("ones8b", [8, 8])
    onesr = p.sb("onesr3", [1, 128])
    id8 = p.sb("id8b", [8, 8])
    identb = p.sb("identb3", [128, 128], BF16)
    iotb = p.sb("iotb", [128, 128])
    jr = p.sb("jr", [1, 8])

    ph = p.ps("phE", [128, 1024])
    po = p.ps("poE", [128, 2048])
    pt = p.ps("ptE", [128, 1024], BF16)
    pt2 = p.ps("ptE2", [128, 1024], BF16)

    ld = p.sem("ld")
    lg = p.sem("lg")
    lw0 = p.sem("lw0")
    lw1 = p.sem("lw1")
    gg0 = p.sem("gg0")
    gg1 = p.sem("gg1")
    dv = p.sem("dv")
    sc = p.sem("sc")
    gp = p.sem("gp")
    mm = p.sem("mm")
    so = p.sem("so")
    so1 = p.sem("so1")
    sw = p.sem("sw")
    sw1 = p.sem("sw1")

    gph = [Guard(), Guard()]
    gpo = [Guard(), Guard()]
    gpt = [Guard(), Guard()]
    ggr = [Guard(), Guard()]
    gw2 = [Guard(), Guard()]
    gos = [Guard(), Guard()]
    ghb = Guard()
    gdt = Guard()
    gidx = Guard()

    p.op("gpsimd", "memset", onesr[:, :], 1.0, inc=gp)
    p.op("gpsimd", "memset", ones8b[:, :], 1.0, inc=gp)
    p.op("gpsimd", "iota", iotb[:, :], pattern=[[1, 128]], base=0,
         channel_multiplier=-1, allow_small_or_imprecise_dtypes=True, inc=gp)
    p.op("gpsimd", "iota", Ptile[:, :], pattern=[[128, 32]], base=0,
         channel_multiplier=1, allow_small_or_imprecise_dtypes=True, inc=gp)
    p.op("gpsimd", "iota", jr[:, :], pattern=[[1, 8]], base=0, channel_multiplier=0,
         allow_small_or_imprecise_dtypes=True, inc=gp)
    p.op("gpsimd", "iota", id8[:, :], pattern=[[1, 8]], base=0,
         channel_multiplier=-1, allow_small_or_imprecise_dtypes=True, inc=gp)
    p.op("gpsimd", "memset", strow[0:1, 8:9], 40000.0, inc=gp)
    p.op("vector", "tensor_scalar", identb[:, :], iotb[:, :], 0.0, None, OP.is_equal,
         waits=[(gp, gp.v)], inc=dv)
    p.drain("vector")
    p.op("vector", "tensor_scalar", id8[:, :], id8[:, :], 0.0, None, OP.is_equal,
         inc=dv)
    p.drain("vector")

    p.op("gpsimd", "dma_start", out=W1[:, :].rearrange("p (k o) -> p k o", k=8),
         in_=ew1.rearrange("(k p) o -> p k o", p=128), inc=lg, incn=16)
    p.op("sync", "dma_start", out=cA[:, :], in_=countsA[:, :], inc=ld, incn=16)
    p.op("sync", "dma_start", out=ohE[:, :], in_=onehotE[:, :], inc=ld, incn=16)
    p.op("sync", "dma_start", out=ccol[:, :], in_=ccol_i[:, :], inc=ld, incn=16)
    p.op("sync", "dma_start", out=eb1b[:, :], in_=eb1c[:, :], inc=ld, incn=16)
    p.op("sync", "dma_start", out=eb2r[:, :], in_=eb2[:, :], inc=ld, incn=16)
    ldv = ld.v
    for kh in range(32):
        ws = kh % 2
        wsem = lw0 if ws == 0 else lw1
        p.op("gpsimd", "dma_start", out=W2s[:, ws * DM:(ws + 1) * DM],
             in_=ew2[kh * 128:(kh + 1) * 128, :],
             waits=gw2[ws].need_w(), inc=wsem, incn=16)
        gw2[ws].did_w(wsem)
        ssem = sw if ws == 0 else sw1
        p.op("sync", "dma_start", out=ew2b[kh * 128:(kh + 1) * 128, :],
             in_=W2s[:, ws * DM:(ws + 1) * DM], waits=[(wsem, wsem.v)],
             inc=ssem, incn=16)
        gw2[ws].did_r(ssem)
    swv = (sw.v, sw1.v)

    p.op("vector", "tensor_tensor_scan", cumb[0:8, 0:8], ones8b[:, :], cA[0:8, :],
         0.0, OP.mult, OP.add, waits=[(ld, ldv)], inc=dv)
    p.drain("vector")
    p.op("vector", "tensor_tensor", cumb[0:8, 0:8], cumb[0:8, 0:8], cA[0:8, :],
         OP.subtract, inc=dv)
    p.drain("vector")
    p.op("vector", "tensor_tensor", cumb[0:8, 0:8], cumb[0:8, 0:8], ohE[0:8, :],
         OP.mult, inc=dv)
    p.drain("vector")
    p.op("vector", "tensor_reduce", stcol[0:8, :], cumb[0:8, 0:8], AX.X, OP.add,
         inc=dv)
    p.drain("vector")
    gps = Guard()
    p.op("tensor", "transpose", ph[0:1, 0:8], ccol[0:8, 0:1], id8[0:8, 0:8],
         waits=gps.need_w() + [(dv, dv.v)], inc=mm)
    gps.did_w(mm)
    p.op("vector", "tensor_copy", crow[:, :], ph[0:1, 0:8], waits=[(mm, mm.v)], inc=dv)
    p.drain("vector")
    gps.did_r(dv)
    p.op("vector", "tensor_tensor_scan", cumrow[:, :], ones8b[0:1, 0:8], crow[:, :],
         0.0, OP.mult, OP.add, inc=dv)
    p.drain("vector")
    p.op("tensor", "transpose", ph[0:1, 0:8], stcol[0:8, 0:1], id8[0:8, 0:8],
         waits=gps.need_w() + [(dv, dv.v)], inc=mm)
    gps.did_w(mm)
    p.op("vector", "tensor_copy", strow[0:1, 0:8], ph[0:1, 0:8], waits=[(mm, mm.v)], inc=dv)
    p.drain("vector")
    gps.did_r(dv)
    p.op("vector", "tensor_tensor", strow[0:1, 0:8], strow[0:1, 0:8], cumrow[:, :],
         OP.subtract, inc=dv)
    p.drain("vector")
    p.op("vector", "tensor_tensor", strow[0:1, 0:8], strow[0:1, 0:8], crow[:, :],
         OP.add, inc=dv)
    p.drain("vector")
    p.op("vector", "scalar_tensor_tensor", strow[0:1, 0:8], jr[:, :], 4096.0,
         strow[0:1, 0:8], OP.mult, OP.add, waits=[(gp, gp.v)], inc=dv)
    p.drain("vector")
    p.op("vector", "tensor_tensor", ddrow[0:1, 0:8], strow[0:1, 1:9], strow[0:1, 0:8],
         OP.subtract, inc=dv)
    p.drain("vector")
    p.op("tensor", "matmul", ph[:, 0:8], onesr[0:1, 0:128], cumrow[0:1, :],
         start=True, stop=True, waits=gps.need_w() + [(dv, dv.v)], inc=mm)
    p.op("tensor", "matmul", ph[:, 8:16], onesr[0:1, 0:128], ddrow[0:1, :],
         start=True, stop=True, inc=mm)
    p.op("tensor", "matmul", ph[:, 16:17], onesr[0:1, 0:128], strow[0:1, 0:1],
         start=True, stop=True, inc=mm)
    gps.did_w(mm)
    p.op("vector", "tensor_copy", cumb[:, :], ph[:, 0:8], waits=[(mm, mm.v)], inc=dv)
    p.drain("vector")
    p.op("vector", "tensor_copy", ddb[:, :], ph[:, 8:16], inc=dv)
    p.drain("vector")
    p.op("vector", "tensor_copy", c0b[:, :], ph[:, 16:17], inc=dv)
    p.drain("vector")
    gps.did_r(dv)
    p.op("vector", "tensor_scalar", idxf[:, :], Ptile[:, :], c0b[0:128, 0:1], None,
         OP.add, waits=[(gp, gp.v)], inc=dv)
    p.drain("vector")
    for j in range(8):
        p.op("vector", "tensor_scalar", msk32[:, :], Ptile[:, :], cumb[0:128, j:j + 1],
             None, OP.is_ge, inc=dv)
        p.drain("vector")
        p.op("vector", "scalar_tensor_tensor", idxf[:, :], msk32[:, :],
             ddb[0:128, j:j + 1], idxf[:, :], OP.mult, OP.add, inc=dv)
        p.drain("vector")
    p.op("vector", "tensor_copy", idxu[:, :], idxf[:, :], inc=dv)
    p.drain("vector")
    idxdone = dv.v

    for sb in range(8):
        for it in range(4):
            gi = it % 2
            gsem = gg0 if gi == 0 else gg1
            p.op("gpsimd", "memset", GR[:, gi * DM:(gi + 1) * DM], 0.0,
                 waits=ggr[gi].need_w() + [(dv, idxdone)], inc=gp)
            ggr[gi].did_w(gp)
            p.op("gpsimd", "indirect_dma_start", out=GR[:, gi * DM:(gi + 1) * DM],
                 out_offset=None, in_=sendall[:, :],
                 in_offset=bass.IndirectOffsetOnAxis(
                     ap=idxu[:, sb * 4 + it:sb * 4 + it + 1], axis=0),
                 bounds_check=8 * NS - 1, oob_is_err=False,
                 waits=[(gp, gp.v)], inc=gsem, incn=16)
            ggr[gi].did_w(gsem)
            for kb in range(8):
                dstp = pt if kb % 2 == 0 else pt2
                gtj = gpt[kb % 2]
                p.op("tensor", "transpose", dstp[:, 0:128],
                     GR[:, gi * DM + kb * 128:gi * DM + (kb + 1) * 128], identb[:, :],
                     waits=gtj.need_w() + [(gsem, gsem.v), (gp, gp.v), (dv, dv.v)],
                     inc=mm)
                gtj.did_w(mm)
                p.op("vector", "tensor_copy",
                     DT[:, kb * 512 + it * 128:kb * 512 + it * 128 + 128],
                     dstp[:, 0:128],
                     waits=[(mm, mm.v)] + (gdt.need_w() if (it == 0 and kb == 0) else []),
                     inc=dv)
                p.drain("vector")
                gtj.did_r(dv)
            ggr[gi].did_r(mm)
        gdt.did_w(dv)
        for hb in range(32):
            ri = hb % 2
            for kb in range(8):
                p.op("tensor", "matmul", ph[:, ri * 512:(ri + 1) * 512],
                     ap_v(W1, kb * H + hb * 128, [[8 * H, 128], [1, 128]]),
                     DT[:, kb * 512:(kb + 1) * 512],
                     start=(kb == 0), stop=(kb == 7),
                     waits=(gph[ri].need_w() + gps.need_w() + [(lg, lg.v), (dv, dv.v)])
                     if kb == 0 else (), inc=mm)
            gph[ri].did_w(mm)
            p.op("scalar", "activation", HBUF[:, hb * 512:(hb + 1) * 512],
                 ph[:, ri * 512:(ri + 1) * 512], AF.Relu, bias=eb1b[:, hb:hb + 1],
                 waits=[(mm, mm.v), (ld, ldv)] + (ghb.need_w() if hb < 2 else []),
                 inc=sc)
            p.drain("scalar")
            gph[ri].did_r(sc)
        ghb.did_w(sc)
        gdt.did_r(mm)
        for sp in range(2):
            for kh in range(32):
                ws = kh % 2
                wsem = lw0 if ws == 0 else lw1
                p.op("gpsimd", "dma_start", out=W2s[:, ws * DM:(ws + 1) * DM],
                     in_=ew2b[kh * 128:(kh + 1) * 128, :],
                     waits=[(sw, swv[0]), (sw1, swv[1])] + gw2[ws].need_w(),
                     inc=wsem, incn=16)
                gw2[ws].did_w(wsem)
                for s2i in range(2):
                    ss = sp * 2 + s2i
                    for nb in range(2):
                        p.op("tensor", "matmul",
                             po[:, s2i * 1024 + nb * 512:s2i * 1024 + (nb + 1) * 512],
                             HBUF[:, kh * 512 + ss * 128:kh * 512 + ss * 128 + 128],
                             W2s[:, ws * DM + nb * 512:ws * DM + (nb + 1) * 512],
                             start=(kh == 0), stop=False,
                             waits=([(wsem, wsem.v), (sc, sc.v)] +
                                    (gpo[s2i].need_w() if (kh == 0 and nb == 0) else []))
                             if (kh == 0 or True) else (), inc=mm)
                gw2[ws].did_r(mm)
            for s2i in range(2):
                ss = sp * 2 + s2i
                p.op("tensor", "matmul", po[:, s2i * 1024:s2i * 1024 + 512],
                     onesr[0:1, 0:128], eb2r[0:1, 0:512],
                     start=False, stop=True, waits=[(ld, ldv)], inc=mm)
                p.op("tensor", "matmul", po[:, s2i * 1024 + 512:s2i * 1024 + 1024],
                     onesr[0:1, 0:128], eb2r[0:1, 512:1024],
                     start=False, stop=True, inc=mm)
                gpo[s2i].did_w(mm)
                p.op("vector", "tensor_copy", OS[:, s2i * DM:(s2i + 1) * DM],
                     po[:, s2i * 1024:(s2i + 1) * 1024],
                     waits=[(mm, mm.v)] + gos[s2i].need_w(), inc=dv)
                p.drain("vector")
                gpo[s2i].did_r(dv)
                gos[s2i].did_w(dv)
                osem = so if s2i == 0 else so1
                p.op("sync", "dma_start",
                     out=eo_o[sb * 512 + ss * 128:sb * 512 + (ss + 1) * 128, :],
                     in_=OS[:, s2i * DM:(s2i + 1) * DM], waits=[(dv, dv.v)],
                     inc=osem, incn=16)
                gos[s2i].did_r(osem)
    for s2 in (so, so1, sw, sw1, ld, lg, lw0, lw1, gg0, gg1):
        p.wait("sync", s2, s2.v)


def expert_io_dram(p):
    NS, H = 4096, 4096
    return {
        "sendall": p.dram("sendall", [8 * NS, DM], BF16),
        "countsA": p.dram("countsA", [8, 8]),
        "onehotE": p.dram("onehotE", [8, 8]),
        "ccol": p.dram("ccol", [8, 1]),
        "ew1": p.dram("ew1", [DM, H]),
        "eb1c": p.dram("eb1c", [128, 32]),
        "ew2": p.dram("ew2", [H, DM]),
        "eb2": p.dram("eb2", [1, DM]),
        "eo": p.dram("eo", [NS, DM], BF16, kind="ExternalOutput"),
    }


def build_moe_expert():
    p = P("moe_expert")
    emit_moe_expert(p, expert_io_dram(p))
    return p.build()


def build_launchB(T=2048):
    p = P("launchB")
    io_f = fin_io_dram(p, T)
    io_e = expert_io_dram(p)
    p.push_scope()
    emit_mamba_fin(p, io_f, T)
    p.pop_scope()
    p.barrier()
    p.push_scope()
    emit_moe_expert(p, io_e)
    p.pop_scope()
    p.barrier()
    return p.build()



def emit_moe_combine(p, io, T=2048):
    N2 = 2 * T
    NTB = T // 128
    eoall = io["eoall"]
    countsA = io["countsA"]
    premask = io["premask"]
    idsf_i = io["idsf"]
    lposf_i = io["lposf"]
    probsf_i = io["probsf"]
    s_i = io["s"]
    x_i = io["x"]
    x1_o = io["x1"]
    xn1_o = io["xn1"]
    xn1T_o = io.get("xn1T")
    cscr = p.dram("cscr", [2, N2], kind="Internal")

    idsf = p.sb("idsf_s", [1, N2])
    lposf = p.sb("lposf_s", [1, N2])
    probsf = p.sb("probsf_s", [1, N2])
    keepr = p.sb("keepr", [1, N2])
    slotr = p.sb("slotr", [1, N2])
    pkr = p.sb("pkr", [1, N2])
    maskT = p.sb("maskTc", [8, N2])
    cA = p.sb("cAc", [8, 8])
    pmc = p.sb("pmc", [8, 1])
    offsrow = p.sb("offsrow", [1, 8])
    offscol = p.sb("offscol", [8, 1])
    iotc8 = p.sb("iotc8c", [8, 1])
    id8 = p.sb("id8c", [8, 8])
    ones8 = p.sb("ones8c", [8, 8])
    onesr = p.sb("onesrc", [1, 128])
    grows = p.sb("growsc", [128, DM], BF16)
    yacc = p.sb("yaccc", [128, DM])
    stile = p.sb("stilec", [128, DM])
    xtile = p.sb("xtilec", [128, DM])
    x1t = p.sb("x1tc", [128, DM])
    sidx = p.sb("sidxc", [128, 1], U32)
    sidxf = p.sb("sidxfc", [128, 1])
    pkc = p.sb("pkcc", [128, 1])
    c1 = p.sb("c1c", [128, 1])
    c2 = p.sb("c2c", [128, 1])
    epscol = p.sb("epscolc", [128, 1])

    identf = p.sb("identfc", [128, 128])
    iotf = p.sb("iotfc", [128, 128])
    tstg = p.sb("tstgc", [128, DM], BF16)
    pp = p.ps("ppc", [128, 1024])
    p8 = p.ps("p8c", [8, 512])

    ld = p.sem("ld")
    dv = p.sem("dv")
    sc = p.sem("sc")
    gp = p.sem("gp")
    mm = p.sem("mm")
    so = p.sem("so")
    scs = p.sem("scs")
    sil = p.sem("sil")
    pkl = p.sem("pkl")
    gsem = p.sem("gsem")

    gpp = Guard()
    gp8 = Guard()
    ggr = Guard()
    gsi = Guard()
    gsf = Guard()
    gpk = Guard()
    gyc = Guard()

    p.op("gpsimd", "memset", onesr[:, :], 1.0, inc=gp)
    p.op("gpsimd", "memset", ones8[:, :], 1.0, inc=gp)
    p.op("gpsimd", "memset", epscol[:, :], EPS, inc=gp)
    p.op("gpsimd", "iota", iotc8[:, :], pattern=[[0, 1]], base=0,
         channel_multiplier=1, allow_small_or_imprecise_dtypes=True, inc=gp)
    p.op("gpsimd", "iota", id8[:, :], pattern=[[1, 8]], base=0,
         channel_multiplier=-1, allow_small_or_imprecise_dtypes=True, inc=gp)
    p.op("vector", "tensor_scalar", id8[:, :], id8[:, :], 0.0, None, OP.is_equal,
         waits=[(gp, gp.v)], inc=dv)
    p.drain("vector")
    p.op("gpsimd", "iota", iotf[:, :], pattern=[[1, 128]], base=0,
         channel_multiplier=-1, allow_small_or_imprecise_dtypes=True, inc=gp)
    p.op("vector", "tensor_scalar", identf[:, :], iotf[:, :], 0.0, None, OP.is_equal,
         waits=[(gp, gp.v)], inc=dv)
    p.drain("vector")

    p.op("sync", "dma_start", out=idsf[:, :], in_=idsf_i, inc=ld, incn=16)
    p.op("sync", "dma_start", out=lposf[:, :], in_=lposf_i, inc=ld, incn=16)
    p.op("sync", "dma_start", out=probsf[:, :], in_=probsf_i, inc=ld, incn=16)
    p.op("sync", "dma_start", out=cA[:, :], in_=countsA[:, :], inc=ld, incn=16)
    p.op("sync", "dma_start", out=pmc[:, :], in_=premask[:, :], inc=ld, incn=16)
    ldv = ld.v

    p.op("tensor", "matmul", pp[0:1, 0:8], pmc[0:8, 0:1], cA[0:8, :],
         start=True, stop=True, waits=gpp.need_w() + [(ld, ldv)], inc=mm)
    gpp.did_w(mm)
    p.op("vector", "tensor_copy", offsrow[:, :], pp[0:1, 0:8], waits=[(mm, mm.v)], inc=dv)
    p.drain("vector")
    gpp.did_r(dv)
    p.op("tensor", "transpose", pp[0:8, 0:1], offsrow[0:1, 0:8], id8[0:1, 0:1],
         waits=gpp.need_w() + [(dv, dv.v)], inc=mm)
    gpp.did_w(mm)
    p.op("vector", "tensor_copy", offscol[:, :], pp[0:8, 0:1], waits=[(mm, mm.v)], inc=dv)
    p.drain("vector")
    gpp.did_r(dv)
    for cb in range(N2 // 512):
        p.op("tensor", "matmul", p8[:, :], ones8[0:1, 0:8],
             idsf[0:1, cb * 512:(cb + 1) * 512], start=True, stop=True,
             waits=gp8.need_w() + [(ld, ldv)], inc=mm)
        gp8.did_w(mm)
        p.op("vector", "tensor_scalar", maskT[:, cb * 512:(cb + 1) * 512],
             p8[0:8, :], iotc8[0:8, 0:1], None, OP.is_equal,
             waits=[(mm, mm.v)], inc=dv)
        p.drain("vector")
        gp8.did_r(dv)
    p.op("vector", "tensor_scalar", maskT[:, :], maskT[:, :], offscol[0:8, 0:1],
         None, OP.mult, inc=dv)
    p.drain("vector")
    for cb in range(N2 // 512):
        p.op("tensor", "matmul", pp[0:1, 0:512], ones8[0:8, 0:1],
             maskT[0:8, cb * 512:(cb + 1) * 512], start=True, stop=True,
             waits=gpp.need_w() + [(dv, dv.v)], inc=mm)
        gpp.did_w(mm)
        p.op("vector", "tensor_copy", keepr[0:1, cb * 512:(cb + 1) * 512],
             pp[0:1, 0:512], waits=[(mm, mm.v)], inc=dv)
        p.drain("vector")
        gpp.did_r(dv)
    p.op("vector", "tensor_tensor", slotr[:, :], lposf[:, :], keepr[:, :], OP.add,
         inc=dv)
    p.drain("vector")
    p.op("vector", "tensor_scalar", keepr[:, :], slotr[:, :], 4095.0, None,
         OP.is_le, inc=dv)
    p.drain("vector")
    p.op("vector", "scalar_tensor_tensor", slotr[:, :], idsf[:, :], 4096.0,
         slotr[:, :], OP.mult, OP.add, inc=dv)
    p.drain("vector")
    p.op("vector", "tensor_tensor", slotr[:, :], slotr[:, :], keepr[:, :],
         OP.mult, inc=dv)
    p.drain("vector")
    p.op("vector", "tensor_tensor", pkr[:, :], probsf[:, :], keepr[:, :], OP.mult,
         inc=dv)
    p.drain("vector")
    p.op("sync", "dma_start", out=cscr[0:1, :], in_=slotr[:, :], waits=[(dv, dv.v)],
         inc=scs, incn=16)
    p.op("sync", "dma_start", out=cscr[1:2, :], in_=pkr[:, :], waits=[(dv, dv.v)],
         inc=scs, incn=16)
    scv = scs.v

    for tb in range(NTB):
        t0 = tb * 128
        w0 = [(so, so.v), (dv, dv.v), (gp, gp.v), (sc, sc.v)] if tb > 0 else []
        p.op("sync", "dma_start", out=stile[:, :], in_=s_i[t0:t0 + 128, :],
             waits=w0, inc=ld, incn=16)
        p.op("sync", "dma_start", out=xtile[:, :], in_=x_i[t0:t0 + 128, :],
             inc=ld, incn=16)
        ldv2 = ld.v
        for k in range(2):
            p.op("sync", "dma_start", out=sidxf[:, :],
                 in_=ap_v_dram(cscr, t0 * 2 + k, [[1, 1], [2, 128]]),
                 allow_slow_non_contiguous=True,
                 waits=[(scs, scv)] + gsf.need_w(), inc=sil, incn=16)
            gsf.did_w(sil)
            p.op("sync", "dma_start", out=pkc[:, :],
                 in_=ap_v_dram(cscr, N2 + t0 * 2 + k, [[1, 1], [2, 128]]),
                 allow_slow_non_contiguous=True,
                 waits=gpk.need_w(), inc=pkl, incn=16)
            gpk.did_w(pkl)
            p.op("vector", "tensor_copy", sidx[:, :], sidxf[:, :],
                 waits=[(sil, sil.v)] + gsi.need_w(), inc=dv)
            p.drain("vector")
            gsf.did_r(dv)
            gsi.did_w(dv)
            p.op("gpsimd", "memset", grows[:, :], 0.0,
                 waits=ggr.need_w(), inc=gp)
            p.op("gpsimd", "indirect_dma_start", out=grows[:, :], out_offset=None,
                 in_=eoall[:, :],
                 in_offset=bass.IndirectOffsetOnAxis(ap=sidx[:, 0:1], axis=0),
                 waits=[(dv, dv.v), (gp, gp.v)], inc=gsem, incn=16)
            gsi.did_r(gsem)
            ggr.did_w(gsem)
            if k == 0:
                p.op("vector", "tensor_scalar", yacc[:, :], grows[:, :],
                     pkc[0:128, 0:1], None, OP.mult,
                     waits=[(gsem, gsem.v), (pkl, pkl.v)], inc=dv)
            else:
                p.op("vector", "scalar_tensor_tensor", yacc[:, :], grows[:, :],
                     pkc[0:128, 0:1], yacc[:, :], OP.mult, OP.add,
                     waits=[(gsem, gsem.v), (pkl, pkl.v)], inc=dv)
            p.drain("vector")
            ggr.did_r(dv)
            gpk.did_r(dv)
        p.op("scalar", "activation", yacc[:, :], yacc[:, :], AF.Silu,
             waits=[(dv, dv.v)], inc=sc)
        p.drain("scalar")
        p.op("gpsimd", "tensor_tensor", stile[:, :], stile[:, :], yacc[:, :],
             OP.mult, waits=[(sc, sc.v), (ld, ldv2)], inc=gp)
        p.drain("gpsimd")
        p.op("gpsimd", "tensor_tensor", x1t[:, :], xtile[:, :], stile[:, :],
             OP.add, inc=gp)
        p.drain("gpsimd")
        gyc.did_w(gp)
        p.op("sync", "dma_start", out=x1_o[t0:t0 + 128, :], in_=x1t[:, :],
             waits=[(gp, gp.v)], inc=so, incn=16)
        p.op("vector", "tensor_copy", xtile[:, :], x1t[:, :], waits=[(gp, gp.v)],
             inc=dv)
        p.drain("vector")
        emit_ln_tile(p, xtile[:, :], 128, DM, xtile[:, :], stile[:, :], c1, c2,
                     dv, sc, epscol)
        p.op("sync", "dma_start", out=xn1_o[t0:t0 + 128, :], in_=xtile[:, :],
             waits=[(dv, dv.v)], inc=so, incn=16)
        if xn1T_o is not None:
            gtx = Guard()
            for kb in range(8):
                reg = (kb % 2) * 512
                p.op("tensor", "transpose", pp[:, reg:reg + 128],
                     xtile[:, kb * 128:(kb + 1) * 128], identf[:, :],
                     waits=(gpp.need_w() + [(dv, dv.v)]) if kb < 2 else gtx.need_w(),
                     inc=mm)
                gtx.did_w(mm)
                if kb >= 6:
                    gpp.did_w(mm)
                p.op("vector", "tensor_copy", tstg[:, kb * 128:(kb + 1) * 128],
                     pp[:, reg:reg + 128], waits=[(mm, mm.v)], inc=dv)
                p.drain("vector")
                gtx.did_r(dv)
                if kb >= 6:
                    gpp.did_r(dv)
                p.op("sync", "dma_start",
                     out=xn1T_o[kb * 128:(kb + 1) * 128, t0:t0 + 128],
                     in_=tstg[:, kb * 128:(kb + 1) * 128], waits=[(dv, dv.v)],
                     inc=so, incn=16)
        p.wait("vector", so, so.v)
        p.wait("gpsimd", so, so.v)
    for s2 in (so, scs, sil, pkl, ld, gsem):
        p.wait("sync", s2, s2.v)


def combine_io_dram(p, T=2048, with_xn1T=False, internal_xn1=False):
    N2 = 2 * T
    knd = "Internal" if internal_xn1 else "ExternalOutput"
    io = {
        "eoall": p.dram("eoall", [8 * 4096, DM], BF16),
        "countsA": p.dram("countsA", [8, 8]),
        "premask": p.dram("premask", [8, 1]),
        "idsf": p.dram("idsf", [1, N2]),
        "lposf": p.dram("lposf", [1, N2]),
        "probsf": p.dram("probsf", [1, N2]),
        "s": p.dram("s", [T, DM]),
        "x": p.dram("x", [T, DM]),
        "x1": p.dram("x1", [T, DM], kind="ExternalOutput"),
        "xn1": p.dram("xn1", [T, DM], kind=knd),
    }
    if with_xn1T:
        io["xn1T"] = p.dram("xn1T", [DM, T], BF16, kind="Internal")
    return io


def build_moe_combine(T=2048):
    p = P("moe_combine")
    io = combine_io_dram(p, T)
    emit_moe_combine(p, io, T)
    return p.build()



def emit_l1_local(p, io, T=2048):
    NCH = T // CH
    xn1T = io["xn1T"]
    xn1 = io["xn1"]
    wg = io["wg"]
    bg = io["bg"]
    H_o = io["H"]
    clr_o = io["clr"]
    hlast_o = io["hlast"]

    XT = p.sb("XTg", [128, 8 * T], BF16)
    XM = p.sb("XMg", [128, 2 * DM], BF16)
    WG = p.sb("WGg", [128, 8], BF16)
    graw = p.sb("graw", [1, T])
    lgr = p.sb("lgr", [1, T])
    lggr = p.sb("lggr", [1, T])
    clr = p.sb("clrg", [1, T])
    brow = p.sb("browg", [1, T])
    scr = p.sb("scrg", [1, T])
    onesT = p.sb("onesTg", [1, T])
    hcar = p.sb("hcarg", [1, DM])
    expc = p.sb("expcg", [1, CH])
    ncol = p.sb("ncolg", [1, 1])
    bgc = p.sb("bgcg", [1, 1])
    Etile = p.sb("Etileg", [128, CH], BF16)
    Htile = p.sb("Htileg", [128, DM])
    identb = p.sb("identbg", [128, 128], BF16)
    vneg = p.sb("vnegg", [128, 128], BF16)
    iot = p.sb("iotg", [128, 128])
    onesr = p.sb("onesrg", [1, 128])

    pg = p.ps("pgg", [128, 512])
    phh = p.ps("phhg", [128, 1024])
    pcar = p.ps("pcarg", [1, 1024])

    ld = p.sem("ld")
    lg_ = p.sem("lg")
    lx0 = p.sem("lx0")
    lx1 = p.sem("lx1")
    dv = p.sem("dv")
    sc = p.sem("sc")
    gp = p.sem("gp")
    mm = p.sem("mm")
    so = p.sem("so")

    gpg = Guard()
    gph = Guard()
    gpc = Guard()
    gxm = [Guard(), Guard()]
    ghc = Guard()
    gE = Guard()
    gHt = Guard()

    p.op("gpsimd", "memset", onesr[:, :], 1.0, inc=gp)
    p.op("gpsimd", "memset", onesT[:, :], 1.0, inc=gp)
    p.op("gpsimd", "memset", hcar[:, :], 0.0, inc=gp)
    p.op("gpsimd", "iota", iot[:, :], pattern=[[1, 128]], base=0,
         channel_multiplier=-1, allow_small_or_imprecise_dtypes=True, inc=gp)
    p.op("vector", "tensor_scalar", identb[:, :], iot[:, :], 0.0, None, OP.is_equal,
         waits=[(gp, gp.v)], inc=dv)
    p.drain("vector")
    p.op("vector", "tensor_scalar", vneg[:, :], iot[:, :], 0.0, None, OP.is_lt, inc=dv)
    p.drain("vector")
    p.op("vector", "tensor_scalar", vneg[:, :], vneg[:, :], float(NEG), None, OP.mult,
         inc=dv)
    p.drain("vector")

    p.op("gpsimd", "dma_start", out=XT[:, :].rearrange("p (k t) -> p k t", k=8),
         in_=xn1T.rearrange("(k p) t -> p k t", p=128), inc=lg_, incn=16)
    p.op("gpsimd", "dma_start", out=WG[:, :].rearrange("p (k o) -> p k o", k=8),
         in_=wg.rearrange("(k p) o -> p k o", p=128),
         allow_slow_non_contiguous=True, inc=lg_, incn=16)
    p.op("sync", "dma_start", out=bgc[:, :], in_=bg[:, :], inc=ld, incn=16)
    ldv = ld.v

    NT5 = max(1, T // 512)
    T5 = min(512, T)
    for t5 in range(NT5):
        for kb in range(8):
            p.op("tensor", "matmul", pg[0:1, 0:T5],
                 ap_v(WG, kb, [[8, 128], [1, 1]]),
                 ap_v(XT, kb * T + t5 * T5, [[8 * T, 128], [1, T5]]),
                 start=(kb == 0), stop=(kb == 7),
                 waits=(gpg.need_w() + [(lg_, lg_.v)]) if kb == 0 else (), inc=mm)
        gpg.did_w(mm)
        p.op("vector", "tensor_scalar", graw[0:1, t5 * T5:(t5 + 1) * T5],
             pg[0:1, 0:T5], bgc[0:1, 0:1], None, OP.add,
             waits=[(mm, mm.v), (ld, ldv)], inc=dv)
        p.drain("vector")
        gpg.did_r(dv)
    p.op("scalar", "activation", scr[:, :], graw[:, :], AF.Abs, waits=[(dv, dv.v)], inc=sc)
    p.drain("scalar")
    p.op("scalar", "activation", scr[:, :], scr[:, :], AF.Exp, scale=-1.0, inc=sc)
    p.drain("scalar")
    p.op("scalar", "activation", scr[:, :], scr[:, :], AF.Ln, bias=onesT[0:1, 0:1], inc=sc)
    p.drain("scalar")
    p.op("vector", "tensor_scalar", lgr[:, :], graw[:, :], 0.0, None, OP.max,
         waits=[(sc, sc.v)], inc=dv)
    p.drain("vector")
    p.op("vector", "tensor_tensor", lgr[:, :], lgr[:, :], scr[:, :], OP.add, inc=dv)
    p.drain("vector")
    p.op("vector", "tensor_scalar", lgr[:, :], lgr[:, :], -1.0, None, OP.mult, inc=dv)
    p.drain("vector")
    p.op("vector", "tensor_scalar", lggr[:, :], graw[:, :], -1.0, None, OP.mult, inc=dv)
    p.drain("vector")
    p.op("vector", "tensor_scalar", lggr[:, :], lggr[:, :], 0.0, None, OP.max, inc=dv)
    p.drain("vector")
    p.op("vector", "tensor_tensor", lggr[:, :], lggr[:, :], scr[:, :], OP.add, inc=dv)
    p.drain("vector")
    p.op("vector", "tensor_scalar", lggr[:, :], lggr[:, :], -1.0, None, OP.mult, inc=dv)
    p.drain("vector")
    p.op("vector", "tensor_tensor_scan", clr[:, :], onesT[:, :], lgr[:, :], 0.0,
         OP.mult, OP.add, inc=dv)
    p.drain("vector")
    p.op("vector", "tensor_tensor", brow[:, :], lggr[:, :], clr[:, :], OP.subtract,
         inc=dv)
    p.drain("vector")
    p.op("sync", "dma_start", out=clr_o[:, :], in_=clr[:, :], waits=[(dv, dv.v)],
         inc=so, incn=16)
    cldone = dv.v

    for c in range(NCH):
        s0 = c * CH
        xi = c % 2
        lx = lx0 if xi == 0 else lx1
        p.op("gpsimd", "dma_start", out=XM[:, xi * DM:(xi + 1) * DM],
             in_=xn1[s0:s0 + CH, :], waits=gxm[xi].need_w(), inc=lx, incn=16)
        gxm[xi].did_w(lx)
        p.op("tensor", "matmul", pg[:, 0:CH], onesr[0:1, 0:128],
             clr[0:1, s0:s0 + CH], start=True, stop=False,
             waits=gpg.need_w() + [(dv, cldone)], inc=mm)
        p.op("tensor", "matmul", pg[:, 0:CH], brow[0:1, s0:s0 + CH],
             onesr[0:1, 0:128], start=False, stop=False, inc=mm)
        p.op("tensor", "matmul", pg[:, 0:CH], identb[:, :], vneg[:, :],
             start=False, stop=True, inc=mm)
        gpg.did_w(mm)
        if c == 0:
            p.op("gpsimd", "memset", ncol[:, :], 0.0, waits=[(sc, sc.v)], inc=gp)
        else:
            p.op("vector", "tensor_scalar", ncol[:, :], clr[0:1, s0 - 1:s0], -1.0,
                 None, OP.mult, waits=gE.need_r() if False else [], inc=dv)
            p.drain("vector")
        p.op("scalar", "activation", Etile[:, :], pg[:, 0:CH], AF.Exp,
             waits=[(mm, mm.v), (gp, gp.v), (dv, dv.v)], inc=sc)
        p.drain("scalar")
        gpg.did_r(sc)
        gE.did_w(sc)
        for nb in range(2):
            p.op("tensor", "matmul", phh[:, nb * 512:(nb + 1) * 512],
                 Etile[:, :], XM[:, xi * DM + nb * 512:xi * DM + (nb + 1) * 512],
                 start=True, stop=False,
                 waits=(gph.need_w() + [(sc, sc.v), (lx, lx.v)]) if nb == 0 else (),
                 inc=mm)
        p.op("scalar", "activation", expc[:, :], clr[0:1, s0:s0 + CH], AF.Exp,
             bias=ncol[0:1, 0:1], waits=[(gp, gp.v), (dv, dv.v)], inc=sc)
        p.drain("scalar")
        gE.did_w(sc)
        for nb in range(2):
            p.op("tensor", "matmul", phh[:, nb * 512:(nb + 1) * 512],
                 expc[0:1, :], hcar[0:1, nb * 512:(nb + 1) * 512],
                 start=False, stop=True,
                 waits=[(sc, sc.v)] + ghc.need_w() if nb == 0 else (), inc=mm)
        gph.did_w(mm)
        ghc.did_r(mm)
        p.op("vector", "tensor_copy", Htile[:, :], phh[:, 0:1024],
             waits=[(mm, mm.v)] + gHt.need_w(), inc=dv)
        p.drain("vector")
        gph.did_r(dv)
        gHt.did_w(dv)
        p.op("sync", "dma_start", out=H_o[s0:s0 + CH, :], in_=Htile[:, :],
             waits=[(dv, dv.v)], inc=so, incn=16)
        gHt.did_r(so)
        for nb in range(2):
            p.op("tensor", "matmul", pcar[0:1, nb * 512:(nb + 1) * 512],
                 Etile[:, CH - 1:CH],
                 XM[:, xi * DM + nb * 512:xi * DM + (nb + 1) * 512],
                 start=True, stop=False,
                 waits=(gpc.need_w() + [(sc, sc.v), (lx, lx.v)]) if nb == 0 else (),
                 inc=mm)
            p.op("tensor", "matmul", pcar[0:1, nb * 512:(nb + 1) * 512],
                 expc[0:1, CH - 1:CH], hcar[0:1, nb * 512:(nb + 1) * 512],
                 start=False, stop=True, inc=mm)
        gpc.did_w(mm)
        gxm[xi].did_r(mm)
        p.op("vector", "tensor_copy", hcar[:, :], pcar[0:1, 0:1024],
             waits=[(mm, mm.v)] + ghc.need_w() + ghc.need_r(), inc=dv)
        p.drain("vector")
        gpc.did_r(dv)
        ghc.did_w(dv)
    p.op("vector", "tensor_copy", Htile[0:1, :], hcar[:, :],
         waits=gHt.need_w() + [(so, so.v)], inc=dv)
    p.drain("vector")
    p.op("sync", "dma_start", out=hlast_o[:, :], in_=Htile[0:1, 0:DM],
         waits=[(dv, dv.v)], inc=so, incn=16)
    for s2 in (so, ld, lg_, lx0, lx1):
        p.wait("sync", s2, s2.v)


def build_l1_local(T=2048):
    p = P("l1_local")
    io = {
        "xn1T": p.dram("xn1T", [DM, T]),
        "xn1": p.dram("xn1", [T, DM]),
        "wg": p.dram("wg", [DM, 1]),
        "bg": p.dram("bg", [1, 1]),
        "H": p.dram("H", [T, DM], kind="ExternalOutput"),
        "clr": p.dram("clr", [1, T], kind="ExternalOutput"),
        "hlast": p.dram("hlast", [1, DM], kind="ExternalOutput"),
    }
    emit_l1_local(p, io, T)
    return p.build()



def emit_pkm(p, io, T=2048):
    NTB = T // 128
    xn1T = io["xn1T"]
    xn1 = io["xn1"]
    wqT = io["wqT"]
    keysr = io["keysr"]
    winT = io["winT"]
    woutT = io["woutT"]
    in_a = io["in_a"]
    in_b = io["in_b"]
    out_a = io["out_a"]
    out_b = io["out_b"]
    pkm_o = io["pkm"]

    WQ = p.sb("WQp", [128, 2 * 4096], BF16)
    XT = p.sb("XTp", [128, 8 * T], BF16)
    XM = p.sb("XMp", [128, DM], BF16)
    KY = p.sb("KYp", [128, 2 * 4 * 128], BF16)
    QB = p.sb("QBp", [128, 4 * 128], BF16)
    SIMS = p.sb("SIMSp", [128, 2 * 1024])
    KX = p.sb("KXp", [128, 8])
    SX = p.sb("SXp", [128, 8])
    KYc = p.sb("KYcp", [128, 8])
    SY = p.sb("SYp", [128, 8])
    GT = p.sb("GTp", [128, 8])
    EXI = p.sb("EXIp", [128, 8], U32)
    EXF = p.sb("EXFp", [128, 8])
    mxc = p.sb("mxcp", [128, 1])
    msks = p.sb("msksp", [128, 1024])
    iotsg = p.sb("iotsgp", [128, 1024])
    AR = p.sb("ARp", [128, DI], BF16)
    BR = p.sb("BRp", [128, DI], BF16)
    lin = p.sb("linp", [128, 8])
    hid = p.sb("hidp", [128, DI])
    hidb = p.sb("hidbp", [128, DI], BF16)
    hT = p.sb("hTp", [128, 16 * 128], BF16)
    outt = p.sb("outtp", [128, DM])
    scr2 = p.sb("scr2p", [128, DI])
    WIN = p.sb("WINp", [128, 8 * DI], BF16)
    WOU = p.sb("WOUp", [128, 16 * DM], BF16)
    identb = p.sb("identbp", [128, 128], BF16)
    iotp = p.sb("iotpp", [128, 128])

    psim = p.ps("psimp", [128, 1024])
    pq = p.ps("pqp", [128, 1024])
    pwo = p.ps("pwop", [128, 1024])
    ptp = p.ps("ptpp", [128, 1024], BF16)
    ptp2 = p.ps("ptp2p", [128, 1024], BF16)

    ld = p.sem("ld")
    lg_ = p.sem("lg")
    lk0 = p.sem("lk0")
    lk1 = p.sem("lk1")
    lwq = p.sem("lwq")
    lxm = p.sem("lxm")
    ga0 = p.sem("ga0")
    gb0 = p.sem("gb0")
    dv = p.sem("dv")
    sc = p.sem("sc")
    gp = p.sem("gp")
    mm = p.sem("mm")
    so = p.sem("so")

    gsim = [Guard(), Guard()]
    gq = [Guard(), Guard()]
    gwo = [Guard(), Guard()]
    gtp = [Guard(), Guard()]
    gky = [Guard(), Guard()]
    gar = Guard()
    gbr = Guard()
    ghid = Guard()
    gout = Guard()
    gxm = Guard()
    gwq = Guard()

    p.op("gpsimd", "iota", iotp[:, :], pattern=[[1, 128]], base=0,
         channel_multiplier=-1, allow_small_or_imprecise_dtypes=True, inc=gp)
    p.op("gpsimd", "iota", iotsg[:, :], pattern=[[0, 8], [1, 128]], base=1000000,
         channel_multiplier=0, allow_small_or_imprecise_dtypes=True, inc=gp)
    p.op("vector", "tensor_scalar", identb[:, :], iotp[:, :], 0.0, None, OP.is_equal,
         waits=[(gp, gp.v)], inc=dv)
    p.drain("vector")

    p.op("gpsimd", "dma_start", out=XT[:, :].rearrange("p (k t) -> p k t", k=8),
         in_=xn1T.rearrange("(k p) t -> p k t", p=128), inc=lg_, incn=16)
    p.op("gpsimd", "dma_start", out=WIN[:, :].rearrange("p (k o) -> p k o", k=8),
         in_=winT.rearrange("(k p) o -> p k o", p=128), inc=lg_, incn=16)
    p.op("gpsimd", "dma_start", out=WOU[:, :].rearrange("p (k o) -> p k o", k=16),
         in_=woutT.rearrange("(k p) o -> p k o", p=128), inc=lg_, incn=16)
    lgv = lg_.v

    for tb in range(NTB):
        t0 = tb * 128
        p.op("gpsimd", "dma_start", out=XM[:, :], in_=xn1[t0:t0 + 128, :],
             waits=gxm.need_w(), inc=lxm, incn=16)
        gxm.did_w(lxm)
        for ph in range(2):
            si = ph % 2
            for h in range(8):
                qs = h % 2
                p.op("gpsimd", "dma_start",
                     out=WQ[:, qs * 4096:qs * 4096 + 8 * 512].rearrange(
                         "p (k o) -> p k o", k=8),
                     in_=wqT[:, ph * 4096 + h * 512:ph * 4096 + (h + 1) * 512].rearrange(
                         "(k p) o -> p k o", p=128),
                     waits=gwq.need_w() if True else [], inc=lwq, incn=16)
                gwq.did_w(lwq)
                for qb in range(4):
                    ri = qb % 2
                    for kb in range(8):
                        p.op("tensor", "matmul", pq[:, ri * 512:ri * 512 + 128],
                             ap_v(WQ, qs * 4096 + kb * 512 + qb * 128,
                                  [[2 * 4096, 128], [1, 128]]),
                             ap_v(XT, kb * T + t0, [[8 * T, 128], [1, 128]]),
                             start=(kb == 0), stop=(kb == 7),
                             waits=(gq[ri].need_w() + [(lwq, lwq.v), (lg_, lgv)])
                             if kb == 0 else (), inc=mm)
                    gq[ri].did_w(mm)
                    p.op("vector", "tensor_copy", QB[:, qb * 128:(qb + 1) * 128],
                         pq[:, ri * 512:ri * 512 + 128], waits=[(mm, mm.v)], inc=dv)
                    p.drain("vector")
                    gq[ri].did_r(dv)
                ks = h % 2
                ksem = lk0 if ks == 0 else lk1
                p.op("gpsimd", "dma_start",
                     out=KY[:, ks * 512:(ks + 1) * 512].rearrange("p (k o) -> p k o", k=4),
                     in_=keysr[(ph * 8 + h) * 512:(ph * 8 + h + 1) * 512, :].rearrange(
                         "(k p) o -> p k o", p=128),
                     waits=gky[ks].need_w(), inc=ksem, incn=16)
                gky[ks].did_w(ksem)
                for qb in range(4):
                    p.op("tensor", "matmul",
                         psim[:, si * 512 + (h % 4) * 128:si * 512 + (h % 4) * 128 + 128]
                         if False else psim[:, (h // 4) * 512 + (h % 4) * 128:
                                            (h // 4) * 512 + (h % 4) * 128 + 128],
                         QB[:, qb * 128:(qb + 1) * 128],
                         KY[:, ks * 512 + qb * 128:ks * 512 + (qb + 1) * 128],
                         start=(qb == 0 and h % 4 == 0) if False else (qb == 0),
                         stop=(qb == 3),
                         waits=(gsim[h // 4].need_w() + [(ksem, ksem.v), (dv, dv.v)])
                         if qb == 0 else (), inc=mm)
                gsim[h // 4].did_w(mm)
                gky[ks].did_r(mm)
                gwq.did_r(mm)
            p.op("vector", "tensor_copy", SIMS[:, ph * 1024:(ph + 1) * 1024],
                 psim[:, 0:1024], waits=[(mm, mm.v)], inc=dv)
            p.drain("vector")
            gsim[0].did_r(dv)
            gsim[1].did_r(dv)
            sv = SIMS[:, ph * 1024:(ph + 1) * 1024].rearrange("p (h k) -> p h k", h=8)
            mxd = SX if ph == 0 else SY
            kxd = KX if ph == 0 else KYc
            p.op("vector", "tensor_reduce", mxd[:, :], sv, AX.X, OP.max, inc=dv)
            p.drain("vector")
            p.op("vector", "tensor_tensor", msks[:, :].rearrange("p (h k) -> p h k", h=8),
                 sv, ap_v(mxd, 0, [[8, 128], [1, 8], [0, 128]]), OP.is_ge, inc=dv)
            p.drain("vector")
            p.op("vector", "scalar_tensor_tensor", msks[:, :], msks[:, :],
                 -1000000.0, iotsg[:, :].rearrange("p f -> p f") if False else iotsg[:, :],
                 OP.mult, OP.add, inc=dv)
            p.drain("vector")
            p.op("vector", "tensor_reduce", kxd[:, :],
                 msks[:, :].rearrange("p (h k) -> p h k", h=8), AX.X, OP.min, inc=dv)
            p.drain("vector")
        p.op("vector", "scalar_tensor_tensor", EXF[:, :], KX[:, :], 128.0, KYc[:, :],
             OP.mult, OP.add, inc=dv)
        p.drain("vector")
        p.op("vector", "tensor_tensor", GT[:, :], SX[:, :], SY[:, :], OP.add, inc=dv)
        p.drain("vector")
        p.op("vector", "tensor_scalar", GT[:, :], GT[:, :], 0.0, None, OP.max, inc=dv)
        p.drain("vector")
        p.op("vector", "tensor_copy", EXI[:, :], EXF[:, :], inc=dv)
        p.drain("vector")
        exdone = dv.v

        for ob in range(4):
            ri = ob % 2
            for kb in range(8):
                p.op("tensor", "matmul", pwo[:, ri * 512:(ri + 1) * 512],
                     ap_v(XT, kb * T + t0, [[8 * T, 128], [1, 128]]),
                     ap_v(WIN, kb * DI + ob * 512, [[8 * DI, 128], [1, 512]]),
                     start=(kb == 0), stop=(kb == 7),
                     waits=(gwo[ri].need_w() + [(lg_, lgv)]) if kb == 0 else (), inc=mm)
            gwo[ri].did_w(mm)
            p.op("vector", "tensor_copy", hid[:, ob * 512:(ob + 1) * 512],
                 pwo[:, ri * 512:(ri + 1) * 512],
                 waits=[(mm, mm.v)] + (ghid.need_w() if ob == 0 else []), inc=dv)
            p.drain("vector")
            gwo[ri].did_r(dv)
        ghid.did_w(dv)
        for h in range(8):
            p.op("gpsimd", "indirect_dma_start", out=AR[:, 0:DM], out_offset=None,
                 in_=in_a[:, :],
                 in_offset=bass.IndirectOffsetOnAxis(ap=EXI[:, h:h + 1], axis=0),
                 waits=gar.need_w() + [(dv, exdone)], inc=ga0, incn=16)
            gar.did_w(ga0)
            p.op("gpsimd", "indirect_dma_start", out=BR[:, :], out_offset=None,
                 in_=in_b[:, :],
                 in_offset=bass.IndirectOffsetOnAxis(ap=EXI[:, h:h + 1], axis=0),
                 waits=gbr.need_w(), inc=gb0, incn=16)
            gbr.did_w(gb0)
            p.op("gpsimd", "tensor_tensor", scr2[:, 0:DM], XM[:, :], AR[:, 0:DM],
                 OP.mult, waits=[(ga0, ga0.v), (lxm, lxm.v), (dv, dv.v)], inc=gp)
            p.drain("gpsimd")
            gar.did_r(gp)
            p.op("vector", "tensor_reduce", lin[:, h:h + 1], scr2[:, 0:DM], AX.X,
                 OP.add, waits=[(gp, gp.v)], inc=dv)
            p.drain("vector")
            p.op("vector", "tensor_tensor", lin[:, h:h + 1], lin[:, h:h + 1],
                 GT[:, h:h + 1], OP.mult, inc=dv)
            p.drain("vector")
            p.op("vector", "scalar_tensor_tensor", hid[:, :], BR[:, :],
                 lin[0:128, h:h + 1], hid[:, :], OP.mult, OP.add,
                 waits=[(gb0, gb0.v)], inc=dv)
            p.drain("vector")
            gbr.did_r(dv)
        p.op("scalar", "activation", hidb[:, :], hid[:, :], AF.Gelu,
             waits=[(dv, dv.v)], inc=sc)
        p.drain("scalar")
        ghid.did_w(sc)
        for kb in range(16):
            dstp = ptp if kb % 2 == 0 else ptp2
            gtj = gtp[kb % 2]
            p.op("tensor", "transpose", dstp[:, 0:128],
                 hidb[:, kb * 128:(kb + 1) * 128], identb[:, :],
                 waits=gtj.need_w() + [(sc, sc.v), (dv, dv.v)], inc=mm)
            gtj.did_w(mm)
            p.op("vector", "tensor_copy", hT[:, kb * 128:(kb + 1) * 128],
                 dstp[:, 0:128], waits=[(mm, mm.v)], inc=dv)
            p.drain("vector")
            gtj.did_r(dv)
        for ob in range(2):
            ri = ob % 2
            for kb in range(16):
                p.op("tensor", "matmul", pwo[:, ri * 512:(ri + 1) * 512],
                     hT[:, kb * 128:(kb + 1) * 128],
                     ap_v(WOU, kb * DM + ob * 512, [[16 * DM, 128], [1, 512]]),
                     start=(kb == 0), stop=(kb == 15),
                     waits=(gwo[ri].need_w() + [(dv, dv.v), (lg_, lgv)])
                     if kb == 0 else (), inc=mm)
            gwo[ri].did_w(mm)
            p.op("vector", "tensor_copy", outt[:, ob * 512:(ob + 1) * 512],
                 pwo[:, ri * 512:(ri + 1) * 512],
                 waits=[(mm, mm.v)] + (gout.need_w() if ob == 0 else []), inc=dv)
            p.drain("vector")
            gwo[ri].did_r(dv)
        gout.did_w(dv)
        for h in range(8):
            p.op("gpsimd", "indirect_dma_start", out=AR[:, :], out_offset=None,
                 in_=out_a[:, :],
                 in_offset=bass.IndirectOffsetOnAxis(ap=EXI[:, h:h + 1], axis=0),
                 waits=gar.need_w() + [(dv, dv.v)], inc=ga0, incn=16)
            gar.did_w(ga0)
            p.op("gpsimd", "indirect_dma_start", out=BR[:, 0:DM], out_offset=None,
                 in_=out_b[:, :],
                 in_offset=bass.IndirectOffsetOnAxis(ap=EXI[:, h:h + 1], axis=0),
                 waits=gbr.need_w(), inc=gb0, incn=16)
            gbr.did_w(gb0)
            p.op("gpsimd", "tensor_tensor", scr2[:, :], hidb[:, :], AR[:, :],
                 OP.mult, waits=[(ga0, ga0.v), (sc, sc.v), (dv, dv.v)], inc=gp)
            p.drain("gpsimd")
            gar.did_r(gp)
            p.op("vector", "tensor_reduce", lin[:, h:h + 1], scr2[:, :], AX.X,
                 OP.add, waits=[(gp, gp.v)], inc=dv)
            p.drain("vector")
            p.op("vector", "tensor_tensor", lin[:, h:h + 1], lin[:, h:h + 1],
                 GT[:, h:h + 1], OP.mult, inc=dv)
            p.drain("vector")
            p.op("vector", "scalar_tensor_tensor", outt[:, :], BR[:, 0:DM],
                 lin[0:128, h:h + 1], outt[:, :], OP.mult, OP.add,
                 waits=[(gb0, gb0.v)], inc=dv)
            p.drain("vector")
            gbr.did_r(dv)
        gout.did_w(dv)
        p.op("sync", "dma_start", out=pkm_o[t0:t0 + 128, :], in_=outt[:, :],
             waits=[(dv, dv.v)], inc=so, incn=16)
        gout.did_r(so)
        gxm.did_r(dv)
        ghid.did_r(dv)
    for s2 in (so, ld, lg_, lk0, lk1, lwq, lxm, ga0, gb0):
        p.wait("sync", s2, s2.v)


def emit_pkm2(p, io, T=2048):
    NTB = T // 128
    xn1T = io["xn1T"]
    xn1 = io["xn1"]
    wqT = io["wqT"]
    keysr = io["keysr"]
    winT = io["winT"]
    woutT = io["woutT"]
    in_a = io["in_a"]
    in_b = io["in_b"]
    out_a = io["out_a"]
    out_b = io["out_b"]
    pkm_o = io["pkm"]

    XT = p.sb("XT2", [128, 8 * T], BF16)
    WQ = p.sb("WQ2", [128, 2 * 8 * 512], BF16)
    KY = p.sb("KY2", [128, 2 * 4 * 128], BF16)
    QF = p.sb("QF2", [128, 4 * T], BF16)
    SIMT = p.sb("SIMT2", [128, 2 * 128])
    SXa = p.sb("SXa2", [128, 16 * NTB])
    KXa = p.sb("KXa2", [128, 16 * NTB])
    msks = p.sb("msks2", [128, 128])
    iotk = p.sb("iotk2", [128, 128])
    mxc = p.sb("mxc2", [128, 1])
    XM = p.sb("XM2", [128, DM], BF16)
    GT = p.sb("GT2", [128, 8])
    EXI = p.sb("EXI2", [128, 8], U32)
    EXF = p.sb("EXF2", [128, 8])
    AR = p.sb("AR2", [128, DI], BF16)
    BR = p.sb("BR2", [128, DI], BF16)
    lin = p.sb("lin2", [128, 8])
    hid = p.sb("hid2", [128, DI])
    hidb = p.sb("hidb2", [128, DI], BF16)
    hT = p.sb("hT2", [128, 16 * 128], BF16)
    outt = p.sb("outt2", [128, DM])
    scr2 = p.sb("scr22", [128, DI])
    WIN = p.sb("WIN2", [128, 8 * DI], BF16)
    WOU = p.sb("WOU2", [128, 16 * DM], BF16)
    identb = p.sb("identb2p", [128, 128], BF16)
    iotp = p.sb("iotp2", [128, 128])

    pq = p.ps("pq2", [128, 1024])
    psim = p.ps("psim2", [128, 1024])
    pwo = p.ps("pwo2", [128, 1024])
    ptp = p.ps("ptp2a", [128, 1024], BF16)
    ptp2 = p.ps("ptp2b", [128, 1024], BF16)

    ld = p.sem("ld")
    lg_ = p.sem("lg")
    lk0 = p.sem("lk0")
    lk1 = p.sem("lk1")
    lwq0 = p.sem("lwq0")
    lwq1 = p.sem("lwq1")
    lxm = p.sem("lxm")
    ga0 = p.sem("ga0")
    gb0 = p.sem("gb0")
    dv = p.sem("dv")
    sc = p.sem("sc")
    gp = p.sem("gp")
    mm = p.sem("mm")
    so = p.sem("so")

    gq = [Guard(), Guard()]
    gsim = [Guard(), Guard()]
    gsimt = [Guard(), Guard()]
    gwq = [Guard(), Guard()]
    gky = [Guard(), Guard()]
    gqf = Guard()
    gwo = [Guard(), Guard()]
    gtp = [Guard(), Guard()]
    gar = Guard()
    gbr = Guard()
    ghid = Guard()
    gout = Guard()
    gxm = Guard()

    p.op("gpsimd", "iota", iotp[:, :], pattern=[[1, 128]], base=0,
         channel_multiplier=-1, allow_small_or_imprecise_dtypes=True, inc=gp)
    p.op("gpsimd", "iota", iotk[:, :], pattern=[[1, 128]], base=1000000,
         channel_multiplier=0, allow_small_or_imprecise_dtypes=True, inc=gp)
    p.op("vector", "tensor_scalar", identb[:, :], iotp[:, :], 0.0, None, OP.is_equal,
         waits=[(gp, gp.v)], inc=dv)
    p.drain("vector")

    p.op("gpsimd", "dma_start", out=XT[:, :].rearrange("p (k t) -> p k t", k=8),
         in_=xn1T.rearrange("(k p) t -> p k t", p=128), inc=lg_, incn=16)
    p.op("gpsimd", "dma_start", out=WIN[:, :].rearrange("p (k o) -> p k o", k=8),
         in_=winT.rearrange("(k p) o -> p k o", p=128), inc=lg_, incn=16)
    p.op("gpsimd", "dma_start", out=WOU[:, :].rearrange("p (k o) -> p k o", k=16),
         in_=woutT.rearrange("(k p) o -> p k o", p=128), inc=lg_, incn=16)
    lgv = lg_.v

    for phh in range(16):
        ph, h = phh // 8, phh % 8
        s = phh % 2
        wsem = lwq0 if s == 0 else lwq1
        c0 = ph * 4096 + h * 512
        p.op("gpsimd", "dma_start",
             out=WQ[:, s * 4096:(s + 1) * 4096].rearrange("p (k o) -> p k o", k=8),
             in_=wqT[:, c0:c0 + 512].rearrange("(k p) o -> p k o", p=128),
             waits=gwq[s].need_w(), inc=wsem, incn=16)
        gwq[s].did_w(wsem)
        ksem = lk0 if s == 0 else lk1
        p.op("gpsimd", "dma_start",
             out=KY[:, s * 512:(s + 1) * 512].rearrange("p (k o) -> p k o", k=4),
             in_=keysr[phh * 512:(phh + 1) * 512, :].rearrange("(k p) o -> p k o", p=128),
             waits=gky[s].need_w(), inc=ksem, incn=16)
        gky[s].did_w(ksem)
        for md in range(4):
            for t5 in range(T // 512):
                ri = (md * (T // 512) + t5) % 2
                reg = ri * 512
                for kb in range(8):
                    p.op("tensor", "matmul", pq[:, reg:reg + 512],
                         ap_v(WQ, s * 4096 + kb * 512 + md * 128,
                              [[2 * 8 * 512, 128], [1, 128]]),
                         ap_v(XT, kb * T + t5 * 512, [[8 * T, 128], [1, 512]]),
                         start=(kb == 0), stop=(kb == 7),
                         waits=(gq[ri].need_w() + [(wsem, wsem.v), (lg_, lgv)] +
                                (gqf.need_r() if (md == 0 and t5 == 0) else []))
                         if kb == 0 else (), inc=mm)
                gq[ri].did_w(mm)
                p.op("vector", "tensor_copy", QF[:, md * T + t5 * 512:md * T + (t5 + 1) * 512],
                     pq[:, reg:reg + 512], waits=[(mm, mm.v)], inc=dv)
                p.drain("vector")
                gq[ri].did_r(dv)
        gwq[s].did_r(mm)
        gqf.did_w(dv)
        for tb in range(NTB):
            si = tb % 2
            reg = si * 512
            for md in range(4):
                p.op("tensor", "matmul", psim[:, reg:reg + 128],
                     ap_v(QF, md * T + tb * 128, [[4 * T, 128], [1, 128]]),
                     KY[:, s * 512 + md * 128:s * 512 + (md + 1) * 128],
                     start=(md == 0), stop=(md == 3),
                     waits=(gsim[si].need_w() + [(ksem, ksem.v), (dv, dv.v)])
                     if md == 0 else (), inc=mm)
            gsim[si].did_w(mm)
            gky[s].did_r(mm)
            p.op("vector", "tensor_copy", SIMT[:, si * 128:(si + 1) * 128],
                 psim[:, reg:reg + 128],
                 waits=[(mm, mm.v)] + gsimt[si].need_w(), inc=dv)
            p.drain("vector")
            gsim[si].did_r(dv)
            gsimt[si].did_w(dv)
            sv = SIMT[:, si * 128:(si + 1) * 128]
            col = phh * NTB + tb
            p.op("vector", "tensor_reduce", mxc[:, :], sv, AX.X, OP.max, inc=dv)
            p.drain("vector")
            p.op("vector", "tensor_copy", SXa[:, col:col + 1], mxc[:, :], inc=dv)
            p.drain("vector")
            p.op("vector", "tensor_scalar", msks[:, :], sv, mxc[0:128, 0:1], None,
                 OP.is_ge, inc=dv)
            p.drain("vector")
            p.op("vector", "scalar_tensor_tensor", msks[:, :], msks[:, :],
                 -1000000.0, iotk[:, :], OP.mult, OP.add,
                 waits=[(gp, gp.v)], inc=dv)
            p.drain("vector")
            p.op("vector", "tensor_reduce", KXa[:, col:col + 1], msks[:, :],
                 AX.X, OP.min, inc=dv)
            p.drain("vector")
            gsimt[si].did_r(dv)
    phA_dv = dv.v

    for tb in range(NTB):
        t0 = tb * 128
        p.op("gpsimd", "dma_start", out=XM[:, :], in_=xn1[t0:t0 + 128, :],
             waits=gxm.need_w(), inc=lxm, incn=16)
        gxm.did_w(lxm)
        sx0 = ap_v(SXa, 0 * NTB + tb, [[16 * NTB, 128], [NTB, 8]])
        sx1 = ap_v(SXa, 8 * NTB + tb, [[16 * NTB, 128], [NTB, 8]])
        kx0 = ap_v(KXa, 0 * NTB + tb, [[16 * NTB, 128], [NTB, 8]])
        kx1 = ap_v(KXa, 8 * NTB + tb, [[16 * NTB, 128], [NTB, 8]])
        p.op("vector", "tensor_tensor", GT[:, :], sx0, sx1, OP.add,
             waits=[(dv, phA_dv)] + ([(gb0, gb0.v)] if tb > 0 else []), inc=dv)
        p.drain("vector")
        p.op("vector", "tensor_scalar", GT[:, :], GT[:, :], 0.0, None, OP.max, inc=dv)
        p.drain("vector")
        p.op("vector", "scalar_tensor_tensor", EXF[:, :], kx0, 128.0, kx1,
             OP.mult, OP.add, inc=dv)
        p.drain("vector")
        p.op("vector", "tensor_copy", EXI[:, :], EXF[:, :], inc=dv)
        p.drain("vector")
        exdone = dv.v

        for ob in range(4):
            ri = ob % 2
            for kb in range(8):
                p.op("tensor", "matmul", pwo[:, ri * 512:(ri + 1) * 512],
                     ap_v(XT, kb * T + t0, [[8 * T, 128], [1, 128]]),
                     ap_v(WIN, kb * DI + ob * 512, [[8 * DI, 128], [1, 512]]),
                     start=(kb == 0), stop=(kb == 7),
                     waits=(gwo[ri].need_w() + [(lg_, lgv)]) if kb == 0 else (), inc=mm)
            gwo[ri].did_w(mm)
            p.op("vector", "tensor_copy", hid[:, ob * 512:(ob + 1) * 512],
                 pwo[:, ri * 512:(ri + 1) * 512],
                 waits=[(mm, mm.v)] + (ghid.need_w() if ob == 0 else []), inc=dv)
            p.drain("vector")
            gwo[ri].did_r(dv)
        ghid.did_w(dv)
        for h in range(8):
            p.op("gpsimd", "indirect_dma_start", out=AR[:, 0:DM], out_offset=None,
                 in_=in_a[:, :],
                 in_offset=bass.IndirectOffsetOnAxis(ap=EXI[:, h:h + 1], axis=0),
                 waits=gar.need_w() + [(dv, exdone)], inc=ga0, incn=16)
            gar.did_w(ga0)
            p.op("gpsimd", "indirect_dma_start", out=BR[:, :], out_offset=None,
                 in_=in_b[:, :],
                 in_offset=bass.IndirectOffsetOnAxis(ap=EXI[:, h:h + 1], axis=0),
                 waits=gbr.need_w(), inc=gb0, incn=16)
            gbr.did_w(gb0)
            p.op("gpsimd", "tensor_tensor", scr2[:, 0:DM], XM[:, :], AR[:, 0:DM],
                 OP.mult, waits=[(ga0, ga0.v), (lxm, lxm.v), (dv, dv.v)], inc=gp)
            p.drain("gpsimd")
            gar.did_r(gp)
            p.op("vector", "tensor_reduce", lin[:, h:h + 1], scr2[:, 0:DM], AX.X,
                 OP.add, waits=[(gp, gp.v)], inc=dv)
            p.drain("vector")
            p.op("vector", "tensor_tensor", lin[:, h:h + 1], lin[:, h:h + 1],
                 GT[:, h:h + 1], OP.mult, inc=dv)
            p.drain("vector")
            p.op("vector", "scalar_tensor_tensor", hid[:, :], BR[:, :],
                 lin[0:128, h:h + 1], hid[:, :], OP.mult, OP.add,
                 waits=[(gb0, gb0.v)], inc=dv)
            p.drain("vector")
            gbr.did_r(dv)
        p.op("scalar", "activation", hidb[:, :], hid[:, :], AF.Gelu,
             waits=[(dv, dv.v)], inc=sc)
        p.drain("scalar")
        ghid.did_w(sc)
        for kb in range(16):
            dstp = ptp if kb % 2 == 0 else ptp2
            gtj = gtp[kb % 2]
            p.op("tensor", "transpose", dstp[:, 0:128],
                 hidb[:, kb * 128:(kb + 1) * 128], identb[:, :],
                 waits=gtj.need_w() + [(sc, sc.v), (dv, dv.v)], inc=mm)
            gtj.did_w(mm)
            p.op("vector", "tensor_copy", hT[:, kb * 128:(kb + 1) * 128],
                 dstp[:, 0:128], waits=[(mm, mm.v)], inc=dv)
            p.drain("vector")
            gtj.did_r(dv)
        for ob in range(2):
            ri = ob % 2
            for kb in range(16):
                p.op("tensor", "matmul", pwo[:, ri * 512:(ri + 1) * 512],
                     hT[:, kb * 128:(kb + 1) * 128],
                     ap_v(WOU, kb * DM + ob * 512, [[16 * DM, 128], [1, 512]]),
                     start=(kb == 0), stop=(kb == 15),
                     waits=(gwo[ri].need_w() + [(dv, dv.v), (lg_, lgv)])
                     if kb == 0 else (), inc=mm)
            gwo[ri].did_w(mm)
            p.op("vector", "tensor_copy", outt[:, ob * 512:(ob + 1) * 512],
                 pwo[:, ri * 512:(ri + 1) * 512],
                 waits=[(mm, mm.v)] + (gout.need_w() if ob == 0 else []), inc=dv)
            p.drain("vector")
            gwo[ri].did_r(dv)
        gout.did_w(dv)
        for h in range(8):
            p.op("gpsimd", "indirect_dma_start", out=AR[:, :], out_offset=None,
                 in_=out_a[:, :],
                 in_offset=bass.IndirectOffsetOnAxis(ap=EXI[:, h:h + 1], axis=0),
                 waits=gar.need_w() + [(dv, dv.v)], inc=ga0, incn=16)
            gar.did_w(ga0)
            p.op("gpsimd", "indirect_dma_start", out=BR[:, 0:DM], out_offset=None,
                 in_=out_b[:, :],
                 in_offset=bass.IndirectOffsetOnAxis(ap=EXI[:, h:h + 1], axis=0),
                 waits=gbr.need_w(), inc=gb0, incn=16)
            gbr.did_w(gb0)
            p.op("gpsimd", "tensor_tensor", scr2[:, :], hidb[:, :], AR[:, :],
                 OP.mult, waits=[(ga0, ga0.v), (sc, sc.v), (dv, dv.v)], inc=gp)
            p.drain("gpsimd")
            gar.did_r(gp)
            p.op("vector", "tensor_reduce", lin[:, h:h + 1], scr2[:, :], AX.X,
                 OP.add, waits=[(gp, gp.v)], inc=dv)
            p.drain("vector")
            p.op("vector", "tensor_tensor", lin[:, h:h + 1], lin[:, h:h + 1],
                 GT[:, h:h + 1], OP.mult, inc=dv)
            p.drain("vector")
            p.op("vector", "scalar_tensor_tensor", outt[:, :], BR[:, 0:DM],
                 lin[0:128, h:h + 1], outt[:, :], OP.mult, OP.add,
                 waits=[(gb0, gb0.v)], inc=dv)
            p.drain("vector")
            gbr.did_r(dv)
        gout.did_w(dv)
        p.op("sync", "dma_start", out=pkm_o[t0:t0 + 128, :], in_=outt[:, :],
             waits=[(dv, dv.v)], inc=so, incn=16)
        gout.did_r(so)
        gxm.did_r(dv)
        ghid.did_r(dv)
    for s2 in (so, ld, lg_, lk0, lk1, lwq0, lwq1, lxm, ga0, gb0):
        p.wait("sync", s2, s2.v)


def pkm_io_dram(p, T=2048):
    return {
        "xn1T": p.dram("xn1T_p", [DM, T]),
        "xn1": p.dram("xn1_p", [T, DM]),
        "wqT": p.dram("wqT", [DM, 8192]),
        "keysr": p.dram("keysr", [2 * 8 * 512, 128]),
        "winT": p.dram("winT", [DM, DI]),
        "woutT": p.dram("woutT", [DI, DM]),
        "in_a": p.dram("in_a", [16384, DM]),
        "in_b": p.dram("in_b", [16384, DI]),
        "out_a": p.dram("out_a", [16384, DI]),
        "out_b": p.dram("out_b", [16384, DM]),
        "pkm": p.dram("pkm", [T, DM], kind="ExternalOutput"),
    }


def build_pkm(T=2048):
    p = P("pkm")
    emit_pkm(p, pkm_io_dram(p), T)
    return p.build()


def build_launchC(T=2048):
    p = P("launchC")
    io_c = combine_io_dram(p, T, with_xn1T=True, internal_xn1=True)
    io_l = {
        "xn1T": io_c["xn1T"],
        "xn1": io_c["xn1"],
        "wg": p.dram("wg", [DM, 1]),
        "bg": p.dram("bg", [1, 1]),
        "H": p.dram("H", [T, DM], kind="ExternalOutput"),
        "clr": p.dram("clr", [1, T], kind="ExternalOutput"),
        "hlast": p.dram("hlast", [1, DM], kind="ExternalOutput"),
    }
    io_p = {
        "xn1T": io_c["xn1T"],
        "xn1": io_c["xn1"],
        "wqT": p.dram("wqT", [DM, 8192]),
        "keysr": p.dram("keysr", [2 * 8 * 512, 128]),
        "winT": p.dram("winT", [DM, DI]),
        "woutT": p.dram("woutT", [DI, DM]),
        "in_a": p.dram("in_a", [16384, DM]),
        "in_b": p.dram("in_b", [16384, DI]),
        "out_a": p.dram("out_a", [16384, DI]),
        "out_b": p.dram("out_b", [16384, DM]),
        "pkm": p.dram("pkm", [T, DM], kind="ExternalOutput"),
    }
    p.push_scope()
    emit_moe_combine(p, io_c, T)
    p.pop_scope()
    p.barrier()
    p.push_scope()
    emit_l1_local(p, io_l, T)
    p.pop_scope()
    p.barrier()
    p.push_scope()
    emit_pkm2(p, io_p, T)
    p.pop_scope()
    p.barrier()
    return p.build()



def emit_l1_fin(p, io, T=2048):
    NTB = T // 128
    H_i = io["H"]
    clr_i = io["clr"]
    clrall_i = io["clrall"]
    hlasts_i = io["hlasts"]
    mrow_i = io["mrow"]
    irow_i = io["irow"]
    woT = io["woT"]
    bout_i = io["bout"]
    pkm_i = io["pkm"]
    x1_i = io["x1"]
    out_o = io["out"]
    clend_i = ap_v_dram(clrall_i, T - 1, [[1, 1], [T, 8]])

    WO = p.sb("WOf", [128, 8 * DM], BF16)
    HL = p.sb("HLf", [8, DM])
    Tac = p.sb("Tacf", [1, DM])
    hinb = p.sb("hinbf", [128, DM])
    clend = p.sb("clendf", [1, 8])
    mrow = p.sb("mrowf", [1, 8])
    irow = p.sb("irowf", [1, 8])
    dec = p.sb("decf", [1, 8])
    Htile = p.sb("Htilef", [128, DM])
    HTt = p.sb("HTtf", [128, 8 * 128], BF16)
    clcol = p.sb("clcolf", [128, 1])
    ecol = p.sb("ecolf", [128, 1])
    ptile = p.sb("ptilef", [128, DM])
    x1t = p.sb("x1tf", [128, DM])
    sout = p.sb("soutf", [128, DM])
    brow_ = p.sb("browf", [1, DM])
    onesr = p.sb("onesrf", [1, 128])
    identb = p.sb("identbf", [128, 128], BF16)
    iot = p.sb("iotf", [128, 128])

    hb0 = p.sb("hb0f", [128, DM], BF16)
    hb1 = p.sb("hb1f", [128, DM], BF16)
    pw = p.ps("pwf", [128, 1024])
    ptA = p.ps("ptAf", [128, 1024], BF16)
    ptB = p.ps("ptBf", [128, 1024], BF16)

    ld = p.sem("ld")
    lg_ = p.sem("lg")
    lt0 = p.sem("lt0")
    dv = p.sem("dv")
    sc = p.sem("sc")
    gp = p.sem("gp")
    mm = p.sem("mm")
    so = p.sem("so")

    gpw = Guard()
    gtp = [Guard(), Guard()]
    gTa = Guard()
    gHt = Guard()
    gso = Guard()

    p.op("gpsimd", "memset", onesr[:, :], 1.0, inc=gp)
    p.op("gpsimd", "memset", Tac[:, :], 0.0, inc=gp)
    p.op("gpsimd", "iota", iot[:, :], pattern=[[1, 128]], base=0,
         channel_multiplier=-1, allow_small_or_imprecise_dtypes=True, inc=gp)
    p.op("vector", "tensor_scalar", identb[:, :], iot[:, :], 0.0, None, OP.is_equal,
         waits=[(gp, gp.v)], inc=dv)
    p.drain("vector")
    p.op("gpsimd", "dma_start", out=WO[:, :].rearrange("p (k o) -> p k o", k=8),
         in_=woT.rearrange("(k p) o -> p k o", p=128), inc=lg_, incn=16)
    p.op("sync", "dma_start", out=HL[:, :], in_=hlasts_i[:, :], inc=ld, incn=16)
    p.op("sync", "dma_start", out=clend[:, :], in_=clend_i, inc=ld, incn=16,
         allow_slow_non_contiguous=True)
    p.op("sync", "dma_start", out=mrow[:, :], in_=mrow_i[:, :], inc=ld, incn=16)
    p.op("sync", "dma_start", out=irow[:, :], in_=irow_i[:, :], inc=ld, incn=16)
    p.op("sync", "dma_start", out=brow_[:, :], in_=bout_i[:, :], inc=ld, incn=16)
    ldv = ld.v

    p.op("scalar", "activation", dec[:, :], clend[:, :], AF.Exp, waits=[(ld, ldv)],
         inc=sc)
    p.drain("scalar")
    p.op("vector", "tensor_tensor", dec[:, :], dec[:, :], mrow[:, :], OP.mult,
         waits=[(sc, sc.v)], inc=dv)
    p.drain("vector")
    p.op("vector", "tensor_tensor", dec[:, :], dec[:, :], irow[:, :], OP.add, inc=dv)
    p.drain("vector")
    wrow = p.sb("wrowf", [1, 8])
    sufp = p.sb("sufpf", [1, 8])
    onesr8 = p.sb("onesr8f", [1, 8])
    p.op("gpsimd", "memset", onesr8[:, :], 1.0, inc=gp)
    zrow8 = p.sb("zrow8f", [1, 8])
    p.op("gpsimd", "memset", zrow8[:, :], 0.0, inc=gp)
    rdec = ap_v(dec, 7, [[8, 1], [-1, 8]])
    rsuf = ap_v(sufp, 7, [[8, 1], [-1, 8]])
    p.op("vector", "tensor_tensor_scan", rsuf, rdec, zrow8[:, :], 1.0,
         OP.mult, OP.add, waits=[(dv, dv.v), (gp, gp.v)], inc=dv)
    p.drain("vector")
    p.op("vector", "tensor_copy", wrow[0:1, 0:7], sufp[0:1, 1:8], inc=dv)
    p.drain("vector")
    p.op("vector", "tensor_copy", wrow[0:1, 7:8], onesr8[0:1, 0:1], inc=dv)
    p.drain("vector")
    p.op("vector", "tensor_tensor", wrow[:, :], wrow[:, :], mrow[:, :], OP.mult,
         inc=dv)
    p.drain("vector")
    id8f = p.sb("id8ff", [8, 8])
    p.op("gpsimd", "iota", id8f[:, :], pattern=[[1, 8]], base=0,
         channel_multiplier=-1, allow_small_or_imprecise_dtypes=True, inc=gp)
    p.op("vector", "tensor_scalar", id8f[:, :], id8f[:, :], 0.0, None, OP.is_equal,
         waits=[(gp, gp.v)], inc=dv)
    p.drain("vector")
    wcol = p.sb("wcolf", [8, 1])
    p.op("tensor", "transpose", pw[0:8, 0:1], wrow[0:1, 0:8], id8f[0:1, 0:1],
         waits=gpw.need_w() + [(dv, dv.v)], inc=mm)
    gpw.did_w(mm)
    p.op("vector", "tensor_copy", wcol[:, :], pw[0:8, 0:1], waits=[(mm, mm.v)], inc=dv)
    p.drain("vector")
    gpw.did_r(dv)
    p.op("tensor", "matmul", pw[0:1, 0:DM - 512] if False else pw[0:1, 0:512],
         wcol[0:8, 0:1], HL[0:8, 0:512], start=True, stop=True,
         waits=gpw.need_w() + [(dv, dv.v), (ld, ldv)], inc=mm)
    p.op("tensor", "matmul", pw[0:1, 512:1024], wcol[0:8, 0:1], HL[0:8, 512:1024],
         start=True, stop=True, inc=mm)
    gpw.did_w(mm)
    p.op("vector", "tensor_copy", Tac[:, :], pw[0:1, 0:1024], waits=[(mm, mm.v)],
         inc=dv)
    p.drain("vector")
    gpw.did_r(dv)
    for i in range(2):
        p.op("tensor", "matmul", pw[:, i * 512:(i + 1) * 512], onesr[0:1, 0:128],
             Tac[0:1, i * 512:(i + 1) * 512], start=True, stop=True,
             waits=gpw.need_w() + [(dv, dv.v)] if i == 0 else (), inc=mm)
    gpw.did_w(mm)
    p.op("vector", "tensor_copy", hinb[:, :], pw[:, 0:1024], waits=[(mm, mm.v)], inc=dv)
    p.drain("vector")
    gpw.did_r(dv)
    hindone = dv.v

    for tb in range(NTB):
        t0 = tb * 128
        p.op("sync", "dma_start", out=Htile[:, :], in_=H_i[t0:t0 + 128, :],
             waits=[(so, so.v), (mm, mm.v)] if tb > 0 else [], inc=lt0, incn=16)
        p.op("sync", "dma_start", out=clcol[:, :],
             in_=ap_v_dram(clr_i, t0, [[1, 128], [1, 1]]),
             inc=lt0, incn=16)
        p.op("sync", "dma_start", out=ptile[:, :], in_=pkm_i[t0:t0 + 128, :],
             inc=lt0, incn=16)
        p.op("sync", "dma_start", out=x1t[:, :], in_=x1_i[t0:t0 + 128, :],
             inc=lt0, incn=16)
        ltv = lt0.v
        p.op("scalar", "activation", ecol[:, :], clcol[:, :], AF.Exp,
             waits=[(lt0, ltv)], inc=sc)
        p.drain("scalar")
        p.op("vector", "scalar_tensor_tensor", Htile[:, :], hinb[:, :],
             ecol[0:128, 0:1], Htile[:, :], OP.mult, OP.add,
             waits=[(sc, sc.v), (dv, hindone)], inc=dv)
        p.drain("vector")
        p.op("scalar", "activation", ptile[:, :], ptile[:, :], AF.Silu, inc=sc)
        p.drain("scalar")
        hb = hb0 if tb % 2 == 0 else hb1
        p.op("vector", "tensor_copy", hb[:, :], Htile[:, :], inc=dv)
        p.drain("vector")
        for kb in range(8):
            dstp = ptA if kb % 2 == 0 else ptB
            gtj = gtp[kb % 2]
            p.op("tensor", "transpose", dstp[:, 0:128],
                 hb[:, kb * 128:(kb + 1) * 128], identb[:, :],
                 waits=gtj.need_w() + [(dv, dv.v)], inc=mm)
            gtj.did_w(mm)
            p.op("vector", "tensor_copy", HTt[:, kb * 128:(kb + 1) * 128],
                 dstp[:, 0:128], waits=[(mm, mm.v)] + (gHt.need_w() if kb == 0 else []),
                 inc=dv)
            p.drain("vector")
            gtj.did_r(dv)
        gHt.did_w(dv)
        for nb in range(2):
            for kb in range(8):
                p.op("tensor", "matmul", pw[:, nb * 512:(nb + 1) * 512],
                     HTt[:, kb * 128:(kb + 1) * 128],
                     ap_v(WO, kb * DM + nb * 512, [[8 * DM, 128], [1, 512]]),
                     start=(kb == 0), stop=False,
                     waits=(gpw.need_w() + [(dv, dv.v), (lg_, lg_.v)]) if (nb == 0 and kb == 0) else (),
                     inc=mm)
            p.op("tensor", "matmul", pw[:, nb * 512:(nb + 1) * 512],
                 onesr[0:1, 0:128], brow_[0:1, nb * 512:(nb + 1) * 512],
                 start=False, stop=True, waits=[(ld, ldv)], inc=mm)
        gpw.did_w(mm)
        gHt.did_r(mm)
        p.op("vector", "tensor_tensor", sout[:, :], pw[:, 0:1024], ptile[:, :],
             OP.mult, waits=[(mm, mm.v), (sc, sc.v)], inc=dv)
        p.drain("vector")
        gpw.did_r(dv)
        p.op("vector", "tensor_tensor", sout[:, :], sout[:, :], x1t[:, :], OP.add,
             inc=dv)
        p.drain("vector")
        p.op("sync", "dma_start", out=out_o[t0:t0 + 128, :], in_=sout[:, :],
             waits=[(dv, dv.v)], inc=so, incn=16)
    for s2 in (so, ld, lg_, lt0):
        p.wait("sync", s2, s2.v)



def build_launchA(T=2048):
    RH = 4096
    N2 = 2 * T
    p = P("launchA")
    io_pre = {
        "x": p.dram("x", [T, DM]),
        "xh": p.dram("xh", [8, DM]),
        "nw": p.dram("nw", [1, DM]),
        "xn": p.dram("xn", [T, DM], kind="ExternalOutput"),
        "xnT": p.dram("xnT", [DM, T], BF16, kind="Internal"),
        "x2Tf": p.dram("x2Tf", [DM, T], BF16, kind="Internal"),
        "x2Tb": p.dram("x2Tb", [DM, T], BF16, kind="Internal"),
        "x2hTf": p.dram("x2hTf", [DM, 4], BF16, kind="Internal"),
        "x2hTb": p.dram("x2hTb", [DM, 4], BF16, kind="Internal"),
    }
    io_m = {}
    for d in ("f", "b"):
        io_m[d] = {
            "x2T": io_pre["x2T" + d],
            "x2hT": io_pre["x2hT" + d],
            "wInT": p.dram("wInT_" + d, [DM, DIP]),
            "cw": p.dram("cw_" + d, [CONVD, 4]),
            "cbr": p.dram("cbr_" + d, [18, 128]),
            "dtb": p.dram("dtb_" + d, [NH, 1]),
            "alog": p.dram("alog_" + d, [NH, 1]),
            "dexpr": p.dram("dexpr_" + d, [1, DI]),
            "yloc": p.dram("yloc_" + d, [T, DI], kind="ExternalOutput"),
            "ztm": p.dram("ztm_" + d, [T, DI], BF16, kind="ExternalOutput"),
            "cfm": p.dram("cfm_" + d, [DST, T], BF16, kind="ExternalOutput"),
            "agT": p.dram("agT_" + d, [T, NH], kind="ExternalOutput"),
            "sout": p.dram("sout_" + d, [DST, DI], BF16, kind="ExternalOutput"),
        }
    io_r = {
        "xnT": io_pre["xnT"],
        "xn": io_pre["xn"],
        "rw1": p.dram("rw1", [DM, RH]),
        "rb1c": p.dram("rb1c", [128, 32]),
        "rw2": p.dram("rw2", [RH, 8]),
        "rb2": p.dram("rb2", [1, 8]),
        "send": p.dram("send", [N2, DM], BF16, kind="ExternalOutput"),
        "counts": p.dram("counts", [8, 1], kind="ExternalOutput"),
        "ids": p.dram("ids", [T, 2], kind="ExternalOutput"),
        "lpos": p.dram("lpos", [T, 2], kind="ExternalOutput"),
        "probs": p.dram("probs", [T, 2], kind="ExternalOutput"),
    }
    p.push_scope()
    emit_l0_pre(p, io_pre, T)
    p.pop_scope()
    p.barrier()
    for d in ("f", "b"):
        p.push_scope()
        emit_mamba_local(p, io_m[d], T, tag=d)
        p.pop_scope()
        p.barrier()
    p.push_scope()
    emit_moe_route(p, io_r, T)
    p.pop_scope()
    p.barrier()
    return p.build()


def build_mega(T=2048):
    RH = 4096
    N2 = 2 * T
    p = P("mega")

    io_pre = {
        "x": p.dram("x", [T, DM]),
        "xh": p.dram("xh", [8, DM]),
        "nw": p.dram("nw", [1, DM]),
        "xn": p.dram("xn", [T, DM], kind="Internal"),
        "xnT": p.dram("xnT", [DM, T], BF16, kind="Internal"),
        "x2Tf": p.dram("x2Tf", [DM, T], BF16, kind="Internal"),
        "x2Tb": p.dram("x2Tb", [DM, T], BF16, kind="Internal"),
        "x2hTf": p.dram("x2hTf", [DM, 4], BF16, kind="Internal"),
        "x2hTb": p.dram("x2hTb", [DM, 4], BF16, kind="Internal"),
    }
    ids = p.dram("ids", [T, 2], kind="Internal")
    lpos = p.dram("lpos", [T, 2], kind="Internal")
    probs = p.dram("probs", [T, 2], kind="Internal")
    send = p.dram("send", [N2, DM], BF16, kind="Internal")
    counts = p.dram("counts", [8, 1], kind="Internal")
    io_r = {
        "xnT": io_pre["xnT"], "xn": io_pre["xn"],
        "rw1": p.dram("rw1", [DM, RH]),
        "rb1c": p.dram("rb1c", [128, 32]),
        "rw2": p.dram("rw2", [RH, 8]),
        "rb2": p.dram("rb2", [1, 8]),
        "send": send, "counts": counts,
        "ids": ids, "lpos": lpos, "probs": probs,
    }
    io_m = {}
    for d in ("f", "b"):
        io_m[d] = {
            "x2T": io_pre["x2T" + d],
            "x2hT": io_pre["x2hT" + d],
            "wInT": p.dram("wInT_" + d, [DM, DIP]),
            "cw": p.dram("cw_" + d, [CONVD, 4]),
            "cbr": p.dram("cbr_" + d, [18, 128]),
            "dtb": p.dram("dtb_" + d, [NH, 1]),
            "alog": p.dram("alog_" + d, [NH, 1]),
            "dexpr": p.dram("dexpr_" + d, [1, DI]),
            "yloc": p.dram("y" + d, [T, DI], kind="Internal"),
            "ztm": p.dram("z" + d, [T, DI], BF16, kind="Internal"),
            "cfm": p.dram("cf" + d, [DST, T], BF16, kind="Internal"),
            "agT": p.dram("ag" + d, [T, NH], kind="Internal"),
            "sout": p.dram("soutloc" + d, [DST, DI], BF16, kind="Internal"),
        }
    sendall = p.dram("sendall", [8 * N2, DM], BF16, kind="Internal")
    countsA = p.dram("countsA", [8, 8], kind="Internal")
    ae = {d: p.dram("ae" + d, [1, 8 * NH], kind="Internal") for d in ("f", "b")}
    so = {d: p.dram("so" + d, [8 * DST, DI], BF16, kind="Internal")
          for d in ("f", "b")}
    io_f = {"projT": p.dram("projT", [2 * DM, DM]),
            "pbias": p.dram("pbias", [1, DM]),
            "xn": io_pre["xn"],
            "s": p.dram("s", [T, DM], kind="Internal")}
    for d in ("f", "b"):
        io_f["y" + d] = io_m[d]["yloc"]
        io_f["z" + d] = io_m[d]["ztm"]
        io_f["cf" + d] = io_m[d]["cfm"]
        io_f["ag" + d] = io_m[d]["agT"]
        io_f["ae" + d] = ae[d]
        io_f["so" + d] = so[d]
        io_f["mrow" + d] = p.dram("mrow" + d, [1, 8])
        io_f["irow" + d] = p.dram("irow" + d, [1, 8])
        io_f["mtil" + d] = p.dram("mtil" + d, [128, 8])
        io_f["wo" + d] = p.dram("wo" + d, [DI, DM])
        io_f["nw" + d] = p.dram("nw" + d, [1, DI])
    ccol = p.dram("ccol", [8, 1], kind="Internal")
    io_e = {
        "sendall": sendall, "countsA": countsA,
        "onehotE": p.dram("onehotE", [8, 8]),
        "ccol": ccol,
        "ew1": p.dram("ew1", [DM, RH]),
        "eb1c": p.dram("eb1c", [128, 32]),
        "ew2": p.dram("ew2", [RH, DM]),
        "eb2": p.dram("eb2", [1, DM]),
        "eo": p.dram("eo", [N2, DM], BF16, kind="Internal"),
    }
    eoall = p.dram("eoall", [8 * N2, DM], BF16, kind="Internal")
    io_c = {
        "eoall": eoall, "countsA": countsA,
        "premask": p.dram("premask", [8, 1]),
        "idsf": ids.rearrange("t k -> (t k)").unsqueeze(0),
        "lposf": lpos.rearrange("t k -> (t k)").unsqueeze(0),
        "probsf": probs.rearrange("t k -> (t k)").unsqueeze(0),
        "s": io_f["s"], "x": io_pre["x"],
        "x1": p.dram("x1", [T, DM], kind="Internal"),
        "xn1": p.dram("xn1", [T, DM], kind="Internal"),
        "xn1T": p.dram("xn1T", [DM, T], BF16, kind="Internal"),
    }
    io_l = {
        "xn1T": io_c["xn1T"], "xn1": io_c["xn1"],
        "wg": p.dram("wg", [DM, 1]),
        "bg": p.dram("bg", [1, 1]),
        "H": p.dram("H", [T, DM], kind="Internal"),
        "clr": p.dram("clr", [1, T], kind="Internal"),
        "hlast": p.dram("hlast", [1, DM], kind="Internal"),
    }
    io_p = {
        "xn1T": io_c["xn1T"], "xn1": io_c["xn1"],
        "wqT": p.dram("wqT", [DM, 8192]),
        "keysr": p.dram("keysr", [2 * 8 * 512, 128]),
        "winT": p.dram("winT", [DM, DI]),
        "woutT": p.dram("woutT", [DI, DM]),
        "in_a": p.dram("in_a", [16384, DM]),
        "in_b": p.dram("in_b", [16384, DI]),
        "out_a": p.dram("out_a", [16384, DI]),
        "out_b": p.dram("out_b", [16384, DM]),
        "pkm": p.dram("pkm", [T, DM], kind="Internal"),
    }
    clrall = p.dram("clrall", [8, T], kind="Internal")
    hlasts = p.dram("hlasts", [8, DM], kind="Internal")
    io_fin = {
        "H": io_l["H"], "clr": io_l["clr"], "clrall": clrall,
        "hlasts": hlasts,
        "mrow": p.dram("mrow1", [1, 8]),
        "irow": p.dram("irow1", [1, 8]),
        "woT": p.dram("woT", [DM, DM]),
        "bout": p.dram("bout", [1, DM]),
        "pkm": io_p["pkm"], "x1": io_c["x1"],
        "out": p.dram("out", [T, DM], kind="ExternalOutput"),
    }

    p.push_scope()
    emit_l0_pre(p, io_pre, T)
    p.pop_scope()
    p.barrier()
    p.push_scope()
    emit_moe_route(p, io_r, T)
    p.pop_scope()
    p.barrier()
    emit_ag(p, send, sendall)
    emit_ag(p, counts, countsA)
    for d, k in (("f", 3), ("b", 5)):
        p.push_scope()
        emit_mamba_local(p, io_m[d], T, tag=d)
        p.pop_scope()
        p.barrier()
        emit_ag(p, io_m[d]["agT"][T - 1:T, :], ae[d])
        emit_ag(p, io_m[d]["sout"], so[d])
    wait_cc(p, 6)
    p.push_scope()
    cA8 = p.sb("ccglue_cA", [8, 8])
    oh8 = p.sb("ccglue_oh", [8, 8])
    cc1 = p.sb("ccglue_c", [8, 1])
    gld = p.sem("ld")
    gdv = p.sem("dv")
    gso = p.sem("so")
    p.op("sync", "dma_start", out=cA8[:, :], in_=countsA[:, :], inc=gld, incn=16)
    p.op("sync", "dma_start", out=oh8[:, :], in_=io_e["onehotE"][:, :],
         inc=gld, incn=16)
    p.op("vector", "tensor_tensor", cA8[:, :], cA8[:, :], oh8[:, :], OP.mult,
         waits=[(gld, gld.v)], inc=gdv)
    p.drain("vector")
    p.op("vector", "tensor_reduce", cc1[:, :], cA8[:, :], AX.X, OP.add, inc=gdv)
    p.drain("vector")
    p.op("sync", "dma_start", out=ccol[:, :], in_=cc1[:, :],
         waits=[(gdv, gdv.v)], inc=gso, incn=16)
    p.wait("sync", gso, gso.v)
    p.pop_scope()
    p.barrier()
    p.push_scope()
    emit_mamba_fin(p, io_f, T)
    p.pop_scope()
    p.barrier()
    p.push_scope()
    emit_moe_expert(p, io_e)
    p.pop_scope()
    p.barrier()
    emit_ag(p, io_e["eo"], eoall)
    wait_cc(p, 7)
    p.push_scope()
    emit_moe_combine(p, io_c, T)
    p.pop_scope()
    p.barrier()
    p.push_scope()
    emit_l1_local(p, io_l, T)
    p.pop_scope()
    p.barrier()
    emit_ag(p, io_l["clr"], clrall)
    emit_ag(p, io_l["hlast"], hlasts)
    p.push_scope()
    emit_pkm2(p, io_p, T)
    p.pop_scope()
    p.barrier()
    wait_cc(p, 9)
    p.push_scope()
    emit_l1_fin(p, io_fin, T)
    p.pop_scope()
    p.barrier()
    return p.build()



def _split_tokens(a, halo=False):
    L = a.shape[0]
    T = L // NCORES
    return [np.ascontiguousarray(a[i * T:(i + 1) * T]) for i in range(NCORES)]


def _halo_rows(x):
    L = x.shape[0]
    T = L // NCORES
    out = []
    for i in range(NCORES):
        h = np.zeros((8, x.shape[1]), np.float32)
        s = i * T
        e = (i + 1) * T
        if s >= 3:
            h[0:3] = x[s - 3:s]
        elif s > 0:
            h[3 - s:3] = x[0:s]
        nb = min(3, L - e)
        if nb > 0:
            h[4:4 + nb] = x[e:e + nb][::-1][-nb:] if False else x[e:e + nb][::-1]
        out.append(h)
    return out


def _runs(nc, in_maps):
    return run_bass_kernel_spmd(nc, in_maps, core_ids=list(range(NCORES))).results


def _t(a):
    return np.ascontiguousarray(np.asarray(a, np.float32).T)


def _c(a):
    return np.ascontiguousarray(np.asarray(a, np.float32))


def kernel(**inputs):
    L = 16384
    T = L // NCORES
    f32 = np.float32
    x = _c(inputs["L0_attn_x"][0])
    xs_ = _split_tokens(x)
    halos = _halo_rows(x)
    nw = _c(inputs["L0_attn_norm_weight"]).reshape(1, DM)

    mprm = {
        "f": dict(W_in=inputs["L0_attn_fwd_W_in"], conv_w=inputs["L0_attn_fwd_conv_w"],
                  conv_b=inputs["L0_attn_fwd_conv_b"], dt_bias=inputs["L0_attn_fwd_dt_bias"],
                  A_log=inputs["L0_attn_fwd_A_log"], D=inputs["L0_attn_fwd_D"],
                  norm_w=inputs["L0_attn_fwd_norm_w"], W_out=inputs["L0_attn_fwd_W_out"]),
        "b": dict(W_in=inputs["L0_attn_bwd_W_in"], conv_w=inputs["L0_attn_bwd_conv_w"],
                  conv_b=inputs["L0_attn_bwd_conv_b"], dt_bias=inputs["L0_attn_bwd_dt_bias"],
                  A_log=inputs["L0_attn_bwd_A_log"], D=inputs["L0_attn_bwd_D"],
                  norm_w=inputs["L0_attn_bwd_norm_w"], W_out=inputs["L0_attn_bwd_W_out"]),
    }
    shared = {
        "nw": nw,
        "rw1": _c(inputs["L0_ffn_rw1"]),
        "rb1c": np.ascontiguousarray(_c(inputs["L0_ffn_rb1"]).reshape(32, 128).T),
        "rw2": _c(inputs["L0_ffn_rw2"]),
        "rb2": _c(inputs["L0_ffn_rb2"]).reshape(1, 8),
        "projT": _t(inputs["L0_attn_proj_W"]),
        "pbias": _c(inputs["L0_attn_proj_b"]).reshape(1, DM),
        "wg": _t(inputs["L1_attn_Wg"]),
        "bg": _c(inputs["L1_attn_bg"]).reshape(1, 1),
        "wqT": _t(inputs["L1_ffn_Wq"]),
        "keysr": np.ascontiguousarray(
            _c(inputs["L1_ffn_keys"]).transpose(2, 0, 3, 1).reshape(2 * 8 * 512, 128)),
        "winT": _t(inputs["L1_ffn_W_in"]),
        "woutT": _t(inputs["L1_ffn_W_out"]),
        "in_a": _c(inputs["L1_ffn_in_a"]),
        "in_b": _c(inputs["L1_ffn_in_b"]),
        "out_a": _c(inputs["L1_ffn_out_a"]),
        "out_b": _c(inputs["L1_ffn_out_b"]),
        "woT": _t(inputs["L1_attn_Wout"]),
        "bout": _c(inputs["L1_attn_bout"]).reshape(1, DM),
    }
    for d in ("f", "b"):
        pp = mprm[d]
        shared["wInT_" + d] = _t(pp["W_in"])
        shared["cw_" + d] = _c(pp["conv_w"])
        shared["cbr_" + d] = _c(pp["conv_b"]).reshape(18, 128)
        shared["dtb_" + d] = _c(pp["dt_bias"]).reshape(NH, 1)
        shared["alog_" + d] = _c(pp["A_log"]).reshape(NH, 1)
        shared["dexpr_" + d] = np.ascontiguousarray(
            np.repeat(_c(pp["D"]), HD).reshape(1, DI))
        shared["wo" + d] = _t(pp["W_out"])
        shared["nw" + d] = _c(pp["norm_w"]).reshape(1, DI)

    nc = build_mega(T)
    ims = []
    for i in range(NCORES):
        im = dict(shared)
        im["x"] = xs_[i]
        im["xh"] = halos[i]
        for d in ("f", "b"):
            mask = (np.arange(8) < i) if d == "f" else (np.arange(8) > i)
            mask = mask.astype(f32)
            im["mrow" + d] = mask.reshape(1, 8)
            im["irow" + d] = (1 - mask).reshape(1, 8)
            im["mtil" + d] = np.tile(mask, (128, 1))
        oh = np.zeros((8, 8), f32)
        oh[:, i] = 1.0
        im["onehotE"] = oh
        im["ew1"] = _c(inputs["L0_ffn_ew1"][i])
        im["eb1c"] = np.ascontiguousarray(_c(inputs["L0_ffn_eb1"][i]).reshape(32, 128).T)
        im["ew2"] = _c(inputs["L0_ffn_ew2"][i])
        im["eb2"] = _c(inputs["L0_ffn_eb2"][i]).reshape(1, DM)
        im["premask"] = (np.arange(8) < i).astype(f32).reshape(8, 1)
        m = (np.arange(8) < i).astype(f32)
        im["mrow1"] = m.reshape(1, 8)
        im["irow1"] = (1 - m).reshape(1, 8)
        ims.append(im)
    res = _runs(nc, ims)
    out = np.concatenate([r["out"] for r in res], 0)
    return out.reshape(1, L, DM)


# version 12
# speedup vs baseline: 1.0414x; 1.0414x over previous
import math
from contextlib import ExitStack

import numpy as np
import ml_dtypes

import concourse.bass as bass
import concourse.mybir as mybir
from concourse.bass_utils import run_bass_kernel_spmd
from concourse.alu_op_type import AluOpType as OP

F32 = mybir.dt.float32
BF16 = mybir.dt.bfloat16
U32 = mybir.dt.uint32
AF = mybir.ActivationFunctionType
AX = mybir.AxisListType

NCORES = 8
DM = 1024
DI = 2048
DST = 128
NH = 32
HD = 64
CONVD = 2304
DIP = 4384
CH = 128
EPS = 1e-5
NEG = -1e30

BF = np.dtype(ml_dtypes.bfloat16)



class Sem:
    def __init__(self, handle):
        self.h = handle
        self.v = 0


class P:

    ENGS = ("sync", "vector", "scalar", "tensor", "gpsimd")

    def __init__(self, name="prog"):
        self.nc = bass.Bass("TRN2", target_bir_lowering=False)
        self.es = ExitStack()
        self.ops = []
        self.name = name
        self._uid = 0
        self.sems = {}
        self.scopes = []

    def uid(self):
        self._uid += 1
        return self._uid

    def dram(self, name, shape, dtype=F32, kind="ExternalInput"):
        return self.nc.dram_tensor(name, list(shape), dtype, kind=kind).ap()

    def _alloc_es(self):
        return self.scopes[-1] if self.scopes else self.es

    def push_scope(self):
        self.scopes.append(ExitStack())

    def pop_scope(self):
        self.scopes.pop().close()

    def sb(self, name, shape, dtype=F32):
        return self._alloc_es().enter_context(
            self.nc.sbuf_tensor(f"{name}_{self.uid()}" if self.scopes else name,
                                list(shape), dtype))

    def ps(self, name, shape, dtype=F32):
        return self._alloc_es().enter_context(
            self.nc.psum_tensor(f"{name}_{self.uid()}" if self.scopes else name,
                                list(shape), dtype))

    def sem(self, name=None):
        if name is not None and name in self.sems:
            return self.sems[name]
        s = Sem(self.es.enter_context(self.nc.semaphore(name or f"s{self.uid()}")))
        if name is not None:
            self.sems[name] = s
        return s

    def barrier(self):
        pairs = [(s.h, s.v) for n, s in self.sems.items()
                 if s.v > 0 and n != "cc"]

        def mk(e):
            for h, v in pairs:
                e.wait_ge(h, v)
        for en in self.ENGS:
            self.emit(en, mk)

    def emit(self, eng, fn):
        self.ops.append((eng, fn))

    def op(self, eng, method, *args, inc=None, incn=1, waits=(), **kw):
        if inc is not None:
            inc.v += incn
        wl = [(s.h, v) for (s, v) in waits]
        incs = None if inc is None else (inc.h, incn)

        def fn(e):
            for h, v in wl:
                e.wait_ge(h, v)
            try:
                ins = getattr(e, method)(*args, **kw)
            except Exception:
                import sys
                def nm(v):
                    t = getattr(v, "tensor", None)
                    return getattr(t, "name", None) or str(type(v).__name__)
                print(f"P.op FAILED: {method} kw={[(k, nm(v)) for k, v in kw.items()]}",
                      file=sys.stderr)
                raise
            if incs is not None:
                ins.then_inc(*incs)
        self.emit(eng, fn)
        return None if inc is None else inc.v

    def wait(self, eng, sem, val=None):
        v = sem.v if val is None else val
        self.emit(eng, lambda e: e.wait_ge(sem.h, v))

    def drain(self, eng):
        self.emit(eng, lambda e: e.drain())

    def build(self):
        block = self.es.enter_context(self.nc.Block())
        ops = self.ops

        def mk(engname):
            def _f(e):
                for en, fn in ops:
                    if en == engname:
                        fn(e)
            return _f
        for en in self.ENGS:
            getattr(block, en)(mk(en))
        self.es.close()
        return self.nc


def ap_v(t, offset, ap):
    return bass.AP(tensor=t, offset=offset, ap=[list(x) for x in ap])


RG8 = [list(range(NCORES))]


def emit_ag(p, in_ap, out_ap):
    cc = p.sem("cc")
    p.op("gpsimd", "collective_compute", "AllGather", OP.bypass, RG8,
         ins=[in_ap], outs=[out_ap], inc=cc)
    return cc.v


def wait_cc(p, val):
    cc = p.sem("cc")
    for en in P.ENGS:
        p.wait(en, cc, val)



def emit_row_bcast(p, row_ap, n, psum, out_sb, ones_r, mm, dv, waits=()):
    nb = (n + 511) // 512
    for i in range(nb):
        c0, c1 = i * 512, min(n, (i + 1) * 512)
        w = list(waits) if i == 0 else []
        if i > 0:
            w.append((dv, dv.v))
        p.op("tensor", "matmul", psum[:, 0:c1 - c0], ones_r[0:1, 0:128],
             row_ap[0:1, c0:c1], start=True, stop=True, waits=w, inc=mm)
        p.op("vector", "tensor_copy", out_sb[:, c0:c1], psum[:, 0:c1 - c0],
             waits=[(mm, mm.v)], inc=dv)
    p.drain("vector")


def emit_ln_tile(p, xt, nP, n, outt, sq_scr, c1, c2, dv, sc, epscol, w_bcast=None,
                 out2=None, c3=None, subtract_mean=True):
    if subtract_mean:
        p.op("vector", "tensor_reduce", c1[0:nP, :], xt[0:nP, 0:n], AX.X, OP.add, inc=dv)
        p.drain("vector")
        p.op("vector", "tensor_scalar", c1[0:nP, :], c1[0:nP, :], 1.0 / n, None, OP.mult, inc=dv)
        p.drain("vector")
        p.op("vector", "tensor_scalar", xt[0:nP, 0:n], xt[0:nP, 0:n], c1[0:nP, 0:1], None,
             OP.subtract, inc=dv)
    p.op("scalar", "activation", sq_scr[0:nP, 0:n], xt[0:nP, 0:n], AF.Square,
         accum_out=c2[0:nP, :], waits=[(dv, dv.v)], inc=sc)
    p.drain("scalar")
    p.op("scalar", "activation", c2[0:nP, :], c2[0:nP, :], AF.Ln, scale=1.0 / n, bias=epscol[0:nP, 0:1], inc=sc)
    p.drain("scalar")
    p.op("scalar", "activation", c2[0:nP, :], c2[0:nP, :], AF.Exp, scale=-0.5, inc=sc)
    p.op("vector", "tensor_scalar", outt[0:nP, 0:n], xt[0:nP, 0:n], c2[0:nP, 0:1], None,
         OP.mult, waits=[(sc, sc.v)], inc=dv)
    if out2 is not None:
        p.op("scalar", "activation", sq_scr[0:nP, 0:n], outt[0:nP, 0:n], AF.Square,
             accum_out=c3[0:nP, :], waits=[(dv, dv.v)], inc=sc)
        p.drain("scalar")
        p.op("scalar", "activation", c3[0:nP, :], c3[0:nP, :], AF.Ln, scale=1.0 / n, bias=epscol[0:nP, 0:1], inc=sc)
        p.drain("scalar")
        p.op("scalar", "activation", c3[0:nP, :], c3[0:nP, :], AF.Exp, scale=-0.5, inc=sc)
        p.op("vector", "scalar_tensor_tensor", out2[0:nP, 0:n], outt[0:nP, 0:n],
             c3[0:nP, 0:1], w_bcast[0:nP, 0:n], OP.mult, OP.mult,
             waits=[(sc, sc.v)], inc=dv)



def emit_l0_pre(p, io, T=2048, parts="all"):
    NT = T // 128
    x, xh, nw = io["x"], io["xh"], io["nw"]
    xn_o, xnT_o = io["xn"], io["xnT"]
    x2Tf_o, x2Tb_o = io["x2Tf"], io["x2Tb"]
    x2hTf_o, x2hTb_o = io["x2hTf"], io["x2hTb"]

    xbuf = p.sb("xbuf", [128, NT * DM])
    x2buf = p.sb("x2buf", [128, NT * DM])
    hbuf = p.sb("hbuf", [8, 2 * DM])
    wrow = p.sb("wrow", [1, DM])
    wb = p.sb("wb", [128, DM])
    c1 = p.sb("c1", [128, 1])
    c2 = p.sb("c2", [128, 1])
    c3 = p.sb("c3", [128, 1])
    epscol = p.sb("epscol", [128, 1])
    ones_r = p.sb("ones_r", [1, 128])
    identf = p.sb("identf", [128, 128])
    iotf = p.sb("iotf", [128, 128])
    stageN = p.sb("stageN", [128, T], BF16)
    stageF = p.sb("stageF", [128, T], BF16)
    stageB = p.sb("stageB", [128, T], BF16)
    hstg = p.sb("hstg", [128, 8], BF16)
    pw = p.ps("pw", [128, 1024])

    ld = p.sem("ld")
    dv = p.sem("dv")
    gp = p.sem("gp")
    sc = p.sem("sc")
    mm = p.sem("mm")
    st = p.sem("st")
    stN = p.sem("stN")
    stF = p.sem("stF")
    stB = p.sem("stB")
    stH = p.sem("stH")

    p.op("sync", "dma_start", out=xbuf[:, :].rearrange("p (t d) -> p t d", d=DM),
         in_=x.rearrange("(t p) d -> p t d", p=128), inc=ld, incn=16)
    p.op("sync", "dma_start", out=hbuf[:, 0:DM], in_=xh[:, :], inc=ld, incn=16)
    p.op("sync", "dma_start", out=wrow[:, :], in_=nw[:, :], inc=ld, incn=16)
    p.op("gpsimd", "memset", ones_r[:, :], 1.0, inc=gp)
    p.op("gpsimd", "memset", epscol[:, :], EPS, inc=gp)
    p.op("gpsimd", "iota", iotf[:, :], pattern=[[1, 128]], base=0,
         channel_multiplier=-1, allow_small_or_imprecise_dtypes=True, inc=gp)
    p.op("vector", "tensor_scalar", identf[:, :], iotf[:, :], 0.0, None, OP.is_equal,
         waits=[(gp, gp.v)], inc=dv)
    p.drain("vector")
    p.wait("scalar", gp, gp.v)
    emit_row_bcast(p, wrow, DM, pw, wb, ones_r, mm, dv,
                   waits=[(gp, gp.v), (ld, ld.v)])

    for t in range(NT):
        xt = ap_v(xbuf, t * DM, [[NT * DM, 128], [1, DM]])
        x2t = ap_v(x2buf, t * DM, [[NT * DM, 128], [1, DM]])
        emit_ln_tile(p, xt, 128, DM, xt, x2t, c1, c2, dv, sc, epscol,
                     w_bcast=wb, out2=x2t, c3=c3)
        p.op("sync", "dma_start",
             out=xn_o.rearrange("(t p) d -> p t d", p=128)[:, t, :],
             in_=xt, waits=[(dv, dv.v)], inc=st, incn=16)
    ht = ap_v(hbuf, 0, [[2 * DM, 8], [1, DM]])
    h2 = ap_v(hbuf, DM, [[2 * DM, 8], [1, DM]])
    emit_ln_tile(p, ht, 8, DM, ht, h2, c1, c2, dv, sc, epscol, w_bcast=wb, out2=h2, c3=c3)
    ln_done_dv = dv.v
    ln_done_sc = sc.v

    gt = [Guard(), Guard()]
    gsN = Guard()
    gsF = Guard()
    gsB = Guard()
    p.wait("tensor", dv, ln_done_dv)
    p.wait("tensor", sc, ln_done_sc)
    for k in range(0 if parts == "ln" else 8):
        for t in range(NT):
            ri = t % 2
            reg = ri * 512
            p.op("tensor", "transpose", pw[:, reg:reg + 128],
                 ap_v(x2buf, t * DM + k * 128, [[NT * DM, 128], [1, 128]]),
                 identf[:, :], waits=gt[ri].need_w(), inc=mm)
            gt[ri].did_w(mm)
            p.op("vector", "tensor_copy", stageF[:, t * 128:(t + 1) * 128],
                 pw[:, reg:reg + 128],
                 waits=[(mm, mm.v)] + (gsF.need_w() if t == 0 else []), inc=dv)
            p.drain("vector")
            if parts not in ("tf",):
                p.op("vector", "tensor_copy",
                     ap_v(stageB, (NT - 1 - t) * 128 + 127, [[T, 128], [-1, 128]]),
                     pw[:, reg:reg + 128],
                     waits=(gsB.need_w() if t == 0 else []), inc=dv)
                p.drain("vector")
            gt[ri].did_r(dv)
            if parts not in ("tf", "tfb"):
                p.op("tensor", "transpose", pw[:, reg:reg + 128],
                     ap_v(xbuf, t * DM + k * 128, [[NT * DM, 128], [1, 128]]),
                     identf[:, :], waits=gt[ri].need_w() + [(dv, dv.v)], inc=mm)
                gt[ri].did_w(mm)
                p.op("vector", "tensor_copy", stageN[:, t * 128:(t + 1) * 128],
                     pw[:, reg:reg + 128],
                     waits=[(mm, mm.v)] + (gsN.need_w() if t == 0 else []), inc=dv)
                p.drain("vector")
                gt[ri].did_r(dv)
        gsF.did_w(dv)
        gsB.did_w(dv)
        gsN.did_w(dv)
        if parts not in ("tf", "tfb"):
            p.op("sync", "dma_start", out=xnT_o[k * 128:(k + 1) * 128, :],
                 in_=stageN[:, :], waits=[(dv, dv.v)], inc=stN, incn=16)
            gsN.did_r(stN)
        p.op("sync", "dma_start", out=x2Tf_o[k * 128:(k + 1) * 128, :],
             in_=stageF[:, :], waits=[(dv, dv.v)], inc=stF, incn=16)
        gsF.did_r(stF)
        if parts not in ("tf",):
            p.op("sync", "dma_start", out=x2Tb_o[k * 128:(k + 1) * 128, :],
                 in_=stageB[:, :], inc=stB, incn=16)
            gsB.did_r(stB)
        if parts != "all":
            continue
        p.op("gpsimd", "memset", hstg[:, :], 0.0,
             waits=[(stH, stH.v)] if k > 0 else [], inc=gp)
        p.op("tensor", "transpose", pw[:, 0:8],
             ap_v(hbuf, DM + k * 128, [[2 * DM, 8], [1, 128]]),
             identf[0:8, 0:8], waits=gt[0].need_w() + [(gp, gp.v)], inc=mm)
        gt[0].did_w(mm)
        p.op("vector", "tensor_copy", hstg[:, 0:8], pw[:, 0:8],
             waits=[(mm, mm.v)], inc=dv)
        p.drain("vector")
        gt[0].did_r(dv)
        p.op("sync", "dma_start", out=x2hTf_o[k * 128:(k + 1) * 128, 0:3],
             in_=hstg[:, 0:3], waits=[(dv, dv.v)], inc=stH, incn=16)
        p.op("sync", "dma_start", out=x2hTf_o[k * 128:(k + 1) * 128, 3:4],
             in_=hstg[:, 3:4], allow_slow_non_contiguous=True, inc=stH, incn=16)
        p.op("sync", "dma_start", out=x2hTb_o[k * 128:(k + 1) * 128, 0:3],
             in_=hstg[:, 4:7], inc=stH, incn=16)
        p.op("sync", "dma_start", out=x2hTb_o[k * 128:(k + 1) * 128, 3:4],
             in_=hstg[:, 3:4], allow_slow_non_contiguous=True, inc=stH, incn=16)
    for s2 in (st, stN, stF, stB, stH, ld):
        p.wait("sync", s2, s2.v)


def build_l0_pre(T=2048):
    p = P("l0_pre")
    io = {
        "x": p.dram("x", [T, DM]),
        "xh": p.dram("xh", [8, DM]),
        "nw": p.dram("nw", [1, DM]),
        "xn": p.dram("xn", [T, DM], kind="ExternalOutput"),
        "xnT": p.dram("xnT", [DM, T], BF16, kind="ExternalOutput"),
        "x2Tf": p.dram("x2Tf", [DM, T], BF16, kind="ExternalOutput"),
        "x2Tb": p.dram("x2Tb", [DM, T], BF16, kind="ExternalOutput"),
        "x2hTf": p.dram("x2hTf", [DM, 4], BF16, kind="ExternalOutput"),
        "x2hTb": p.dram("x2hTb", [DM, 4], BF16, kind="ExternalOutput"),
    }
    emit_l0_pre(p, io, T)
    return p.build()




class Guard:
    def __init__(self):
        self.rd = {}
        self.wr = None

    def need_w(self):
        out = [(s, v) for s, v in self.rd.items()]
        if self.wr is not None:
            out.append(self.wr)
        return out

    def need_r(self):
        return [] if self.wr is None else [self.wr]

    def did_w(self, sem):
        self.wr = (sem, sem.v)
        self.rd = {}

    def did_r(self, sem):
        self.rd[sem] = sem.v


def emit_mamba_local(p, io, T=2048, tag="f", debug=False):
    NCH = T // CH
    NT1 = max(1, T // 512)
    TT1 = min(512, T)
    NTB = T // 128
    TS = T + 4

    x2T = io["x2T"]
    x2hT = io["x2hT"]
    wInT = io["wInT"]
    cw = io["cw"]
    cbr = io["cbr"]
    dtb = io["dtb"]
    alog = io["alog"]
    dexpr = io["dexpr"]
    yloc = io["yloc"]
    ztm = io["ztm"]
    cfm_o = io["cfm"]
    agT_o = io["agT"]
    sout_o = io["sout"]
    xbcpre = p.dram("xbcpre_" + tag, [CONVD, TS], BF16, kind="Internal")
    abscr = p.dram("abscr_" + tag, [2, NH, T], kind="Internal")

    A1 = max(8 * DIP, 18 * T)
    arena1 = p.sb("arena1", [128, A1 + 2 * DI + 1024], BF16)
    A1s = arena1.shape[1]

    def a1v(off, n):
        return bass.AP(tensor=arena1, offset=off, ap=[[A1s, 128], [1, n]])
    W16o = 0
    CV16o = 0
    A2 = max(8 * 2052, 2 * TS + NH * CH * 2 + 1024 + DI)
    arena2 = p.sb("arena2", [128, A2], BF16)

    def a2v(off, n):
        return bass.AP(tensor=arena2, offset=off, ap=[[A2, 128], [1, n]])
    CV16 = a1v(0, 18 * T)
    zst = a1v(A1, 2 * DI)
    xst = a1v(A1 + 2 * DI, 1024)
    X16o = 0
    cvino = 0
    eDo = 2 * TS
    scoresTo = eDo
    tmp2o = eDo + NH * CH
    Dbco = tmp2o + 1024
    eD = a2v(eDo, NH * CH)
    scoresT = a2v(scoresTo, NH * CH)
    tmp2 = a2v(tmp2o, 1024)
    Dbc = a2v(Dbco, DI)
    cacc = p.sb("cacc", [128, 1024])
    tmp1 = bass.AP(tensor=cacc, offset=0, ap=[[1024, 128], [1, 512]])
    iot = bass.AP(tensor=cacc, offset=512, ap=[[1024, 128], [1, 128]])
    wch = bass.AP(tensor=cacc, offset=640, ap=[[1024, NH], [1, CH]])
    onesNH = bass.AP(tensor=cacc, offset=768, ap=[[1024, NH], [1, CH]])
    dtraw = p.sb("dtraw", [NH, T])
    lndt = p.sb("lndt", [NH, T])
    arel = p.sb("arel", [NH, T])
    scr32 = p.sb("scr32", [NH, T])
    btile = scr32
    aoff = p.sb("aoff", [NH, 1])
    dtbc = p.sb("dtbc", [NH, 1])
    acol = p.sb("acol", [NH, 1])

    S16 = p.sb("S16", [128, DI], BF16)
    xs_tm = p.sb("xs_tm", [128, DI], BF16)
    btm = p.sb("btm", [128, DST], BF16)
    wx = p.sb("wx", [128, DI], BF16)
    cbt = p.sb("cbt", [128, CH])
    aTt = p.sb("aTt", [128, NH])
    waT = p.sb("waT", [128, NH])
    expaT = p.sb("expaT", [128, NH])
    Pb = p.sb("Pb", [128, NH])
    prow = p.sb("prow", [1, NH])
    brow = p.sb("brow", [1, NH * CH])
    arow = p.sb("arow", [1, NH * CH])

    ybuf = p.sb("ybuf", [128, DI])
    onesr = p.sb("onesr", [1, 128])

    onec = p.sb("onec", [NH, 1])
    identf = p.sb("identf", [128, 128])
    ident = p.sb("identb", [128, 128], BF16)
    vneg = p.sb("vneg", [128, 128], BF16)

    cbcol = p.sb("cbcol", [128, 18])
    cwt = p.sb("cwt", [128, 4 * 18])

    pd = p.ps("pd", [128, 2048])
    py = p.ps("py", [128, 1024])
    pt = p.ps("pt", [128, 1024], BF16)
    pt2 = p.ps("pt2", [128, 1024], BF16)

    ld = p.sem("ld")
    lg = p.sem("lg")
    ab0 = p.sem("ab0")
    ab1 = p.sem("ab1")
    shf = p.sem("shf")
    dv = p.sem("dv")
    sc = p.sem("sc")
    gp = p.sem("gp")
    mm = p.sem("mm")
    so = p.sem("so")
    sx0 = p.sem("sx0")
    sx1 = p.sem("sx1")
    sz0 = p.sem("sz0")
    sz1 = p.sem("sz1")
    s2s = p.sem("s2s")
    sy = p.sem("sy")
    cvl0 = p.sem("cvl0")
    cvl1 = p.sem("cvl1")

    gA = Guard()
    gB = Guard()
    gy0 = Guard()
    gy1 = Guard()
    gt0 = Guard()
    gt1 = Guard()
    gz = [Guard(), Guard()]
    gx = [Guard(), Guard()]
    gcv = [Guard(), Guard()]

    p.op("gpsimd", "memset", onesr[:, :], 1.0, inc=gp)
    p.op("gpsimd", "memset", onec[:, :], 1.0, inc=gp)
    p.op("gpsimd", "memset", onesNH[:, :], 1.0, inc=gp)

    p.op("gpsimd", "memset", S16[:, :], 0.0, inc=gp)
    p.op("gpsimd", "memset", a2v(X16o, 8 * 2052), 0.0, inc=gp)
    p.op("gpsimd", "iota", iot[:, :], pattern=[[1, 128]], base=0,
         channel_multiplier=-1, allow_small_or_imprecise_dtypes=True, inc=gp)
    p.op("vector", "tensor_scalar", ident[:, :], iot[:, :], 0.0, None, OP.is_equal,
         waits=[(gp, gp.v)], inc=dv)
    p.drain("vector")
    p.op("vector", "tensor_scalar", identf[:, :], iot[:, :], 0.0, None, OP.is_equal, inc=dv)
    p.drain("vector")
    p.op("vector", "tensor_scalar", vneg[:, :], iot[:, :], 0.0, None, OP.is_lt, inc=dv)
    p.drain("vector")
    p.op("vector", "tensor_scalar", vneg[:, :], vneg[:, :], float(NEG), None, OP.mult, inc=dv)
    p.drain("vector")
    v_const = dv.v

    p.op("gpsimd", "dma_start",
         out=a1v(W16o, 8 * DIP).rearrange("p (k o) -> p k o", k=8),
         in_=wInT.rearrange("(k p) o -> p k o", p=128), inc=lg, incn=16)
    p.op("gpsimd", "dma_start",
         out=a2v(X16o, 8 * 2052).rearrange("p (k c) -> p k c", k=8)[:, :, 4:4 + T],
         in_=x2T.rearrange("(k p) t -> p k t", p=128),
         waits=[(gp, gp.v)], inc=lg, incn=16)
    p.op("gpsimd", "dma_start",
         out=a2v(X16o, 8 * 2052).rearrange("p (k c) -> p k c", k=8)[:, :, 0:4],
         in_=x2hT.rearrange("(k p) t -> p k t", p=128), inc=lg, incn=16)
    p.op("sync", "dma_start", out=cbcol[:, :], in_=cbr.rearrange("o p -> p o"),
         allow_slow_non_contiguous=True, inc=ld, incn=16)
    p.op("sync", "dma_start", out=cwt[:, :].rearrange("p (k o) -> p k o", k=4),
         in_=cw.rearrange("(o p) k -> p k o", p=128), allow_slow_non_contiguous=True,
         inc=ld, incn=16)
    p.op("sync", "dma_start", out=acol[:, :], in_=alog[:, :], inc=ld, incn=16)
    p.op("sync", "dma_start", out=dtbc[:, :], in_=dtb[:, :], inc=ld, incn=16)
    p.op("sync", "dma_start", out=ybuf[0:1, :], in_=dexpr[:, :], inc=ld, incn=16)
    ld_all = ld.v

    p.wait("tensor", lg, lg.v)
    p.wait("tensor", dv, v_const)

    def xw(kb, c0, c1):
        return a2v(X16o + kb * 2052 + c0, c1 - c0)

    def wsl(kb, o0, o1):
        return a1v(W16o + kb * DIP + o0, o1 - o0)

    regs = [(0, gA), (512, gA)]
    gR = [Guard(), Guard()]
    for oi in range(16, 35):
        o0 = oi * 128
        nP = 128 if oi < 34 else 32
        for tb in range(NT1):
            t0 = tb * TT1
            ri = tb % 2
            reg = ri * 512
            for kb in range(8):
                p.op("tensor", "matmul", pd[0:nP, reg:reg + TT1],
                     wsl(kb, o0, o0 + nP), xw(kb, 4 + t0, 4 + t0 + TT1),
                     start=(kb == 0), stop=(kb == 7),
                     waits=gR[ri].need_w() if kb == 0 else (), inc=mm)
            gR[ri].did_w(mm)
            if oi < 34:
                xi = tb % 2
                p.op("vector", "tensor_copy", xst[:, xi * 512:xi * 512 + TT1],
                     pd[:, reg:reg + TT1],
                     waits=[(mm, mm.v)] + gx[xi].need_w(), inc=dv)
                gR[ri].did_r(dv)
                gx[xi].did_w(dv)
                sxs = sx0 if xi == 0 else sx1
                p.op("sync", "dma_start",
                     out=xbcpre[(oi - 16) * 128:(oi - 15) * 128, 3 + t0:3 + t0 + TT1],
                     in_=xst[:, xi * 512:xi * 512 + TT1], waits=[(dv, dv.v)],
                     inc=sxs, incn=16)
                gx[xi].did_r(sxs)
            else:
                p.op("vector", "tensor_copy", dtraw[:, t0:t0 + TT1],
                     pd[0:32, reg:reg + TT1], waits=[(mm, mm.v)], inc=dv)
                gR[ri].did_r(dv)
    gH = Guard()
    for oi in range(16, 34):
        o0 = oi * 128
        for kb in range(8):
            p.op("tensor", "matmul", pd[:, 1024:1028],
                 wsl(kb, o0, o0 + 128), xw(kb, 0, 4),
                 start=(kb == 0), stop=(kb == 7),
                 waits=gH.need_w() if kb == 0 else (), inc=mm)
        gH.did_w(mm)
        xi = oi % 2
        p.op("vector", "tensor_copy", xst[:, xi * 512:xi * 512 + 4], pd[:, 1024:1028],
             waits=[(mm, mm.v)] + gx[xi].need_w(), inc=dv)
        gH.did_r(dv)
        gx[xi].did_w(dv)
        sxs = sx0 if xi == 0 else sx1
        p.op("sync", "dma_start", out=xbcpre[(oi - 16) * 128:(oi - 15) * 128, 0:3],
             in_=xst[:, xi * 512:xi * 512 + 3], waits=[(dv, dv.v)], inc=sxs, incn=16)
        gx[xi].did_r(sxs)
    gY = [gy0, gy1]
    for tb in range(NTB):
        zi = tb % 2
        for ob in range(4):
            o0 = ob * 512
            ri = ob % 2
            reg = ri * 512
            for kb in range(8):
                p.op("tensor", "matmul", py[:, reg:reg + 512],
                     xw(kb, 4 + tb * 128, 4 + tb * 128 + 128),
                     wsl(kb, o0, o0 + 512),
                     start=(kb == 0), stop=(kb == 7),
                     waits=gY[ri].need_w() if kb == 0 else (), inc=mm)
            gY[ri].did_w(mm)
            p.op("scalar", "activation", zst[:, zi * DI + o0:zi * DI + o0 + 512],
                 py[:, reg:reg + 512], AF.Silu,
                 waits=[(mm, mm.v)] + (gz[zi].need_w() if ob == 0 else []), inc=sc)
            gY[ri].did_r(sc)
            gz[zi].did_w(sc)
        szs = sz0 if zi == 0 else sz1
        p.op("sync", "dma_start", out=ztm[tb * 128:(tb + 1) * 128, :],
             in_=zst[:, zi * DI:(zi + 1) * DI], waits=[(sc, sc.v)], inc=szs, incn=16)
        gz[zi].did_r(szs)
    st1_mm = mm.v
    st1_sx = (sx0.v, sx1.v)

    p.drain("vector")
    p.op("vector", "tensor_scalar", dtraw[:, :], dtraw[:, :], dtbc[0:NH, 0:1], None,
         OP.add, waits=[(ld, ld_all)], inc=dv)
    p.drain("vector")
    p.op("scalar", "activation", scr32[:, :], dtraw[:, :], AF.Abs,
         waits=[(dv, dv.v)], inc=sc)
    p.drain("scalar")
    p.op("scalar", "activation", scr32[:, :], scr32[:, :], AF.Exp, scale=-1.0, inc=sc)
    p.drain("scalar")
    p.op("scalar", "activation", scr32[:, :], scr32[:, :], AF.Ln, bias=onec[0:NH, 0:1], inc=sc)
    p.drain("scalar")
    p.op("vector", "tensor_scalar", lndt[:, :], dtraw[:, :], 0.0, None, OP.max, inc=dv)
    p.drain("vector")
    p.op("vector", "tensor_tensor", dtraw[:, :], lndt[:, :], scr32[:, :], OP.add,
         waits=[(sc, sc.v)], inc=dv)
    p.drain("vector")
    p.op("scalar", "activation", lndt[:, :], dtraw[:, :], AF.Ln, waits=[(dv, dv.v)], inc=sc)
    p.drain("scalar")
    p.op("scalar", "activation", acol[:, :], acol[:, :], AF.Exp, inc=sc)
    p.drain("scalar")
    p.op("scalar", "mul", acol[:, :], acol[:, :], -1.0, inc=sc)
    p.drain("scalar")
    p.op("vector", "tensor_scalar", scr32[:, :], dtraw[:, :], acol[0:NH, 0:1], None,
         OP.mult, waits=[(sc, sc.v)], inc=dv)
    p.drain("vector")
    for c in range(NCH):
        s0 = c * CH
        p.op("vector", "tensor_tensor_scan", arel[:, s0:s0 + CH],
             onesNH[:, :], scr32[:, s0:s0 + CH], 0.0, OP.mult, OP.add, inc=dv)
        p.drain("vector")
    p.op("gpsimd", "memset", aoff[:, :], 0.0, waits=[(ld, ld_all)], inc=gp)
    for c in range(NCH):
        s0 = c * CH
        p.op("vector", "tensor_scalar", dtraw[:, s0:s0 + CH], arel[:, s0:s0 + CH],
             aoff[0:NH, 0:1], None, OP.add, waits=[(gp, gp.v)], inc=dv)
        p.drain("vector")
        p.op("vector", "tensor_copy", aoff[:, :], dtraw[:, s0 + CH - 1:s0 + CH], inc=dv)
        p.drain("vector")
    p.op("vector", "tensor_tensor", btile[:, :], lndt[:, :], arel[:, :], OP.subtract,
         inc=dv)
    p.drain("vector")
    p.op("sync", "dma_start", out=abscr[0, :, :], in_=arel[:, :], waits=[(dv, dv.v)],
         inc=s2s, incn=16)
    p.op("sync", "dma_start", out=abscr[1, :, :], in_=btile[:, :], waits=[(dv, dv.v)],
         inc=s2s, incn=16)
    st2_so = s2s.v
    st2_dv = dv.v

    p.wait("tensor", mm, st1_mm)
    for i in range(4):
        p.op("tensor", "matmul", pd[:, i * 512:(i + 1) * 512], onesr[0:1, 0:128],
             ybuf[0:1, i * 512:(i + 1) * 512], start=True, stop=True,
             waits=(gR[0].need_w() + gR[1].need_w() + gH.need_w() + [(ld, ld_all)])
             if i == 0 else (), inc=mm)
    gR[0].did_w(mm)
    gR[1].did_w(mm)
    gH.did_w(mm)
    p.op("vector", "tensor_copy", Dbc[:, :], pd[:, 0:2048],
         waits=[(mm, mm.v), (lg, lg.v)], inc=dv)
    p.drain("vector")
    gR[0].did_r(dv)
    gR[1].did_r(dv)
    gH.did_r(dv)
    CB = min(1024, T)
    NB = T // CB
    Hh = CB // 2
    gcc = [Guard(), Guard()]
    for oi in range(18):
        ci = oi % 2
        cv = cvl0 if ci == 0 else cvl1
        p.op("sync", "dma_start", out=a2v(cvino + ci * TS, T + 3),
             in_=xbcpre[oi * 128:(oi + 1) * 128, 0:T + 3],
             waits=[(sx0, st1_sx[0]), (sx1, st1_sx[1]), (dv, dv.v)] + gcv[ci].need_w(),
             inc=cv, incn=16)
        gcv[ci].did_w(cv)
        for blk in range(NB):
            b0 = blk * CB
            h0 = 0
            p.op("vector", "tensor_scalar", cacc[:, h0:h0 + Hh],
                 a2v(cvino + ci * TS + b0 + h0, Hh),
                 cwt[:, 0 * 18 + oi:0 * 18 + oi + 1], None, OP.mult,
                 waits=[(cv, cv.v), (ld, ld_all)] + gcc[0].need_w(), inc=dv)
            p.drain("vector")
            for k in range(1, 4):
                p.op("vector", "scalar_tensor_tensor", cacc[:, h0:h0 + Hh],
                     a2v(cvino + ci * TS + b0 + h0 + k, Hh),
                     cwt[:, k * 18 + oi:k * 18 + oi + 1], cacc[:, h0:h0 + Hh],
                     OP.mult, OP.add, inc=dv)
                p.drain("vector")
            gcv[ci].did_r(dv)
            gcc[0].did_w(dv)
            h0 = Hh

            p.op("vector", "tensor_scalar", cacc[:, h0:h0 + Hh],
                 a2v(cvino + ci * TS + b0 + h0, Hh),
                 cwt[:, 0 * 18 + oi:0 * 18 + oi + 1], None, OP.mult,
                 waits=[(cv, cv.v), (ld, ld_all)] + gcc[1].need_w(), inc=dv)
            p.drain("vector")
            for k in range(1, 4):
                p.op("vector", "scalar_tensor_tensor", cacc[:, h0:h0 + Hh],
                     a2v(cvino + ci * TS + b0 + h0 + k, Hh),
                     cwt[:, k * 18 + oi:k * 18 + oi + 1], cacc[:, h0:h0 + Hh],
                     OP.mult, OP.add, inc=dv)
                p.drain("vector")
            gcv[ci].did_r(dv)
            gcc[1].did_w(dv)
            p.op("scalar", "activation", CV16[:, oi * T + b0:oi * T + b0 + CB],
                 cacc[:, 0:CB], AF.Silu, bias=cbcol[:, oi:oi + 1],
                 waits=[(dv, dv.v), (gp, gp.v)], inc=sc)
            p.drain("scalar")
            gcc[0].did_r(sc)
            gcc[1].did_r(sc)
    p.op("sync", "dma_start", out=cfm_o[:, :], in_=CV16[:, 17 * T:18 * T],
         waits=[(sc, sc.v)], inc=so, incn=16)
    st25_sc = sc.v

    p.wait("tensor", sc, st25_sc)
    p.wait("tensor", dv, st2_dv)
    gab = [Guard(), Guard()]
    gpd0 = Guard()
    gpd1 = Guard()
    gsm = Guard()
    gyq = [Guard(), Guard()]
    gtp = [Guard(), Guard()]
    gyb = Guard()
    gS = Guard()
    gpd0.rd = dict(gR[0].rd)
    gpd0.wr = gR[0].wr
    gpd1.rd = dict(gH.rd)
    gpd1.wr = gH.wr
    gyq[0].rd = dict(gY[0].rd)
    gyq[0].wr = gY[0].wr
    gyq[1].rd = dict(gY[1].rd)
    gyq[1].wr = gY[1].wr

    def abload(c):
        ci = c % 2
        s0 = c * CH
        sem = ab0 if ci == 0 else ab1
        w = [(s2s, st2_so)] + gab[0].need_w()
        p.op("sync", "dma_start",
             out=arow[0:1, :].rearrange("p (h t) -> p h t", t=CH),
             in_=abscr[0, :, s0:s0 + CH].unsqueeze(0), waits=w, inc=sem, incn=16)
        p.op("sync", "dma_start",
             out=brow[0:1, :].rearrange("p (h t) -> p h t", t=CH),
             in_=abscr[1, :, s0:s0 + CH].unsqueeze(0), inc=sem, incn=16)
        gab[0].did_w(sem)

    abload(0)
    for c in range(NCH):
        ci = c % 2
        s0 = c * CH
        absem = ab0 if ci == 0 else ab1
        abv = absem.v


        Bch = CV16[:, 16 * T + s0:16 * T + s0 + CH]
        Cch = CV16[:, 17 * T + s0:17 * T + s0 + CH]

        p.op("tensor", "matmul", pd[:, 0:CH], Bch, Cch, start=True, stop=True,
             waits=gsm.need_w() + gpd0.need_w(), inc=mm)
        gsm.did_w(mm)
        p.op("vector", "tensor_copy", cbt[:, :], pd[:, 0:CH], waits=[(mm, mm.v)], inc=dv)
        p.drain("vector")
        gsm.did_r(dv)
        p.op("tensor", "transpose", pd[:, 128:128 + NH], arel[:, s0:s0 + CH],
             identf[0:NH, 0:NH], waits=gsm.need_w(), inc=mm)
        gsm.did_w(mm)
        p.op("vector", "tensor_copy", aTt[:, :], pd[:, 128:128 + NH], waits=[(mm, mm.v)], inc=dv)
        p.drain("vector")
        gsm.did_r(dv)
        p.op("scalar", "activation", expaT[:, :], aTt[:, :], AF.Exp, waits=[(dv, dv.v)], inc=sc)
        p.drain("scalar")
        p.op("scalar", "activation", wch[:, :], btile[:, s0:s0 + CH], AF.Exp,
             bias=arel[0:NH, s0 + CH - 1:s0 + CH], inc=sc)
        p.drain("scalar")
        p.op("tensor", "transpose", pd[:, 160:160 + NH], wch[:, :],
             identf[0:NH, 0:NH], waits=gsm.need_w() + [(sc, sc.v)], inc=mm)
        gsm.did_w(mm)
        p.op("vector", "tensor_copy", waT[:, :], pd[:, 160:160 + NH], waits=[(mm, mm.v)], inc=dv)
        p.drain("vector")
        gsm.did_r(dv)
        p.op("scalar", "activation", prow[:, :],
             arow[0:1, :].rearrange("p (h t) -> p h t", t=CH)[:, :, CH - 1],
             AF.Exp, waits=[(absem, abv)], inc=sc)
        gab[0].did_r(sc)
        p.drain("scalar")
        p.op("tensor", "matmul", pd[:, 192:192 + NH], onesr[0:1, 0:128], prow[0:1, :],
             start=True, stop=True, waits=gsm.need_w() + [(sc, sc.v)], inc=mm)
        gsm.did_w(mm)
        p.op("vector", "tensor_copy", Pb[:, :], pd[:, 192:192 + NH], waits=[(mm, mm.v)], inc=dv)
        p.drain("vector")
        gsm.did_r(dv)

        for j in range(17):
            src = CV16[:, j * T + s0:j * T + s0 + CH] if j < 16 else Bch
            dstp = pt if j % 2 == 0 else pt2
            gtj = gtp[j % 2]
            p.op("tensor", "transpose", dstp[:, 0:128], src, ident[:, :],
                 waits=gtj.need_w(), inc=mm)
            gtj.did_w(mm)
            dst = xs_tm[:, j * 128:(j + 1) * 128] if j < 16 else btm[:, 0:128]
            p.op("vector", "tensor_copy", dst, dstp[:, 0:128], waits=[(mm, mm.v)], inc=dv)
            p.drain("vector")
            gtj.did_r(dv)
        xsdone = dv.v

        for H in range(2):
            wv = gpd0.need_w() + gpd1.need_w() + [(absem, abv)] + (gsm.need_w() if H == 0 else [])
            for b4 in range(4):
                h4 = H * 16 + b4 * 4
                p.op("tensor", "matmul", pd[:, b4 * 512:(b4 + 1) * 512],
                     onesr[0:1, 0:128], arow[0:1, h4 * CH:(h4 + 4) * CH],
                     start=True, stop=False,
                     waits=wv if b4 == 0 else (), inc=mm)
            for hh in range(16):
                h = H * 16 + hh
                p.op("tensor", "matmul", pd[:, hh * 128:(hh + 1) * 128],
                     brow[0:1, h * CH:(h + 1) * CH], onesr[0:1, 0:128],
                     start=False, stop=False, inc=mm)
            for b in range(4):
                p.op("tensor", "matmul", pd[:, b * 512:(b + 1) * 512], ident[:, :],
                     ap_v(vneg, 0, [[128, 128], [0, 4], [1, 128]]),
                     start=False, stop=True, inc=mm)
            gpd0.did_w(mm)
            gpd1.did_w(mm)
            for b in range(4):
                p.op("scalar", "activation",
                     eD[:, H * 2048 + b * 512:H * 2048 + (b + 1) * 512],
                     pd[:, b * 512:(b + 1) * 512], AF.Exp, waits=[(mm, mm.v)], inc=sc)
                p.drain("scalar")
            gpd0.did_r(sc)
            gpd1.did_r(sc)
        gab[0].did_r(mm)
        if c + 1 < NCH:
            abload(c + 1)
        p.op("vector", "tensor_tensor",
             scoresT[:, :].rearrange("p (h t) -> p h t", h=NH),
             eD[:, :].rearrange("p (h t) -> p h t", h=NH),
             ap_v(cbt, 0, [[CH, 128], [0, NH], [1, CH]]),
             OP.mult, waits=[(sc, sc.v), (dv, dv.v)], inc=dv)
        p.drain("vector")
        scdone = dv.v

        for q in range(4):
            yr = (q % 2) * 512
            gq = gyq[q % 2]
            gpdq = gpd0 if q % 2 == 0 else gpd1
            for hh in range(8):
                h = q * 8 + hh
                p.op("tensor", "matmul", py[:, yr + hh * 64:yr + (hh + 1) * 64],
                     scoresT[:, h * CH:(h + 1) * CH], xs_tm[:, h * 64:(h + 1) * 64],
                     start=(hh == 0), stop=(hh == 7),
                     waits=(gq.need_w() + [(dv, scdone), (dv, xsdone)]) if hh == 0 else (),
                     inc=mm)
            gq.did_w(mm)
            p.op("tensor", "matmul", pd[:, (q % 2) * 512:(q % 2) * 512 + 512],
                 Cch, S16[:, q * 512:(q + 1) * 512], start=True, stop=True,
                 waits=gpdq.need_w() + gS.need_r(), inc=mm)
            gpdq.did_w(mm)
            gS.did_r(mm)
            p.op("vector", "tensor_tensor",
                 tmp1[:, 0:512].rearrange("p (h q) -> p h q", h=8),
                 pd[:, (q % 2) * 512:(q % 2) * 512 + 512].rearrange("p (h q) -> p h q", h=8),
                 ap_v(expaT, q * 8, [[NH, 128], [1, 8], [0, 64]]),
                 OP.mult, waits=[(mm, mm.v)], inc=dv)
            p.drain("vector")
            gpdq.did_r(dv)
            p.op("vector", "tensor_tensor", ybuf[:, q * 512:(q + 1) * 512],
                 py[:, yr:yr + 512], tmp1[:, 0:512], OP.add,
                 waits=(gyb.need_w() if q == 0 else []), inc=dv)
            p.drain("vector")
            gq.did_r(dv)
            p.op("vector", "tensor_tensor", tmp2[:, 0:512], xs_tm[:, q * 512:(q + 1) * 512],
                 Dbc[:, q * 512:(q + 1) * 512], OP.mult, waits=[(dv, xsdone)], inc=dv)
            p.drain("vector")
            p.op("vector", "tensor_tensor", ybuf[:, q * 512:(q + 1) * 512],
                 ybuf[:, q * 512:(q + 1) * 512], tmp2[:, 0:512], OP.add, inc=dv)
            p.drain("vector")
            gyb.did_w(dv)
        p.op("sync", "dma_start", out=yloc[s0:s0 + CH, :], in_=ybuf[:, :],
             waits=[(dv, dv.v)], inc=sy, incn=16)
        gyb.did_r(sy)

        p.op("vector", "tensor_tensor",
             wx[:, :].rearrange("p (h q) -> p h q", h=32),
             xs_tm[:, :].rearrange("p (h q) -> p h q", h=32),
             ap_v(waT, 0, [[NH, 128], [1, 32], [0, 64]]), OP.mult,
             waits=[(dv, xsdone)], inc=dv)
        p.drain("vector")
        for qq in range(4):
            gpdq = gpd0 if qq < 2 else gpd1
            p.op("tensor", "matmul", pd[:, qq * 512:(qq + 1) * 512],
                 btm[:, 0:128], wx[:, qq * 512:(qq + 1) * 512],
                 start=True, stop=True,
                 waits=(gpdq.need_w() + [(dv, dv.v)]) if qq % 2 == 0 else (),
                 inc=mm)
            if qq % 2 == 1:
                gpdq.did_w(mm)
            else:
                gpdq.did_w(mm)
        for half in range(2):
            hof = half * 1024
            p.op("vector", "tensor_tensor",
                 tmp2[:, :].rearrange("p (h q) -> p h q", h=16),
                 S16[:, hof:hof + 1024].rearrange("p (h q) -> p h q", h=16),
                 ap_v(Pb, half * 16, [[NH, 128], [1, 16], [0, 64]]), OP.mult,
                 waits=(gS.need_r() + gS.need_w() + [(gp, gp.v)]) if half == 0 else (), inc=dv)
            p.drain("vector")
            gS.did_r(dv)
            p.op("vector", "tensor_tensor", S16[:, hof:hof + 1024], tmp2[:, :],
                 pd[:, hof:hof + 1024], OP.add, waits=[(mm, mm.v)], inc=dv)
            p.drain("vector")
            (gpd0 if half == 0 else gpd1).did_r(dv)
            gS.did_w(dv)
    p.op("sync", "dma_start", out=sout_o[:, :], in_=S16[:, :], waits=[(dv, dv.v)],
         inc=so, incn=16)
    agst = p.sb("agst", [128, 2 * NH])
    gag = [Guard(), Guard()]
    gagS = [Guard(), Guard()]
    for tt in range(NTB):
        t0 = tt * 128
        ri = tt % 2
        reg = ri * 512
        w0 = [(dv, dv.v), (gp, gp.v), (sc, sc.v)] if tt < 2 else []
        p.op("tensor", "transpose", pd[:, reg:reg + NH], dtraw[:, t0:t0 + 128],
             identf[0:NH, 0:NH], waits=gag[ri].need_w() + w0, inc=mm)
        gag[ri].did_w(mm)
        p.op("vector", "tensor_copy", agst[:, ri * NH:(ri + 1) * NH],
             pd[:, reg:reg + NH],
             waits=[(mm, mm.v)] + gagS[ri].need_w(), inc=dv)
        p.drain("vector")
        gag[ri].did_r(dv)
        gagS[ri].did_w(dv)
        p.op("sync", "dma_start", out=agT_o[t0:t0 + 128, :],
             in_=agst[:, ri * NH:(ri + 1) * NH], waits=[(dv, dv.v)], inc=so, incn=16)
        gagS[ri].did_r(so)
    if debug:
        dbg = {}
        for nm, t, n in (("d_scores", scoresT, NH * CH), ("d_xs", xs_tm, DI),
                         ("d_cbt", cbt, CH), ("d_expaT", expaT, NH),
                         ("d_waT", waT, NH), ("d_Pb", Pb, NH), ("d_btm", btm, DST),
                         ("d_arel", arel, T), ("d_brow", brow, NH * CH),
                         ("d_wx", wx, DI)):
            npart = 32 if nm in ("d_arel",) else (1 if nm == "d_brow" else 128)
            dt_ = BF16 if nm in ("d_scores", "d_xs", "d_btm", "d_wx") else F32
            o = p.dram(nm, [npart, n], dt_, kind="ExternalOutput")
            p.op("sync", "dma_start", out=o[:, :], in_=t[0:npart, 0:n],
                 waits=[(dv, dv.v), (gp, gp.v), (sc, sc.v)], inc=so, incn=16)
    for s in (so, sx0, sx1, sz0, sz1, s2s, sy, ab0, ab1, cvl0, cvl1, lg, ld):
        p.wait("sync", s, s.v)


def build_mamba_local(T=2048, debug=False):
    p = P("mamba_local")
    io = {
        "x2T": p.dram("x2T", [DM, T], BF16),
        "x2hT": p.dram("x2hT", [DM, 4], BF16),
        "wInT": p.dram("wInT", [DM, DIP]),
        "cw": p.dram("cw", [CONVD, 4]),
        "cbr": p.dram("cbr", [18, 128]),
        "dtb": p.dram("dtb", [NH, 1]),
        "alog": p.dram("alog", [NH, 1]),
        "dexpr": p.dram("dexpr", [1, DI]),
        "yloc": p.dram("yloc", [T, DI], kind="ExternalOutput"),
        "ztm": p.dram("ztm", [T, DI], BF16, kind="ExternalOutput"),
        "cfm": p.dram("cfm", [DST, T], BF16, kind="ExternalOutput"),
        "agT": p.dram("agT", [T, NH], kind="ExternalOutput"),
        "sout": p.dram("sout", [DST, DI], BF16, kind="ExternalOutput"),
    }
    emit_mamba_local(p, io, T, tag="x", debug=debug)
    return p.build()



def emit_mamba_fin(p, io, T=2048):
    NTB = T // 128
    NB5 = max(1, T // 512)
    TB5 = min(512, T)
    projT = io["projT"]
    pbias = io["pbias"]
    xn = io["xn"]
    s_o = io["s"]
    mscr = p.dram("mscr", [2, DM, T], BF16, kind="Internal")

    WO = p.sb("WO", [128, 16 * DM], BF16)
    SO = p.sb("SO", [128, 8 * DI], BF16)
    PJ = p.sb("PJ", [128, 16 * DM], BF16)
    ynT = p.sb("ynT", [128, 16 * TB5], BF16)
    ytile = p.sb("ytile", [128, DI])
    tmpf = p.sb("tmpf", [128, DI])
    ygt = p.sb("ygt", [128, DI])
    ygb = p.sb("ygb", [128, DI], BF16)
    ztile = p.sb("ztile", [128, DI], BF16)
    cft = p.sb("cft", [128, T], BF16)
    agt = p.sb("agt", [128, NH])
    expag = p.sb("expag", [128, NH])
    Sin = p.sb("Sin", [128, DI], BF16)
    Tacc = p.sb("Tacc", [128, DI])
    Tb = p.sb("Tb", [128, DI])
    Prow = p.sb("Prow", [1, NH])
    Pbc = p.sb("Pbc", [128, NH])
    aer = p.sb("aer", [1, 8 * NH])
    mrow = p.sb("mrow", [1, 8])
    irow = p.sb("irow", [1, 8])
    mtil = p.sb("mtil", [128, 8])
    nwbc = p.sb("nwbc", [128, DI], BF16)
    nwrow = p.sb("nwrow", [1, DI])
    xnt = p.sb("xnt", [128, DM])
    sout = p.sb("sout", [128, DM])
    mstage = p.sb("mstage", [128, 2 * 512], BF16)
    mtv = p.sb("mtv", [128, 2 * 128], BF16)
    pbrow = p.sb("pbrow", [1, DM])
    onesr = p.sb("onesr", [1, 128])
    ident = p.sb("identb2", [128, 128], BF16)
    iot = p.sb("iot2", [128, 128])
    rcol = p.sb("rcol", [128, 1])
    epscol = p.sb("epscol2", [128, 1])

    pc = p.ps("pc", [128, 2048])
    pt = p.ps("ptA", [128, 1024], BF16)
    pt2 = p.ps("ptB", [128, 1024], BF16)
    pw = p.ps("pw", [128, 1024])

    ld = p.sem("ld")
    l2 = p.sem("l2")
    dv = p.sem("dv")
    sc = p.sem("sc")
    gp = p.sem("gp")
    mm = p.sem("mm")
    so = p.sem("so")
    sm = p.sem("sm")
    sm1 = p.sem("sm1")
    lv0 = p.sem("lv0")
    lv1 = p.sem("lv1")

    gpc = Guard()
    gpt = [Guard(), Guard()]
    gpw = Guard()
    gyn = Guard()
    gms = [Guard(), Guard()]
    gmtv = [Guard(), Guard()]
    gTb = Guard()

    p.op("gpsimd", "memset", onesr[:, :], 1.0, inc=gp)
    p.op("gpsimd", "memset", epscol[:, :], EPS, inc=gp)
    p.op("gpsimd", "iota", iot[:, :], pattern=[[1, 128]], base=0,
         channel_multiplier=-1, allow_small_or_imprecise_dtypes=True, inc=gp)
    p.op("vector", "tensor_scalar", ident[:, :], iot[:, :], 0.0, None, OP.is_equal,
         waits=[(gp, gp.v)], inc=dv)
    p.drain("vector")
    p.op("sync", "dma_start", out=pbrow[:, :], in_=pbias[:, :], inc=ld, incn=16)
    p.op("gpsimd", "dma_start", out=PJ[:, :].rearrange("p (k o) -> p k o", k=16),
         in_=projT.rearrange("(k p) o -> p k o", p=128), inc=l2, incn=16)

    for di, d in enumerate(("f", "b")):
        p.op("gpsimd", "dma_start", out=SO[:, :].rearrange("p (j n) -> p j n", j=8),
             in_=io["so" + d].rearrange("(j p) n -> p j n", p=128),
             waits=[(dv, dv.v), (gp, gp.v)], inc=l2, incn=16)
        p.op("gpsimd", "dma_start", out=WO[:, :].rearrange("p (k o) -> p k o", k=16),
             in_=io["wo" + d].rearrange("(k p) o -> p k o", p=128), inc=l2, incn=16)
        p.op("gpsimd", "dma_start", out=cft[:, :], in_=io["cf" + d][:, :], inc=l2, incn=16)
        p.op("sync", "dma_start", out=aer[:, :], in_=io["ae" + d][:, :], inc=ld, incn=16)
        p.op("sync", "dma_start", out=mrow[:, :], in_=io["mrow" + d][:, :], inc=ld, incn=16)
        p.op("sync", "dma_start", out=irow[:, :], in_=io["irow" + d][:, :], inc=ld, incn=16)
        p.op("sync", "dma_start", out=mtil[:, :], in_=io["mtil" + d][:, :], inc=ld, incn=16)
        p.op("sync", "dma_start", out=nwrow[:, :], in_=io["nw" + d][:, :], inc=ld, incn=16)
        ldv = ld.v
        l2v = l2.v
        for i in range(4):
            p.op("tensor", "matmul", pw[:, (i % 2) * 512:(i % 2) * 512 + 512],
                 onesr[0:1, 0:128], nwrow[0:1, i * 512:(i + 1) * 512],
                 start=True, stop=True,
                 waits=(gpw.need_w() + [(ld, ldv), (dv, dv.v)]) if i < 2 else [(dv, dv.v)],
                 inc=mm)
            p.op("vector", "tensor_copy", nwbc[:, i * 512:(i + 1) * 512],
                 pw[:, (i % 2) * 512:(i % 2) * 512 + 512], waits=[(mm, mm.v)], inc=dv)
            p.drain("vector")
        gpw.did_w(mm)
        gpw.did_r(dv)

        p.op("gpsimd", "memset", Tacc[:, :], 0.0, waits=[(dv, dv.v)], inc=gp)
        order = range(8) if d == "f" else range(7, -1, -1)
        gpr = Guard()
        for j in order:
            p.op("scalar", "activation", Prow[:, :], aer[0:1, j * NH:(j + 1) * NH],
                 AF.Exp, waits=[(ld, ldv), (gp, gp.v)] + gpr.need_w(), inc=sc)
            p.drain("scalar")
            gpr.did_w(sc)
            p.op("vector", "tensor_scalar", Prow[:, :], Prow[:, :],
                 mrow[0:1, j:j + 1], irow[0:1, j:j + 1], OP.mult, op1=OP.add,
                 waits=[(sc, sc.v)], inc=dv)
            p.drain("vector")
            gpr.did_w(dv)
            p.op("tensor", "matmul", pw[:, 0:NH], onesr[0:1, 0:128], Prow[0:1, :],
                 start=True, stop=True, waits=gpw.need_w() + [(dv, dv.v)], inc=mm)
            gpw.did_w(mm)
            gpr.did_r(mm)
            p.op("vector", "tensor_copy", Pbc[:, :], pw[:, 0:NH], waits=[(mm, mm.v)], inc=dv)
            p.drain("vector")
            gpw.did_r(dv)
            p.op("vector", "tensor_tensor",
                 Tb[:, :].rearrange("p (h q) -> p h q", h=NH),
                 Tacc[:, :].rearrange("p (h q) -> p h q", h=NH),
                 ap_v(Pbc, 0, [[NH, 128], [1, NH], [0, HD]]), OP.mult,
                 waits=gTb.need_w() + [(gp, gp.v)], inc=dv)
            p.drain("vector")
            p.op("vector", "scalar_tensor_tensor", Tacc[:, :],
                 ap_v(SO, j * DI, [[8 * DI, 128], [1, DI]]),
                 mtil[0:128, j:j + 1], Tb[:, :], OP.mult, OP.add,
                 waits=[(l2, l2v)], inc=dv)
            p.drain("vector")
            gTb.did_r(dv)
        p.op("vector", "tensor_copy", Sin[:, :], Tacc[:, :], inc=dv)
        p.drain("vector")

        for bb in range(NB5):
            for i in range(TB5 // 128):
                tt = bb * (TB5 // 128) + i
                t0 = tt * 128
                p.op("sync", "dma_start", out=ytile[:, :], in_=io["y" + d][t0:t0 + 128, :],
                     waits=gyn.need_w() + [(dv, dv.v)], inc=ld, incn=16)
                p.op("sync", "dma_start", out=ztile[:, :], in_=io["z" + d][t0:t0 + 128, :],
                     inc=ld, incn=16)
                p.op("sync", "dma_start", out=agt[:, :], in_=io["ag" + d][t0:t0 + 128, :],
                     inc=ld, incn=16)
                ldv2 = ld.v
                p.op("scalar", "activation", expag[:, :], agt[:, :], AF.Exp,
                     waits=[(ld, ldv2)], inc=sc)
                p.drain("scalar")
                for q in range(4):
                    p.op("tensor", "matmul", pc[:, q * 512:(q + 1) * 512],
                         cft[:, t0:t0 + 128], Sin[:, q * 512:(q + 1) * 512],
                         start=True, stop=True,
                         waits=(gpc.need_w() + [(dv, dv.v), (l2, l2v)]) if q == 0 else (),
                         inc=mm)
                gpc.did_w(mm)
                p.op("vector", "tensor_tensor",
                     tmpf[:, :].rearrange("p (h q) -> p h q", h=NH),
                     pc[:, :].rearrange("p (h q) -> p h q", h=NH),
                     ap_v(expag, 0, [[NH, 128], [1, NH], [0, HD]]), OP.mult,
                     waits=[(mm, mm.v), (sc, sc.v)], inc=dv)
                p.drain("vector")
                gpc.did_r(dv)
                p.op("vector", "tensor_tensor", ygt[:, :], ytile[:, :], tmpf[:, :],
                     OP.add, waits=[(ld, ldv2)], inc=dv)
                p.drain("vector")
                p.op("vector", "tensor_tensor", ygb[:, :], ygt[:, :], ztile[:, :],
                     OP.mult, inc=dv)
                p.drain("vector")
                p.op("scalar", "activation", tmpf[:, :], ygb[:, :], AF.Square,
                     accum_out=rcol[:, :], waits=[(dv, dv.v)], inc=sc)
                p.drain("scalar")
                p.op("scalar", "activation", rcol[:, :], rcol[:, :], AF.Ln,
                     scale=1.0 / DI, bias=epscol[0:128, 0:1], inc=sc)
                p.drain("scalar")
                p.op("scalar", "activation", rcol[:, :], rcol[:, :], AF.Exp, scale=-0.5,
                     inc=sc)
                p.drain("scalar")
                p.op("vector", "scalar_tensor_tensor", ztile[:, :], ygb[:, :],
                     rcol[0:128, 0:1], nwbc[:, :], OP.mult, OP.mult,
                     waits=[(sc, sc.v)], inc=dv)
                p.drain("vector")
                for kb in range(16):
                    dstp = pt if kb % 2 == 0 else pt2
                    gtj = gpt[kb % 2]
                    p.op("tensor", "transpose", dstp[:, 0:128],
                         ztile[:, kb * 128:(kb + 1) * 128], ident[:, :],
                         waits=gtj.need_w() + gyn.need_w() + [(dv, dv.v)], inc=mm)
                    gtj.did_w(mm)
                    p.op("vector", "tensor_copy", ynT[:, kb * TB5 + i * 128:kb * TB5 + i * 128 + 128],
                         dstp[:, 0:128], waits=[(mm, mm.v)], inc=dv)
                    p.drain("vector")
                    gtj.did_r(dv)
                gyn.did_w(dv)
            for ob in range(8):
                ri = ob % 2
                for kb in range(16):
                    if di == 0:
                        rhsv = ynT[:, kb * TB5:(kb + 1) * TB5]
                    else:
                        rhsv = ap_v(ynT, kb * TB5 + TB5 - 1,
                                    [[16 * TB5, 128], [-1, TB5]])
                    p.op("tensor", "matmul", pw[:, ri * 512:ri * 512 + TB5],
                         WO[:, kb * DM + ob * 128:kb * DM + (ob + 1) * 128],
                         rhsv,
                         start=(kb == 0), stop=(kb == 15),
                         waits=(gpw.need_w() + [(dv, dv.v), (l2, l2v)]) if kb == 0 else (),
                         inc=mm)
                gpw.did_w(mm)
                si = ob % 2
                p.op("vector", "tensor_copy", mstage[:, si * 512:si * 512 + TB5],
                     pw[:, ri * 512:ri * 512 + TB5],
                     waits=[(mm, mm.v)] + gms[si].need_w(), inc=dv)
                p.drain("vector")
                gpw.did_r(dv)
                gms[si].did_w(dv)
                gyn.did_r(mm)
                sms = sm if si == 0 else sm1
                c0s = bb * TB5 if di == 0 else T - (bb + 1) * TB5
                p.op("sync", "dma_start",
                     out=mscr[di, ob * 128:(ob + 1) * 128, c0s:c0s + TB5],
                     in_=mstage[:, si * 512:si * 512 + TB5], waits=[(dv, dv.v)],
                     inc=sms, incn=16)
                gms[si].did_r(sms)

    smv = sm.v
    smv1 = sm1.v
    ldC = ld.v
    gxn = Guard()
    gso = Guard()
    for tt in range(NTB):
        t0 = tt * 128
        p.op("sync", "dma_start", out=xnt[:, :], in_=xn[t0:t0 + 128, :],
             waits=gxn.need_w(), inc=ld, incn=16)
        ldv3 = ld.v
        for half in range(2):
            for di in range(2):
                for kb in range(8):
                    msl = mscr[di, kb * 128:(kb + 1) * 128, t0:t0 + 128]
                    li = (di * 8 + kb) % 2
                    lsem = lv0 if li == 0 else lv1
                    p.op("gpsimd", "dma_start", out=mtv[:, li * 128:(li + 1) * 128],
                         in_=msl, waits=[(sm, smv), (sm1, smv1)] + gmtv[li].need_w(),
                         inc=lsem, incn=16)
                    gmtv[li].did_w(lsem)
                    p.op("tensor", "matmul", pw[:, half * 512:(half + 1) * 512],
                         mtv[:, li * 128:(li + 1) * 128],
                         PJ[:, (di * 8 + kb) * DM + half * 512:(di * 8 + kb) * DM + half * 512 + 512],
                         start=(di == 0 and kb == 0), stop=False,
                         waits=[(lsem, lsem.v)] + (gpw.need_w() if (di == 0 and kb == 0) else []),
                         inc=mm)
                    gmtv[li].did_r(mm)
            p.op("tensor", "matmul", pw[:, half * 512:(half + 1) * 512],
                 onesr[0:1, 0:128], pbrow[0:1, half * 512:(half + 1) * 512],
                 start=False, stop=True, waits=[(ld, ldC)], inc=mm)
        gpw.did_w(mm)
        p.op("vector", "tensor_tensor", sout[:, :], pw[:, 0:1024], xnt[:, :], OP.add,
             waits=[(mm, mm.v), (ld, ldv3)], inc=dv)
        p.drain("vector")
        gpw.did_r(dv)
        gxn.did_r(dv)
        p.op("sync", "dma_start", out=s_o[t0:t0 + 128, :], in_=sout[:, :],
             waits=[(dv, dv.v)], inc=so, incn=16)
        gso.did_w(so)
        p.wait("vector", so, so.v)
    for s2 in (so, sm, sm1, ld, l2, lv0, lv1):
        p.wait("sync", s2, s2.v)


def fin_io_dram(p, T=2048, kinds=None):
    kinds = kinds or {}
    io = {}
    for d in ("f", "b"):
        io["y" + d] = p.dram("y" + d, [T, DI])
        io["z" + d] = p.dram("z" + d, [T, DI], BF16)
        io["cf" + d] = p.dram("cf" + d, [DST, T], BF16)
        io["ag" + d] = p.dram("ag" + d, [T, NH])
        io["ae" + d] = p.dram("ae" + d, [1, 8 * NH])
        io["so" + d] = p.dram("so" + d, [8 * DST, DI], BF16)
        io["mrow" + d] = p.dram("mrow" + d, [1, 8])
        io["irow" + d] = p.dram("irow" + d, [1, 8])
        io["mtil" + d] = p.dram("mtil" + d, [128, 8])
        io["wo" + d] = p.dram("wo" + d, [DI, DM])
        io["nw" + d] = p.dram("nw" + d, [1, DI])
    io["projT"] = p.dram("projT", [2 * DM, DM])
    io["pbias"] = p.dram("pbias", [1, DM])
    io["xn"] = p.dram("xn_fin", [T, DM]) if kinds.get("xn_shared") is None else kinds["xn_shared"]
    io["s"] = p.dram("s", [T, DM], kind="ExternalOutput")
    return io


def build_mamba_fin(T=2048):
    p = P("mamba_fin")
    io = fin_io_dram(p, T)
    emit_mamba_fin(p, io, T)
    return p.build()



def emit_moe_route(p, io, T=2048):
    RH = 4096
    NTB = T // 128
    NT5 = max(1, T // 512)
    T5 = min(512, T)
    SUB = T5 // 128
    N2 = 2 * T
    xnT = io["xnT"]
    xn = io["xn"]
    rw1 = io["rw1"]
    rb1c = io["rb1c"]
    rw2 = io["rw2"]
    rb2 = io["rb2"]
    send_o = io["send"]
    counts_o = io["counts"]
    ids_o = io["ids"]
    lpos_o = io["lpos"]
    probs_o = io["probs"]
    rscr = p.dram("rscr", [2, N2], kind="Internal")

    W1 = p.sb("W1", [128, 8 * RH], BF16)
    XT = p.sb("XT", [128, 8 * T], BF16)
    XR = p.sb("XR", [128, 2 * DM], BF16)
    HB = p.sb("HB", [128, 2 * T5], BF16)
    W2 = p.sb("W2", [128, 32 * 8], BF16)
    rb1b = p.sb("rb1b", [128, 32])
    logi = p.sb("logi", [128, NTB * 8])
    idsb = p.sb("idsb", [128, NTB * 2])
    lposb = p.sb("lposb", [128, NTB * 2])
    probsb = p.sb("probsb", [128, NTB * 2])
    m1c = p.sb("m1c", [128, 1])
    m2c = p.sb("m2c", [128, 1])
    lgm = p.sb("lgm", [128, 8])
    msk = p.sb("msk", [128, 8])
    tsc = p.sb("tsc", [128, 8])
    iob8 = p.sb("iob8", [128, 8])
    maskT = p.sb("maskT", [8, N2])
    cumT = p.sb("cumT", [8, N2])
    rowA = p.sb("rowA", [1, N2])
    rowB = p.sb("rowB", [1, N2])
    ones2T = p.sb("ones2T", [8, N2 // 8 if False else 512])
    ccol = p.sb("ccol", [8, 1])
    crow = p.sb("crow", [1, 8])
    startrow = p.sb("startrow", [1, 8])
    startcol = p.sb("startcol", [8, 1])
    iotc8 = p.sb("iotc8", [8, 1])
    ones8 = p.sb("ones8", [8, 1])
    onesr = p.sb("onesr", [1, 128])
    rb2r = p.sb("rb2r", [1, 8])
    id8 = p.sb("id8", [8, 8])
    sidx = p.sb("sidx", [128, 1], U32)
    sidxf = p.sb("sidxf", [128, 1])

    ph = p.ps("ph", [128, 1024])
    pl = p.ps("pl", [128, 2048])
    ps8 = p.ps("ps8", [8, 512])

    ld = p.sem("ld")
    lg = p.sem("lg")
    lgx0 = p.sem("lgx0")
    lgx1 = p.sem("lgx1")
    dv = p.sem("dv")
    sc = p.sem("sc")
    gp = p.sem("gp")
    mm = p.sem("mm")
    sr = p.sem("sr")
    sidl = p.sem("sidl")
    sst = p.sem("sst")
    soo = p.sem("soo")

    gph = [Guard(), Guard()]
    gpl = Guard()
    gp8 = Guard()
    ghb = [Guard(), Guard()]
    gxr = [Guard(), Guard()]
    gsx = Guard()
    gsi = Guard()

    p.op("gpsimd", "memset", onesr[:, :], 1.0, inc=gp)
    p.op("gpsimd", "memset", ones8[:, :], 1.0, inc=gp)
    p.op("gpsimd", "memset", ones2T[:, :], 1.0, inc=gp)
    p.op("gpsimd", "iota", iob8[:, :], pattern=[[1, 8]], base=1000000,
         channel_multiplier=0, allow_small_or_imprecise_dtypes=True, inc=gp)
    p.op("gpsimd", "iota", iotc8[:, :], pattern=[[0, 1]], base=0,
         channel_multiplier=1, allow_small_or_imprecise_dtypes=True, inc=gp)
    p.op("gpsimd", "iota", id8[:, :], pattern=[[1, 8]], base=0,
         channel_multiplier=-1, allow_small_or_imprecise_dtypes=True, inc=gp)
    p.op("vector", "tensor_scalar", id8[:, :], id8[:, :], 0.0, None, OP.is_equal,
         waits=[(gp, gp.v)], inc=dv)
    p.drain("vector")

    p.op("gpsimd", "dma_start", out=W1[:, :].rearrange("p (k o) -> p k o", k=8),
         in_=rw1.rearrange("(k p) o -> p k o", p=128), inc=lg, incn=16)
    p.op("gpsimd", "dma_start", out=XT[:, :].rearrange("p (k t) -> p k t", k=8),
         in_=xnT.rearrange("(k p) t -> p k t", p=128), inc=lg, incn=16)
    p.op("gpsimd", "dma_start", out=W2[:, :].rearrange("p (k o) -> p k o", k=32),
         in_=rw2.rearrange("(k p) o -> p k o", p=128), inc=lg, incn=16)
    p.op("sync", "dma_start", out=rb1b[:, :], in_=rb1c[:, :], inc=ld, incn=16)
    p.op("sync", "dma_start", out=rb2r[:, :], in_=rb2[:, :], inc=ld, incn=16)
    ldv = ld.v

    for t5 in range(NT5):
        t0 = t5 * T5
        for hb in range(32):
            ri = hb % 2
            for kb in range(8):
                p.op("tensor", "matmul", ph[:, ri * 512:ri * 512 + T5],
                     ap_v(W1, kb * RH + hb * 128, [[8 * RH, 128], [1, 128]]),
                     ap_v(XT, kb * T + t0, [[8 * T, 128], [1, T5]]),
                     start=(kb == 0), stop=(kb == 7),
                     waits=(gph[ri].need_w() + [(lg, lg.v)]) if kb == 0 else (), inc=mm)
            gph[ri].did_w(mm)
            p.op("scalar", "activation", HB[:, ri * T5:(ri + 1) * T5],
                 ph[:, ri * 512:ri * 512 + T5], AF.Relu,
                 bias=rb1b[:, hb:hb + 1],
                 waits=[(mm, mm.v), (ld, ldv)] + ghb[ri].need_w(), inc=sc)
            p.drain("scalar")
            gph[ri].did_r(sc)
            ghb[ri].did_w(sc)
            for tt in range(SUB):
                p.op("tensor", "matmul", pl[:, tt * 512:tt * 512 + 8],
                     HB[:, ri * T5 + tt * 128:ri * T5 + (tt + 1) * 128],
                     W2[:, hb * 8:(hb + 1) * 8],
                     start=(hb == 0), stop=False,
                     waits=[(sc, sc.v)] + (gpl.need_w() if (hb == 0 and tt == 0) else []),
                     inc=mm)
            ghb[ri].did_r(mm)
        for tt in range(SUB):
            p.op("tensor", "matmul", pl[:, tt * 512:tt * 512 + 8], onesr[0:1, 0:128],
                 rb2r[0:1, :], start=False, stop=True, inc=mm)
        gpl.did_w(mm)
        p.op("vector", "tensor_copy",
             logi[:, t5 * SUB * 8:(t5 + 1) * SUB * 8].rearrange("p (s e) -> p s e", e=8),
             ap_v(pl, 0, [[2048, 128], [512, SUB], [1, 8]]), waits=[(mm, mm.v)], inc=dv)
        p.drain("vector")
        gpl.did_r(dv)

    for tb in range(NTB):
        lt = logi[:, tb * 8:(tb + 1) * 8]
        p.op("vector", "tensor_reduce", m1c[:, :], lt, AX.X, OP.max, inc=dv)
        p.drain("vector")
        p.op("vector", "tensor_scalar", msk[:, :], lt, m1c[0:128, 0:1], None,
             OP.is_ge, inc=dv)
        p.drain("vector")
        p.op("vector", "scalar_tensor_tensor", tsc[:, :], msk[:, :], -1000000.0,
             iob8[:, :], OP.mult, OP.add, inc=dv)
        p.drain("vector")
        p.op("vector", "tensor_reduce", idsb[:, tb * 2:tb * 2 + 1], tsc[:, :],
             AX.X, OP.min, inc=dv)
        p.drain("vector")
        p.op("vector", "scalar_tensor_tensor", lgm[:, :], msk[:, :], -2000000.0,
             lt, OP.mult, OP.add, inc=dv)
        p.drain("vector")
        p.op("vector", "tensor_reduce", m2c[:, :], lgm[:, :], AX.X, OP.max, inc=dv)
        p.drain("vector")
        p.op("vector", "tensor_scalar", msk[:, :], lgm[:, :], m2c[0:128, 0:1], None,
             OP.is_ge, inc=dv)
        p.drain("vector")
        p.op("vector", "scalar_tensor_tensor", tsc[:, :], msk[:, :], -1000000.0,
             iob8[:, :], OP.mult, OP.add, inc=dv)
        p.drain("vector")
        p.op("vector", "tensor_reduce", idsb[:, tb * 2 + 1:tb * 2 + 2], tsc[:, :],
             AX.X, OP.min, inc=dv)
        p.drain("vector")
        p.op("vector", "tensor_tensor", m2c[:, :], m1c[:, :], m2c[:, :], OP.subtract,
             inc=dv)
        p.drain("vector")
        p.op("scalar", "activation", probsb[:, tb * 2:tb * 2 + 1], m2c[:, :],
             AF.Sigmoid, waits=[(dv, dv.v)], inc=sc)
        p.drain("scalar")
        p.op("vector", "tensor_scalar", probsb[:, tb * 2 + 1:tb * 2 + 2],
             probsb[:, tb * 2:tb * 2 + 1], -1.0, 1.0, OP.mult, op1=OP.add,
             waits=[(sc, sc.v)], inc=dv)
        p.drain("vector")
    p.op("sync", "dma_start", out=ids_o.rearrange("(b p) k -> p b k", p=128),
         in_=idsb[:, :].rearrange("p (b k) -> p b k", k=2),
         waits=[(dv, dv.v)], inc=soo, incn=16)
    p.op("sync", "dma_start", out=probs_o.rearrange("(b p) k -> p b k", p=128),
         in_=probsb[:, :].rearrange("p (b k) -> p b k", k=2),
         waits=[(dv, dv.v)], inc=soo, incn=16)
    p.op("sync", "dma_start",
         out=rscr[0:1, :].rearrange("q (b p k) -> (q p) b k", b=NTB, k=2),
         in_=idsb[:, :].rearrange("p (b k) -> p b k", k=2),
         waits=[(dv, dv.v)], inc=sr, incn=16)
    srv = sr.v

    p.wait("sync", sr, srv)
    for cblk in range(N2 // 512):
        p.op("sync", "dma_start", out=rowA[0:1, cblk * 512:(cblk + 1) * 512],
             in_=rscr[0:1, cblk * 512:(cblk + 1) * 512],
             waits=[(sr, srv)], inc=ld, incn=16)
        p.op("tensor", "matmul", ps8[:, :], ones2T[0:1, 0:8],
             rowA[0:1, cblk * 512:(cblk + 1) * 512], start=True, stop=True,
             waits=[(ld, ld.v)] + gp8.need_w() + [(gp, gp.v)], inc=mm)
        gp8.did_w(mm)
        p.op("vector", "tensor_scalar", maskT[:, cblk * 512:(cblk + 1) * 512],
             ps8[0:8, :], iotc8[0:8, 0:1], None, OP.is_equal,
             waits=[(mm, mm.v)], inc=dv)
        p.drain("vector")
        gp8.did_r(dv)
    p.op("vector", "tensor_tensor_scan", cumT[:, :], ap_v(ones2T, 0, [[512, 8], [0, N2]]),
         maskT[:, :], 0.0, OP.mult, OP.add, inc=dv)
    p.drain("vector")
    p.op("vector", "tensor_copy", ccol[:, :], cumT[0:8, N2 - 1:N2], inc=dv)
    p.drain("vector")
    p.op("sync", "dma_start", out=counts_o[:, :], in_=ccol[0:8, :],
         waits=[(dv, dv.v)], inc=soo, incn=16)
    p.op("tensor", "transpose", pl[0:1, 0:8], ccol[0:8, 0:1], id8[0:8, 0:8],
         waits=gpl.need_w() + [(dv, dv.v)], inc=mm)
    gpl.did_w(mm)
    p.op("vector", "tensor_copy", crow[:, :], pl[0:1, 0:8], waits=[(mm, mm.v)], inc=dv)
    p.drain("vector")
    gpl.did_r(dv)
    p.op("vector", "tensor_tensor_scan", startrow[:, :], ones2T[0:1, 0:8], crow[:, :],
         0.0, OP.mult, OP.add, inc=dv)
    p.drain("vector")
    p.op("vector", "tensor_tensor", startrow[:, :], startrow[:, :], crow[:, :],
         OP.subtract, inc=dv)
    p.drain("vector")
    p.op("tensor", "transpose", pl[0:8, 0:1], startrow[0:1, 0:8], id8[0:1, 0:1],
         waits=gpl.need_w() + [(dv, dv.v)], inc=mm)
    gpl.did_w(mm)
    p.op("vector", "tensor_copy", startcol[0:8, :], pl[0:8, 0:1], waits=[(mm, mm.v)], inc=dv)
    p.drain("vector")
    gpl.did_r(dv)
    p.op("vector", "tensor_scalar", cumT[:, :], cumT[:, :], -1.0, None, OP.add, inc=dv)
    p.drain("vector")
    p.op("vector", "tensor_scalar", cumT[:, :], cumT[:, :], startcol[0:8, 0:1], None,
         OP.add, inc=dv)
    p.drain("vector")
    p.op("vector", "tensor_tensor", cumT[:, :], cumT[:, :], maskT[:, :], OP.mult,
         inc=dv)
    p.drain("vector")
    for cblk in range(N2 // 512):
        p.op("tensor", "matmul", pl[0:1, 0:512], ones2T[0:8, 0:1],
             cumT[0:8, cblk * 512:(cblk + 1) * 512],
             start=True, stop=True, waits=(gpl.need_w() + [(dv, dv.v)]), inc=mm)
        gpl.did_w(mm)
        p.op("vector", "tensor_copy", rowB[0:1, cblk * 512:(cblk + 1) * 512],
             pl[0:1, 0:512], waits=[(mm, mm.v)], inc=dv)
        p.drain("vector")
        gpl.did_r(dv)
    p.op("sync", "dma_start", out=rscr[1:2, :], in_=rowB[0:1, :],
         waits=[(dv, dv.v)], inc=sr, incn=16)
    srv2 = sr.v

    p.wait("sync", sr, srv2)
    for tb in range(NTB):
        t0 = tb * 128
        xi = tb % 2
        lgx = lgx0 if xi == 0 else lgx1
        p.op("gpsimd", "dma_start", out=XR[:, xi * DM:(xi + 1) * DM],
             in_=xn[t0:t0 + 128, :], waits=gxr[xi].need_w(), inc=lgx, incn=16)
        gxr[xi].did_w(lgx)
        for k in range(2):
            p.op("sync", "dma_start", out=sidxf[:, :],
                 in_=ap_v_dram(rscr, N2 + t0 * 2 + k, [[1, 1], [2, 128]]),
                 allow_slow_non_contiguous=True,
                 waits=[(sr, srv2)] + gsx.need_w(), inc=sidl, incn=16)
            gsx.did_w(sidl)
            p.op("vector", "tensor_copy", sidx[:, :], sidxf[:, :],
                 waits=[(sidl, sidl.v)] + gsi.need_w(), inc=dv)
            p.drain("vector")
            gsx.did_r(dv)
            gsi.did_w(dv)
            p.op("gpsimd", "indirect_dma_start", out=send_o[:, :],
                 out_offset=bass.IndirectOffsetOnAxis(ap=sidx[:, 0:1], axis=0),
                 in_=XR[:, xi * DM:(xi + 1) * DM], in_offset=None,
                 waits=[(dv, dv.v), (lgx, lgx.v)], inc=sst, incn=16)
            gxr[xi].did_r(sst)
            gsi.did_r(sst)
    p.op("vector", "tensor_scalar", maskT[:, :], maskT[:, :], startcol[0:8, 0:1],
         None, OP.mult, inc=dv)
    p.drain("vector")
    for cblk in range(N2 // 512):
        p.op("tensor", "matmul", pl[0:1, 0:512], ones2T[0:8, 0:1],
             maskT[0:8, cblk * 512:(cblk + 1) * 512],
             start=True, stop=True, waits=(gpl.need_w() + [(dv, dv.v)]), inc=mm)
        gpl.did_w(mm)
        p.op("vector", "tensor_copy", rowA[0:1, cblk * 512:(cblk + 1) * 512],
             pl[0:1, 0:512], waits=[(mm, mm.v)], inc=dv)
        p.drain("vector")
        gpl.did_r(dv)
    p.op("vector", "tensor_tensor", rowB[:, :], rowB[:, :], rowA[:, :], OP.subtract,
         inc=dv)
    p.drain("vector")
    p.op("sync", "dma_start", out=lpos_o.rearrange("t k -> (t k)").unsqueeze(0),
         in_=rowB[0:1, :], waits=[(dv, dv.v)], inc=soo, incn=16)
    for s2 in (soo, sst, sr, sidl, ld, lg, lgx0, lgx1):
        p.wait("sync", s2, s2.v)


def build_moe_route(T=2048):
    RH = 4096
    N2 = 2 * T
    p = P("moe_route")
    io = {
        "xnT": p.dram("xnT", [DM, T], BF16),
        "xn": p.dram("xn", [T, DM]),
        "rw1": p.dram("rw1", [DM, RH]),
        "rb1c": p.dram("rb1c", [128, 32]),
        "rw2": p.dram("rw2", [RH, 8]),
        "rb2": p.dram("rb2", [1, 8]),
        "send": p.dram("send", [N2, DM], BF16, kind="ExternalOutput"),
        "counts": p.dram("counts", [8, 1], kind="ExternalOutput"),
        "ids": p.dram("ids", [T, 2], kind="ExternalOutput"),
        "lpos": p.dram("lpos", [T, 2], kind="ExternalOutput"),
        "probs": p.dram("probs", [T, 2], kind="ExternalOutput"),
    }
    emit_moe_route(p, io, T)
    return p.build()


def ap_v_dram(t, offset, ap):
    return bass.AP(tensor=t.tensor, offset=offset, ap=[list(x) for x in ap])



def emit_moe_expert(p, io):
    NS = 4096
    H = 4096
    sendall = io["sendall"]
    countsA = io["countsA"]
    onehotE = io["onehotE"]
    ccol_i = io["ccol"]
    ew1 = io["ew1"]
    eb1c = io["eb1c"]
    ew2 = io["ew2"]
    eb2 = io["eb2"]
    eo_o = io["eo"]
    ew2b = p.dram("ew2b", [H, DM], BF16, kind="Internal")

    W1 = p.sb("W1e", [128, 8 * H], BF16)
    HBUF = p.sb("HBUF", [128, 32 * 512], BF16)
    DT = p.sb("DT", [128, 8 * 512], BF16)
    GR = p.sb("GR", [128, 2 * DM], BF16)
    W2s = p.sb("W2s", [128, 2 * DM], BF16)
    OS = p.sb("OS", [128, 2 * DM], BF16)
    cA = p.sb("cA", [8, 8])
    ohE = p.sb("ohE", [8, 8])
    ccol = p.sb("ccol_s", [8, 1])
    crow = p.sb("crow2", [1, 8])
    cumrow = p.sb("cumrow", [1, 8])
    strow = p.sb("strow", [1, 9])
    ddrow = p.sb("ddrow", [1, 8])
    stcol = p.sb("stcol", [8, 1])
    cumb = p.sb("cumb", [128, 8])
    ddb = p.sb("ddb", [128, 8])
    c0b = p.sb("c0b", [128, 1])
    Ptile = p.sb("Ptile", [128, 32])
    idxf = p.sb("idxf", [128, 32])
    idxu = p.sb("idxu", [128, 32], U32)
    msk32 = p.sb("msk32", [128, 32])
    eb1b = p.sb("eb1b", [128, 32])
    eb2r = p.sb("eb2r", [1, DM])
    ones8b = p.sb("ones8b", [8, 8])
    onesr = p.sb("onesr3", [1, 128])
    id8 = p.sb("id8b", [8, 8])
    identb = p.sb("identb3", [128, 128], BF16)
    iotb = p.sb("iotb", [128, 128])
    jr = p.sb("jr", [1, 8])

    ph = p.ps("phE", [128, 1024])
    po = p.ps("poE", [128, 2048])
    pt = p.ps("ptE", [128, 1024], BF16)
    pt2 = p.ps("ptE2", [128, 1024], BF16)

    ld = p.sem("ld")
    lg = p.sem("lg")
    lw0 = p.sem("lw0")
    lw1 = p.sem("lw1")
    gg0 = p.sem("gg0")
    gg1 = p.sem("gg1")
    dv = p.sem("dv")
    sc = p.sem("sc")
    gp = p.sem("gp")
    mm = p.sem("mm")
    so = p.sem("so")
    so1 = p.sem("so1")
    sw = p.sem("sw")
    sw1 = p.sem("sw1")

    gph = [Guard(), Guard()]
    gpo = [Guard(), Guard()]
    gpt = [Guard(), Guard()]
    ggr = [Guard(), Guard()]
    gw2 = [Guard(), Guard()]
    gos = [Guard(), Guard()]
    ghb = Guard()
    gdt = Guard()
    gidx = Guard()

    p.op("gpsimd", "memset", onesr[:, :], 1.0, inc=gp)
    p.op("gpsimd", "memset", ones8b[:, :], 1.0, inc=gp)
    p.op("gpsimd", "iota", iotb[:, :], pattern=[[1, 128]], base=0,
         channel_multiplier=-1, allow_small_or_imprecise_dtypes=True, inc=gp)
    p.op("gpsimd", "iota", Ptile[:, :], pattern=[[128, 32]], base=0,
         channel_multiplier=1, allow_small_or_imprecise_dtypes=True, inc=gp)
    p.op("gpsimd", "iota", jr[:, :], pattern=[[1, 8]], base=0, channel_multiplier=0,
         allow_small_or_imprecise_dtypes=True, inc=gp)
    p.op("gpsimd", "iota", id8[:, :], pattern=[[1, 8]], base=0,
         channel_multiplier=-1, allow_small_or_imprecise_dtypes=True, inc=gp)
    p.op("gpsimd", "memset", strow[0:1, 8:9], 40000.0, inc=gp)
    p.op("vector", "tensor_scalar", identb[:, :], iotb[:, :], 0.0, None, OP.is_equal,
         waits=[(gp, gp.v)], inc=dv)
    p.drain("vector")
    p.op("vector", "tensor_scalar", id8[:, :], id8[:, :], 0.0, None, OP.is_equal,
         inc=dv)
    p.drain("vector")

    p.op("gpsimd", "dma_start", out=W1[:, :].rearrange("p (k o) -> p k o", k=8),
         in_=ew1.rearrange("(k p) o -> p k o", p=128), inc=lg, incn=16)
    p.op("sync", "dma_start", out=cA[:, :], in_=countsA[:, :], inc=ld, incn=16)
    p.op("sync", "dma_start", out=ohE[:, :], in_=onehotE[:, :], inc=ld, incn=16)
    p.op("sync", "dma_start", out=ccol[:, :], in_=ccol_i[:, :], inc=ld, incn=16)
    p.op("sync", "dma_start", out=eb1b[:, :], in_=eb1c[:, :], inc=ld, incn=16)
    p.op("sync", "dma_start", out=eb2r[:, :], in_=eb2[:, :], inc=ld, incn=16)
    ldv = ld.v
    for kh in range(32):
        ws = kh % 2
        wsem = lw0 if ws == 0 else lw1
        p.op("gpsimd", "dma_start", out=W2s[:, ws * DM:(ws + 1) * DM],
             in_=ew2[kh * 128:(kh + 1) * 128, :],
             waits=gw2[ws].need_w(), inc=wsem, incn=16)
        gw2[ws].did_w(wsem)
        ssem = sw if ws == 0 else sw1
        p.op("sync", "dma_start", out=ew2b[kh * 128:(kh + 1) * 128, :],
             in_=W2s[:, ws * DM:(ws + 1) * DM], waits=[(wsem, wsem.v)],
             inc=ssem, incn=16)
        gw2[ws].did_r(ssem)
    swv = (sw.v, sw1.v)

    p.op("vector", "tensor_tensor_scan", cumb[0:8, 0:8], ones8b[:, :], cA[0:8, :],
         0.0, OP.mult, OP.add, waits=[(ld, ldv)], inc=dv)
    p.drain("vector")
    p.op("vector", "tensor_tensor", cumb[0:8, 0:8], cumb[0:8, 0:8], cA[0:8, :],
         OP.subtract, inc=dv)
    p.drain("vector")
    p.op("vector", "tensor_tensor", cumb[0:8, 0:8], cumb[0:8, 0:8], ohE[0:8, :],
         OP.mult, inc=dv)
    p.drain("vector")
    p.op("vector", "tensor_reduce", stcol[0:8, :], cumb[0:8, 0:8], AX.X, OP.add,
         inc=dv)
    p.drain("vector")
    gps = Guard()
    p.op("tensor", "transpose", ph[0:1, 0:8], ccol[0:8, 0:1], id8[0:8, 0:8],
         waits=gps.need_w() + [(dv, dv.v)], inc=mm)
    gps.did_w(mm)
    p.op("vector", "tensor_copy", crow[:, :], ph[0:1, 0:8], waits=[(mm, mm.v)], inc=dv)
    p.drain("vector")
    gps.did_r(dv)
    p.op("vector", "tensor_tensor_scan", cumrow[:, :], ones8b[0:1, 0:8], crow[:, :],
         0.0, OP.mult, OP.add, inc=dv)
    p.drain("vector")
    p.op("tensor", "transpose", ph[0:1, 0:8], stcol[0:8, 0:1], id8[0:8, 0:8],
         waits=gps.need_w() + [(dv, dv.v)], inc=mm)
    gps.did_w(mm)
    p.op("vector", "tensor_copy", strow[0:1, 0:8], ph[0:1, 0:8], waits=[(mm, mm.v)], inc=dv)
    p.drain("vector")
    gps.did_r(dv)
    p.op("vector", "tensor_tensor", strow[0:1, 0:8], strow[0:1, 0:8], cumrow[:, :],
         OP.subtract, inc=dv)
    p.drain("vector")
    p.op("vector", "tensor_tensor", strow[0:1, 0:8], strow[0:1, 0:8], crow[:, :],
         OP.add, inc=dv)
    p.drain("vector")
    p.op("vector", "scalar_tensor_tensor", strow[0:1, 0:8], jr[:, :], 4096.0,
         strow[0:1, 0:8], OP.mult, OP.add, waits=[(gp, gp.v)], inc=dv)
    p.drain("vector")
    p.op("vector", "tensor_tensor", ddrow[0:1, 0:8], strow[0:1, 1:9], strow[0:1, 0:8],
         OP.subtract, inc=dv)
    p.drain("vector")
    p.op("tensor", "matmul", ph[:, 0:8], onesr[0:1, 0:128], cumrow[0:1, :],
         start=True, stop=True, waits=gps.need_w() + [(dv, dv.v)], inc=mm)
    p.op("tensor", "matmul", ph[:, 8:16], onesr[0:1, 0:128], ddrow[0:1, :],
         start=True, stop=True, inc=mm)
    p.op("tensor", "matmul", ph[:, 16:17], onesr[0:1, 0:128], strow[0:1, 0:1],
         start=True, stop=True, inc=mm)
    gps.did_w(mm)
    p.op("vector", "tensor_copy", cumb[:, :], ph[:, 0:8], waits=[(mm, mm.v)], inc=dv)
    p.drain("vector")
    p.op("vector", "tensor_copy", ddb[:, :], ph[:, 8:16], inc=dv)
    p.drain("vector")
    p.op("vector", "tensor_copy", c0b[:, :], ph[:, 16:17], inc=dv)
    p.drain("vector")
    gps.did_r(dv)
    p.op("vector", "tensor_scalar", idxf[:, :], Ptile[:, :], c0b[0:128, 0:1], None,
         OP.add, waits=[(gp, gp.v)], inc=dv)
    p.drain("vector")
    for j in range(8):
        p.op("vector", "tensor_scalar", msk32[:, :], Ptile[:, :], cumb[0:128, j:j + 1],
             None, OP.is_ge, inc=dv)
        p.drain("vector")
        p.op("vector", "scalar_tensor_tensor", idxf[:, :], msk32[:, :],
             ddb[0:128, j:j + 1], idxf[:, :], OP.mult, OP.add, inc=dv)
        p.drain("vector")
    p.op("vector", "tensor_copy", idxu[:, :], idxf[:, :], inc=dv)
    p.drain("vector")
    idxdone = dv.v

    for sb in range(8):
        for it in range(4):
            gi = it % 2
            gsem = gg0 if gi == 0 else gg1
            p.op("gpsimd", "memset", GR[:, gi * DM:(gi + 1) * DM], 0.0,
                 waits=ggr[gi].need_w() + [(dv, idxdone)], inc=gp)
            ggr[gi].did_w(gp)
            p.op("gpsimd", "indirect_dma_start", out=GR[:, gi * DM:(gi + 1) * DM],
                 out_offset=None, in_=sendall[:, :],
                 in_offset=bass.IndirectOffsetOnAxis(
                     ap=idxu[:, sb * 4 + it:sb * 4 + it + 1], axis=0),
                 bounds_check=8 * NS - 1, oob_is_err=False,
                 waits=[(gp, gp.v)], inc=gsem, incn=16)
            ggr[gi].did_w(gsem)
            for kb in range(8):
                dstp = pt if kb % 2 == 0 else pt2
                gtj = gpt[kb % 2]
                p.op("tensor", "transpose", dstp[:, 0:128],
                     GR[:, gi * DM + kb * 128:gi * DM + (kb + 1) * 128], identb[:, :],
                     waits=gtj.need_w() + [(gsem, gsem.v), (gp, gp.v), (dv, dv.v)],
                     inc=mm)
                gtj.did_w(mm)
                p.op("vector", "tensor_copy",
                     DT[:, kb * 512 + it * 128:kb * 512 + it * 128 + 128],
                     dstp[:, 0:128],
                     waits=[(mm, mm.v)] + (gdt.need_w() if (it == 0 and kb == 0) else []),
                     inc=dv)
                p.drain("vector")
                gtj.did_r(dv)
            ggr[gi].did_r(mm)
        gdt.did_w(dv)
        for hb in range(32):
            ri = hb % 2
            for kb in range(8):
                p.op("tensor", "matmul", ph[:, ri * 512:(ri + 1) * 512],
                     ap_v(W1, kb * H + hb * 128, [[8 * H, 128], [1, 128]]),
                     DT[:, kb * 512:(kb + 1) * 512],
                     start=(kb == 0), stop=(kb == 7),
                     waits=(gph[ri].need_w() + gps.need_w() + [(lg, lg.v), (dv, dv.v)])
                     if kb == 0 else (), inc=mm)
            gph[ri].did_w(mm)
            p.op("scalar", "activation", HBUF[:, hb * 512:(hb + 1) * 512],
                 ph[:, ri * 512:(ri + 1) * 512], AF.Relu, bias=eb1b[:, hb:hb + 1],
                 waits=[(mm, mm.v), (ld, ldv)] + (ghb.need_w() if hb < 2 else []),
                 inc=sc)
            p.drain("scalar")
            gph[ri].did_r(sc)
        ghb.did_w(sc)
        gdt.did_r(mm)
        for sp in range(2):
            for kh in range(32):
                ws = kh % 2
                wsem = lw0 if ws == 0 else lw1
                p.op("gpsimd", "dma_start", out=W2s[:, ws * DM:(ws + 1) * DM],
                     in_=ew2b[kh * 128:(kh + 1) * 128, :],
                     waits=[(sw, swv[0]), (sw1, swv[1])] + gw2[ws].need_w(),
                     inc=wsem, incn=16)
                gw2[ws].did_w(wsem)
                for s2i in range(2):
                    ss = sp * 2 + s2i
                    for nb in range(2):
                        p.op("tensor", "matmul",
                             po[:, s2i * 1024 + nb * 512:s2i * 1024 + (nb + 1) * 512],
                             HBUF[:, kh * 512 + ss * 128:kh * 512 + ss * 128 + 128],
                             W2s[:, ws * DM + nb * 512:ws * DM + (nb + 1) * 512],
                             start=(kh == 0), stop=False,
                             waits=([(wsem, wsem.v), (sc, sc.v)] +
                                    (gpo[s2i].need_w() if (kh == 0 and nb == 0) else []))
                             if (kh == 0 or True) else (), inc=mm)
                gw2[ws].did_r(mm)
            for s2i in range(2):
                ss = sp * 2 + s2i
                p.op("tensor", "matmul", po[:, s2i * 1024:s2i * 1024 + 512],
                     onesr[0:1, 0:128], eb2r[0:1, 0:512],
                     start=False, stop=True, waits=[(ld, ldv)], inc=mm)
                p.op("tensor", "matmul", po[:, s2i * 1024 + 512:s2i * 1024 + 1024],
                     onesr[0:1, 0:128], eb2r[0:1, 512:1024],
                     start=False, stop=True, inc=mm)
                gpo[s2i].did_w(mm)
                p.op("vector", "tensor_copy", OS[:, s2i * DM:(s2i + 1) * DM],
                     po[:, s2i * 1024:(s2i + 1) * 1024],
                     waits=[(mm, mm.v)] + gos[s2i].need_w(), inc=dv)
                p.drain("vector")
                gpo[s2i].did_r(dv)
                gos[s2i].did_w(dv)
                osem = so if s2i == 0 else so1
                p.op("sync", "dma_start",
                     out=eo_o[sb * 512 + ss * 128:sb * 512 + (ss + 1) * 128, :],
                     in_=OS[:, s2i * DM:(s2i + 1) * DM], waits=[(dv, dv.v)],
                     inc=osem, incn=16)
                gos[s2i].did_r(osem)
    for s2 in (so, so1, sw, sw1, ld, lg, lw0, lw1, gg0, gg1):
        p.wait("sync", s2, s2.v)


def expert_io_dram(p):
    NS, H = 4096, 4096
    return {
        "sendall": p.dram("sendall", [8 * NS, DM], BF16),
        "countsA": p.dram("countsA", [8, 8]),
        "onehotE": p.dram("onehotE", [8, 8]),
        "ccol": p.dram("ccol", [8, 1]),
        "ew1": p.dram("ew1", [DM, H]),
        "eb1c": p.dram("eb1c", [128, 32]),
        "ew2": p.dram("ew2", [H, DM]),
        "eb2": p.dram("eb2", [1, DM]),
        "eo": p.dram("eo", [NS, DM], BF16, kind="ExternalOutput"),
    }


def build_moe_expert():
    p = P("moe_expert")
    emit_moe_expert(p, expert_io_dram(p))
    return p.build()


def build_launchB(T=2048):
    p = P("launchB")
    io_f = fin_io_dram(p, T)
    io_e = expert_io_dram(p)
    p.push_scope()
    emit_mamba_fin(p, io_f, T)
    p.pop_scope()
    p.barrier()
    p.push_scope()
    emit_moe_expert(p, io_e)
    p.pop_scope()
    p.barrier()
    return p.build()



def emit_moe_combine(p, io, T=2048):
    N2 = 2 * T
    NTB = T // 128
    eoall = io["eoall"]
    countsA = io["countsA"]
    premask = io["premask"]
    idsf_i = io["idsf"]
    lposf_i = io["lposf"]
    probsf_i = io["probsf"]
    s_i = io["s"]
    x_i = io["x"]
    x1_o = io["x1"]
    xn1_o = io["xn1"]
    xn1T_o = io.get("xn1T")
    cscr = p.dram("cscr", [2, N2], kind="Internal")

    idsf = p.sb("idsf_s", [1, N2])
    lposf = p.sb("lposf_s", [1, N2])
    probsf = p.sb("probsf_s", [1, N2])
    keepr = p.sb("keepr", [1, N2])
    slotr = p.sb("slotr", [1, N2])
    pkr = p.sb("pkr", [1, N2])
    maskT = p.sb("maskTc", [8, N2])
    cA = p.sb("cAc", [8, 8])
    pmc = p.sb("pmc", [8, 1])
    offsrow = p.sb("offsrow", [1, 8])
    offscol = p.sb("offscol", [8, 1])
    iotc8 = p.sb("iotc8c", [8, 1])
    id8 = p.sb("id8c", [8, 8])
    ones8 = p.sb("ones8c", [8, 8])
    onesr = p.sb("onesrc", [1, 128])
    grows = p.sb("growsc", [128, DM], BF16)
    yacc = p.sb("yaccc", [128, DM])
    stile = p.sb("stilec", [128, DM])
    xtile = p.sb("xtilec", [128, DM])
    x1t = p.sb("x1tc", [128, DM])
    sidx = p.sb("sidxc", [128, 1], U32)
    sidxf = p.sb("sidxfc", [128, 1])
    pkc = p.sb("pkcc", [128, 1])
    c1 = p.sb("c1c", [128, 1])
    c2 = p.sb("c2c", [128, 1])
    epscol = p.sb("epscolc", [128, 1])

    identf = p.sb("identfc", [128, 128])
    iotf = p.sb("iotfc", [128, 128])
    tstg = p.sb("tstgc", [128, DM], BF16)
    pp = p.ps("ppc", [128, 1024])
    p8 = p.ps("p8c", [8, 512])

    ld = p.sem("ld")
    dv = p.sem("dv")
    sc = p.sem("sc")
    gp = p.sem("gp")
    mm = p.sem("mm")
    so = p.sem("so")
    scs = p.sem("scs")
    sil = p.sem("sil")
    pkl = p.sem("pkl")
    gsem = p.sem("gsem")

    gpp = Guard()
    gp8 = Guard()
    ggr = Guard()
    gsi = Guard()
    gsf = Guard()
    gpk = Guard()
    gyc = Guard()

    p.op("gpsimd", "memset", onesr[:, :], 1.0, inc=gp)
    p.op("gpsimd", "memset", ones8[:, :], 1.0, inc=gp)
    p.op("gpsimd", "memset", epscol[:, :], EPS, inc=gp)
    p.op("gpsimd", "iota", iotc8[:, :], pattern=[[0, 1]], base=0,
         channel_multiplier=1, allow_small_or_imprecise_dtypes=True, inc=gp)
    p.op("gpsimd", "iota", id8[:, :], pattern=[[1, 8]], base=0,
         channel_multiplier=-1, allow_small_or_imprecise_dtypes=True, inc=gp)
    p.op("vector", "tensor_scalar", id8[:, :], id8[:, :], 0.0, None, OP.is_equal,
         waits=[(gp, gp.v)], inc=dv)
    p.drain("vector")
    p.op("gpsimd", "iota", iotf[:, :], pattern=[[1, 128]], base=0,
         channel_multiplier=-1, allow_small_or_imprecise_dtypes=True, inc=gp)
    p.op("vector", "tensor_scalar", identf[:, :], iotf[:, :], 0.0, None, OP.is_equal,
         waits=[(gp, gp.v)], inc=dv)
    p.drain("vector")

    p.op("sync", "dma_start", out=idsf[:, :], in_=idsf_i, inc=ld, incn=16)
    p.op("sync", "dma_start", out=lposf[:, :], in_=lposf_i, inc=ld, incn=16)
    p.op("sync", "dma_start", out=probsf[:, :], in_=probsf_i, inc=ld, incn=16)
    p.op("sync", "dma_start", out=cA[:, :], in_=countsA[:, :], inc=ld, incn=16)
    p.op("sync", "dma_start", out=pmc[:, :], in_=premask[:, :], inc=ld, incn=16)
    ldv = ld.v

    p.op("tensor", "matmul", pp[0:1, 0:8], pmc[0:8, 0:1], cA[0:8, :],
         start=True, stop=True, waits=gpp.need_w() + [(ld, ldv)], inc=mm)
    gpp.did_w(mm)
    p.op("vector", "tensor_copy", offsrow[:, :], pp[0:1, 0:8], waits=[(mm, mm.v)], inc=dv)
    p.drain("vector")
    gpp.did_r(dv)
    p.op("tensor", "transpose", pp[0:8, 0:1], offsrow[0:1, 0:8], id8[0:1, 0:1],
         waits=gpp.need_w() + [(dv, dv.v)], inc=mm)
    gpp.did_w(mm)
    p.op("vector", "tensor_copy", offscol[:, :], pp[0:8, 0:1], waits=[(mm, mm.v)], inc=dv)
    p.drain("vector")
    gpp.did_r(dv)
    for cb in range(N2 // 512):
        p.op("tensor", "matmul", p8[:, :], ones8[0:1, 0:8],
             idsf[0:1, cb * 512:(cb + 1) * 512], start=True, stop=True,
             waits=gp8.need_w() + [(ld, ldv)], inc=mm)
        gp8.did_w(mm)
        p.op("vector", "tensor_scalar", maskT[:, cb * 512:(cb + 1) * 512],
             p8[0:8, :], iotc8[0:8, 0:1], None, OP.is_equal,
             waits=[(mm, mm.v)], inc=dv)
        p.drain("vector")
        gp8.did_r(dv)
    p.op("vector", "tensor_scalar", maskT[:, :], maskT[:, :], offscol[0:8, 0:1],
         None, OP.mult, inc=dv)
    p.drain("vector")
    for cb in range(N2 // 512):
        p.op("tensor", "matmul", pp[0:1, 0:512], ones8[0:8, 0:1],
             maskT[0:8, cb * 512:(cb + 1) * 512], start=True, stop=True,
             waits=gpp.need_w() + [(dv, dv.v)], inc=mm)
        gpp.did_w(mm)
        p.op("vector", "tensor_copy", keepr[0:1, cb * 512:(cb + 1) * 512],
             pp[0:1, 0:512], waits=[(mm, mm.v)], inc=dv)
        p.drain("vector")
        gpp.did_r(dv)
    p.op("vector", "tensor_tensor", slotr[:, :], lposf[:, :], keepr[:, :], OP.add,
         inc=dv)
    p.drain("vector")
    p.op("vector", "tensor_scalar", keepr[:, :], slotr[:, :], 4095.0, None,
         OP.is_le, inc=dv)
    p.drain("vector")
    p.op("vector", "scalar_tensor_tensor", slotr[:, :], idsf[:, :], 4096.0,
         slotr[:, :], OP.mult, OP.add, inc=dv)
    p.drain("vector")
    p.op("vector", "tensor_tensor", slotr[:, :], slotr[:, :], keepr[:, :],
         OP.mult, inc=dv)
    p.drain("vector")
    p.op("vector", "tensor_tensor", pkr[:, :], probsf[:, :], keepr[:, :], OP.mult,
         inc=dv)
    p.drain("vector")
    p.op("sync", "dma_start", out=cscr[0:1, :], in_=slotr[:, :], waits=[(dv, dv.v)],
         inc=scs, incn=16)
    p.op("sync", "dma_start", out=cscr[1:2, :], in_=pkr[:, :], waits=[(dv, dv.v)],
         inc=scs, incn=16)
    scv = scs.v

    for tb in range(NTB):
        t0 = tb * 128
        w0 = [(so, so.v), (dv, dv.v), (gp, gp.v), (sc, sc.v)] if tb > 0 else []
        p.op("sync", "dma_start", out=stile[:, :], in_=s_i[t0:t0 + 128, :],
             waits=w0, inc=ld, incn=16)
        p.op("sync", "dma_start", out=xtile[:, :], in_=x_i[t0:t0 + 128, :],
             inc=ld, incn=16)
        ldv2 = ld.v
        for k in range(2):
            p.op("sync", "dma_start", out=sidxf[:, :],
                 in_=ap_v_dram(cscr, t0 * 2 + k, [[1, 1], [2, 128]]),
                 allow_slow_non_contiguous=True,
                 waits=[(scs, scv)] + gsf.need_w(), inc=sil, incn=16)
            gsf.did_w(sil)
            p.op("sync", "dma_start", out=pkc[:, :],
                 in_=ap_v_dram(cscr, N2 + t0 * 2 + k, [[1, 1], [2, 128]]),
                 allow_slow_non_contiguous=True,
                 waits=gpk.need_w(), inc=pkl, incn=16)
            gpk.did_w(pkl)
            p.op("vector", "tensor_copy", sidx[:, :], sidxf[:, :],
                 waits=[(sil, sil.v)] + gsi.need_w(), inc=dv)
            p.drain("vector")
            gsf.did_r(dv)
            gsi.did_w(dv)
            p.op("gpsimd", "memset", grows[:, :], 0.0,
                 waits=ggr.need_w(), inc=gp)
            p.op("gpsimd", "indirect_dma_start", out=grows[:, :], out_offset=None,
                 in_=eoall[:, :],
                 in_offset=bass.IndirectOffsetOnAxis(ap=sidx[:, 0:1], axis=0),
                 waits=[(dv, dv.v), (gp, gp.v)], inc=gsem, incn=16)
            gsi.did_r(gsem)
            ggr.did_w(gsem)
            if k == 0:
                p.op("vector", "tensor_scalar", yacc[:, :], grows[:, :],
                     pkc[0:128, 0:1], None, OP.mult,
                     waits=[(gsem, gsem.v), (pkl, pkl.v)], inc=dv)
            else:
                p.op("vector", "scalar_tensor_tensor", yacc[:, :], grows[:, :],
                     pkc[0:128, 0:1], yacc[:, :], OP.mult, OP.add,
                     waits=[(gsem, gsem.v), (pkl, pkl.v)], inc=dv)
            p.drain("vector")
            ggr.did_r(dv)
            gpk.did_r(dv)
        p.op("scalar", "activation", yacc[:, :], yacc[:, :], AF.Silu,
             waits=[(dv, dv.v)], inc=sc)
        p.drain("scalar")
        p.op("vector", "tensor_tensor", stile[:, :], stile[:, :], yacc[:, :],
             OP.mult, waits=[(sc, sc.v), (ld, ldv2)], inc=dv)
        p.drain("vector")
        p.op("vector", "tensor_tensor", x1t[:, :], xtile[:, :], stile[:, :],
             OP.add, inc=dv)
        p.drain("vector")
        gyc.did_w(dv)
        p.op("sync", "dma_start", out=x1_o[t0:t0 + 128, :], in_=x1t[:, :],
             waits=[(dv, dv.v)], inc=so, incn=16)
        p.op("vector", "tensor_copy", xtile[:, :], x1t[:, :], inc=dv)
        p.drain("vector")
        emit_ln_tile(p, xtile[:, :], 128, DM, xtile[:, :], stile[:, :], c1, c2,
                     dv, sc, epscol)
        p.op("sync", "dma_start", out=xn1_o[t0:t0 + 128, :], in_=xtile[:, :],
             waits=[(dv, dv.v)], inc=so, incn=16)
        if xn1T_o is not None:
            gtx = Guard()
            for kb in range(8):
                reg = (kb % 2) * 512
                p.op("tensor", "transpose", pp[:, reg:reg + 128],
                     xtile[:, kb * 128:(kb + 1) * 128], identf[:, :],
                     waits=(gpp.need_w() + [(dv, dv.v)]) if kb < 2 else gtx.need_w(),
                     inc=mm)
                gtx.did_w(mm)
                if kb >= 6:
                    gpp.did_w(mm)
                p.op("vector", "tensor_copy", tstg[:, kb * 128:(kb + 1) * 128],
                     pp[:, reg:reg + 128], waits=[(mm, mm.v)], inc=dv)
                p.drain("vector")
                gtx.did_r(dv)
                if kb >= 6:
                    gpp.did_r(dv)
                p.op("sync", "dma_start",
                     out=xn1T_o[kb * 128:(kb + 1) * 128, t0:t0 + 128],
                     in_=tstg[:, kb * 128:(kb + 1) * 128], waits=[(dv, dv.v)],
                     inc=so, incn=16)
        p.wait("vector", so, so.v)
        p.wait("gpsimd", so, so.v)
    for s2 in (so, scs, sil, pkl, ld, gsem):
        p.wait("sync", s2, s2.v)


def combine_io_dram(p, T=2048, with_xn1T=False, internal_xn1=False):
    N2 = 2 * T
    knd = "Internal" if internal_xn1 else "ExternalOutput"
    io = {
        "eoall": p.dram("eoall", [8 * 4096, DM], BF16),
        "countsA": p.dram("countsA", [8, 8]),
        "premask": p.dram("premask", [8, 1]),
        "idsf": p.dram("idsf", [1, N2]),
        "lposf": p.dram("lposf", [1, N2]),
        "probsf": p.dram("probsf", [1, N2]),
        "s": p.dram("s", [T, DM]),
        "x": p.dram("x", [T, DM]),
        "x1": p.dram("x1", [T, DM], kind="ExternalOutput"),
        "xn1": p.dram("xn1", [T, DM], kind=knd),
    }
    if with_xn1T:
        io["xn1T"] = p.dram("xn1T", [DM, T], BF16, kind="Internal")
    return io


def build_moe_combine(T=2048):
    p = P("moe_combine")
    io = combine_io_dram(p, T)
    emit_moe_combine(p, io, T)
    return p.build()



def emit_l1_local(p, io, T=2048):
    NCH = T // CH
    xn1T = io["xn1T"]
    xn1 = io["xn1"]
    wg = io["wg"]
    bg = io["bg"]
    H_o = io["H"]
    clr_o = io["clr"]
    hlast_o = io["hlast"]

    XT = p.sb("XTg", [128, 8 * T], BF16)
    XM = p.sb("XMg", [128, 2 * DM], BF16)
    WG = p.sb("WGg", [128, 8], BF16)
    graw = p.sb("graw", [1, T])
    lgr = p.sb("lgr", [1, T])
    lggr = p.sb("lggr", [1, T])
    clr = p.sb("clrg", [1, T])
    brow = p.sb("browg", [1, T])
    scr = p.sb("scrg", [1, T])
    onesT = p.sb("onesTg", [1, T])
    hcar = p.sb("hcarg", [1, DM])
    expc = p.sb("expcg", [1, CH])
    ncol = p.sb("ncolg", [1, 1])
    bgc = p.sb("bgcg", [1, 1])
    Etile = p.sb("Etileg", [128, CH], BF16)
    Htile = p.sb("Htileg", [128, DM])
    identb = p.sb("identbg", [128, 128], BF16)
    vneg = p.sb("vnegg", [128, 128], BF16)
    iot = p.sb("iotg", [128, 128])
    onesr = p.sb("onesrg", [1, 128])

    pg = p.ps("pgg", [128, 512])
    phh = p.ps("phhg", [128, 1024])
    pcar = p.ps("pcarg", [1, 1024])

    ld = p.sem("ld")
    lg_ = p.sem("lg")
    lx0 = p.sem("lx0")
    lx1 = p.sem("lx1")
    dv = p.sem("dv")
    sc = p.sem("sc")
    gp = p.sem("gp")
    mm = p.sem("mm")
    so = p.sem("so")

    gpg = Guard()
    gph = Guard()
    gpc = Guard()
    gxm = [Guard(), Guard()]
    ghc = Guard()
    gE = Guard()
    gHt = Guard()

    p.op("gpsimd", "memset", onesr[:, :], 1.0, inc=gp)
    p.op("gpsimd", "memset", onesT[:, :], 1.0, inc=gp)
    p.op("gpsimd", "memset", hcar[:, :], 0.0, inc=gp)
    p.op("gpsimd", "iota", iot[:, :], pattern=[[1, 128]], base=0,
         channel_multiplier=-1, allow_small_or_imprecise_dtypes=True, inc=gp)
    p.op("vector", "tensor_scalar", identb[:, :], iot[:, :], 0.0, None, OP.is_equal,
         waits=[(gp, gp.v)], inc=dv)
    p.drain("vector")
    p.op("vector", "tensor_scalar", vneg[:, :], iot[:, :], 0.0, None, OP.is_lt, inc=dv)
    p.drain("vector")
    p.op("vector", "tensor_scalar", vneg[:, :], vneg[:, :], float(NEG), None, OP.mult,
         inc=dv)
    p.drain("vector")

    p.op("gpsimd", "dma_start", out=XT[:, :].rearrange("p (k t) -> p k t", k=8),
         in_=xn1T.rearrange("(k p) t -> p k t", p=128), inc=lg_, incn=16)
    p.op("gpsimd", "dma_start", out=WG[:, :].rearrange("p (k o) -> p k o", k=8),
         in_=wg.rearrange("(k p) o -> p k o", p=128),
         allow_slow_non_contiguous=True, inc=lg_, incn=16)
    p.op("sync", "dma_start", out=bgc[:, :], in_=bg[:, :], inc=ld, incn=16)
    ldv = ld.v

    NT5 = max(1, T // 512)
    T5 = min(512, T)
    for t5 in range(NT5):
        for kb in range(8):
            p.op("tensor", "matmul", pg[0:1, 0:T5],
                 ap_v(WG, kb, [[8, 128], [1, 1]]),
                 ap_v(XT, kb * T + t5 * T5, [[8 * T, 128], [1, T5]]),
                 start=(kb == 0), stop=(kb == 7),
                 waits=(gpg.need_w() + [(lg_, lg_.v)]) if kb == 0 else (), inc=mm)
        gpg.did_w(mm)
        p.op("vector", "tensor_scalar", graw[0:1, t5 * T5:(t5 + 1) * T5],
             pg[0:1, 0:T5], bgc[0:1, 0:1], None, OP.add,
             waits=[(mm, mm.v), (ld, ldv)], inc=dv)
        p.drain("vector")
        gpg.did_r(dv)
    p.op("scalar", "activation", scr[:, :], graw[:, :], AF.Abs, waits=[(dv, dv.v)], inc=sc)
    p.drain("scalar")
    p.op("scalar", "activation", scr[:, :], scr[:, :], AF.Exp, scale=-1.0, inc=sc)
    p.drain("scalar")
    p.op("scalar", "activation", scr[:, :], scr[:, :], AF.Ln, bias=onesT[0:1, 0:1], inc=sc)
    p.drain("scalar")
    p.op("vector", "tensor_scalar", lgr[:, :], graw[:, :], 0.0, None, OP.max,
         waits=[(sc, sc.v)], inc=dv)
    p.drain("vector")
    p.op("vector", "tensor_tensor", lgr[:, :], lgr[:, :], scr[:, :], OP.add, inc=dv)
    p.drain("vector")
    p.op("vector", "tensor_scalar", lgr[:, :], lgr[:, :], -1.0, None, OP.mult, inc=dv)
    p.drain("vector")
    p.op("vector", "tensor_scalar", lggr[:, :], graw[:, :], -1.0, None, OP.mult, inc=dv)
    p.drain("vector")
    p.op("vector", "tensor_scalar", lggr[:, :], lggr[:, :], 0.0, None, OP.max, inc=dv)
    p.drain("vector")
    p.op("vector", "tensor_tensor", lggr[:, :], lggr[:, :], scr[:, :], OP.add, inc=dv)
    p.drain("vector")
    p.op("vector", "tensor_scalar", lggr[:, :], lggr[:, :], -1.0, None, OP.mult, inc=dv)
    p.drain("vector")
    p.op("vector", "tensor_tensor_scan", clr[:, :], onesT[:, :], lgr[:, :], 0.0,
         OP.mult, OP.add, inc=dv)
    p.drain("vector")
    p.op("vector", "tensor_tensor", brow[:, :], lggr[:, :], clr[:, :], OP.subtract,
         inc=dv)
    p.drain("vector")
    p.op("sync", "dma_start", out=clr_o[:, :], in_=clr[:, :], waits=[(dv, dv.v)],
         inc=so, incn=16)
    cldone = dv.v

    for c in range(NCH):
        s0 = c * CH
        xi = c % 2
        lx = lx0 if xi == 0 else lx1
        p.op("gpsimd", "dma_start", out=XM[:, xi * DM:(xi + 1) * DM],
             in_=xn1[s0:s0 + CH, :], waits=gxm[xi].need_w(), inc=lx, incn=16)
        gxm[xi].did_w(lx)
        p.op("tensor", "matmul", pg[:, 0:CH], onesr[0:1, 0:128],
             clr[0:1, s0:s0 + CH], start=True, stop=False,
             waits=gpg.need_w() + [(dv, cldone)], inc=mm)
        p.op("tensor", "matmul", pg[:, 0:CH], brow[0:1, s0:s0 + CH],
             onesr[0:1, 0:128], start=False, stop=False, inc=mm)
        p.op("tensor", "matmul", pg[:, 0:CH], identb[:, :], vneg[:, :],
             start=False, stop=True, inc=mm)
        gpg.did_w(mm)
        if c == 0:
            p.op("gpsimd", "memset", ncol[:, :], 0.0, waits=[(sc, sc.v)], inc=gp)
        else:
            p.op("vector", "tensor_scalar", ncol[:, :], clr[0:1, s0 - 1:s0], -1.0,
                 None, OP.mult, waits=gE.need_r() if False else [], inc=dv)
            p.drain("vector")
        p.op("scalar", "activation", Etile[:, :], pg[:, 0:CH], AF.Exp,
             waits=[(mm, mm.v), (gp, gp.v), (dv, dv.v)], inc=sc)
        p.drain("scalar")
        gpg.did_r(sc)
        gE.did_w(sc)
        for nb in range(2):
            p.op("tensor", "matmul", phh[:, nb * 512:(nb + 1) * 512],
                 Etile[:, :], XM[:, xi * DM + nb * 512:xi * DM + (nb + 1) * 512],
                 start=True, stop=False,
                 waits=(gph.need_w() + [(sc, sc.v), (lx, lx.v)]) if nb == 0 else (),
                 inc=mm)
        p.op("scalar", "activation", expc[:, :], clr[0:1, s0:s0 + CH], AF.Exp,
             bias=ncol[0:1, 0:1], waits=[(gp, gp.v), (dv, dv.v)], inc=sc)
        p.drain("scalar")
        gE.did_w(sc)
        for nb in range(2):
            p.op("tensor", "matmul", phh[:, nb * 512:(nb + 1) * 512],
                 expc[0:1, :], hcar[0:1, nb * 512:(nb + 1) * 512],
                 start=False, stop=True,
                 waits=[(sc, sc.v)] + ghc.need_w() if nb == 0 else (), inc=mm)
        gph.did_w(mm)
        ghc.did_r(mm)
        p.op("vector", "tensor_copy", Htile[:, :], phh[:, 0:1024],
             waits=[(mm, mm.v)] + gHt.need_w(), inc=dv)
        p.drain("vector")
        gph.did_r(dv)
        gHt.did_w(dv)
        p.op("sync", "dma_start", out=H_o[s0:s0 + CH, :], in_=Htile[:, :],
             waits=[(dv, dv.v)], inc=so, incn=16)
        gHt.did_r(so)
        for nb in range(2):
            p.op("tensor", "matmul", pcar[0:1, nb * 512:(nb + 1) * 512],
                 Etile[:, CH - 1:CH],
                 XM[:, xi * DM + nb * 512:xi * DM + (nb + 1) * 512],
                 start=True, stop=False,
                 waits=(gpc.need_w() + [(sc, sc.v), (lx, lx.v)]) if nb == 0 else (),
                 inc=mm)
            p.op("tensor", "matmul", pcar[0:1, nb * 512:(nb + 1) * 512],
                 expc[0:1, CH - 1:CH], hcar[0:1, nb * 512:(nb + 1) * 512],
                 start=False, stop=True, inc=mm)
        gpc.did_w(mm)
        gxm[xi].did_r(mm)
        p.op("vector", "tensor_copy", hcar[:, :], pcar[0:1, 0:1024],
             waits=[(mm, mm.v)] + ghc.need_w() + ghc.need_r(), inc=dv)
        p.drain("vector")
        gpc.did_r(dv)
        ghc.did_w(dv)
    p.op("vector", "tensor_copy", Htile[0:1, :], hcar[:, :],
         waits=gHt.need_w() + [(so, so.v)], inc=dv)
    p.drain("vector")
    p.op("sync", "dma_start", out=hlast_o[:, :], in_=Htile[0:1, 0:DM],
         waits=[(dv, dv.v)], inc=so, incn=16)
    for s2 in (so, ld, lg_, lx0, lx1):
        p.wait("sync", s2, s2.v)


def build_l1_local(T=2048):
    p = P("l1_local")
    io = {
        "xn1T": p.dram("xn1T", [DM, T]),
        "xn1": p.dram("xn1", [T, DM]),
        "wg": p.dram("wg", [DM, 1]),
        "bg": p.dram("bg", [1, 1]),
        "H": p.dram("H", [T, DM], kind="ExternalOutput"),
        "clr": p.dram("clr", [1, T], kind="ExternalOutput"),
        "hlast": p.dram("hlast", [1, DM], kind="ExternalOutput"),
    }
    emit_l1_local(p, io, T)
    return p.build()



def emit_pkm(p, io, T=2048):
    NTB = T // 128
    xn1T = io["xn1T"]
    xn1 = io["xn1"]
    wqT = io["wqT"]
    keysr = io["keysr"]
    winT = io["winT"]
    woutT = io["woutT"]
    in_a = io["in_a"]
    in_b = io["in_b"]
    out_a = io["out_a"]
    out_b = io["out_b"]
    pkm_o = io["pkm"]

    WQ = p.sb("WQp", [128, 2 * 4096], BF16)
    XT = p.sb("XTp", [128, 8 * T], BF16)
    XM = p.sb("XMp", [128, DM], BF16)
    KY = p.sb("KYp", [128, 2 * 4 * 128], BF16)
    QB = p.sb("QBp", [128, 4 * 128], BF16)
    SIMS = p.sb("SIMSp", [128, 2 * 1024])
    KX = p.sb("KXp", [128, 8])
    SX = p.sb("SXp", [128, 8])
    KYc = p.sb("KYcp", [128, 8])
    SY = p.sb("SYp", [128, 8])
    GT = p.sb("GTp", [128, 8])
    EXI = p.sb("EXIp", [128, 8], U32)
    EXF = p.sb("EXFp", [128, 8])
    mxc = p.sb("mxcp", [128, 1])
    msks = p.sb("msksp", [128, 1024])
    iotsg = p.sb("iotsgp", [128, 1024])
    AR = p.sb("ARp", [128, DI], BF16)
    BR = p.sb("BRp", [128, DI], BF16)
    lin = p.sb("linp", [128, 8])
    hid = p.sb("hidp", [128, DI])
    hidb = p.sb("hidbp", [128, DI], BF16)
    hT = p.sb("hTp", [128, 16 * 128], BF16)
    outt = p.sb("outtp", [128, DM])
    scr2 = p.sb("scr2p", [128, DI])
    WIN = p.sb("WINp", [128, 8 * DI], BF16)
    WOU = p.sb("WOUp", [128, 16 * DM], BF16)
    identb = p.sb("identbp", [128, 128], BF16)
    iotp = p.sb("iotpp", [128, 128])

    psim = p.ps("psimp", [128, 1024])
    pq = p.ps("pqp", [128, 1024])
    pwo = p.ps("pwop", [128, 1024])
    ptp = p.ps("ptpp", [128, 1024], BF16)
    ptp2 = p.ps("ptp2p", [128, 1024], BF16)

    ld = p.sem("ld")
    lg_ = p.sem("lg")
    lk0 = p.sem("lk0")
    lk1 = p.sem("lk1")
    lwq = p.sem("lwq")
    lxm = p.sem("lxm")
    ga0 = p.sem("ga0")
    gb0 = p.sem("gb0")
    dv = p.sem("dv")
    sc = p.sem("sc")
    gp = p.sem("gp")
    mm = p.sem("mm")
    so = p.sem("so")

    gsim = [Guard(), Guard()]
    gq = [Guard(), Guard()]
    gwo = [Guard(), Guard()]
    gtp = [Guard(), Guard()]
    gky = [Guard(), Guard()]
    gar = Guard()
    gbr = Guard()
    ghid = Guard()
    gout = Guard()
    gxm = Guard()
    gwq = Guard()

    p.op("gpsimd", "iota", iotp[:, :], pattern=[[1, 128]], base=0,
         channel_multiplier=-1, allow_small_or_imprecise_dtypes=True, inc=gp)
    p.op("gpsimd", "iota", iotsg[:, :], pattern=[[0, 8], [1, 128]], base=1000000,
         channel_multiplier=0, allow_small_or_imprecise_dtypes=True, inc=gp)
    p.op("vector", "tensor_scalar", identb[:, :], iotp[:, :], 0.0, None, OP.is_equal,
         waits=[(gp, gp.v)], inc=dv)
    p.drain("vector")

    p.op("gpsimd", "dma_start", out=XT[:, :].rearrange("p (k t) -> p k t", k=8),
         in_=xn1T.rearrange("(k p) t -> p k t", p=128), inc=lg_, incn=16)
    p.op("gpsimd", "dma_start", out=WIN[:, :].rearrange("p (k o) -> p k o", k=8),
         in_=winT.rearrange("(k p) o -> p k o", p=128), inc=lg_, incn=16)
    p.op("gpsimd", "dma_start", out=WOU[:, :].rearrange("p (k o) -> p k o", k=16),
         in_=woutT.rearrange("(k p) o -> p k o", p=128), inc=lg_, incn=16)
    lgv = lg_.v

    for tb in range(NTB):
        t0 = tb * 128
        p.op("gpsimd", "dma_start", out=XM[:, :], in_=xn1[t0:t0 + 128, :],
             waits=gxm.need_w(), inc=lxm, incn=16)
        gxm.did_w(lxm)
        for ph in range(2):
            si = ph % 2
            for h in range(8):
                qs = h % 2
                p.op("gpsimd", "dma_start",
                     out=WQ[:, qs * 4096:qs * 4096 + 8 * 512].rearrange(
                         "p (k o) -> p k o", k=8),
                     in_=wqT[:, ph * 4096 + h * 512:ph * 4096 + (h + 1) * 512].rearrange(
                         "(k p) o -> p k o", p=128),
                     waits=gwq.need_w() if True else [], inc=lwq, incn=16)
                gwq.did_w(lwq)
                for qb in range(4):
                    ri = qb % 2
                    for kb in range(8):
                        p.op("tensor", "matmul", pq[:, ri * 512:ri * 512 + 128],
                             ap_v(WQ, qs * 4096 + kb * 512 + qb * 128,
                                  [[2 * 4096, 128], [1, 128]]),
                             ap_v(XT, kb * T + t0, [[8 * T, 128], [1, 128]]),
                             start=(kb == 0), stop=(kb == 7),
                             waits=(gq[ri].need_w() + [(lwq, lwq.v), (lg_, lgv)])
                             if kb == 0 else (), inc=mm)
                    gq[ri].did_w(mm)
                    p.op("vector", "tensor_copy", QB[:, qb * 128:(qb + 1) * 128],
                         pq[:, ri * 512:ri * 512 + 128], waits=[(mm, mm.v)], inc=dv)
                    p.drain("vector")
                    gq[ri].did_r(dv)
                ks = h % 2
                ksem = lk0 if ks == 0 else lk1
                p.op("gpsimd", "dma_start",
                     out=KY[:, ks * 512:(ks + 1) * 512].rearrange("p (k o) -> p k o", k=4),
                     in_=keysr[(ph * 8 + h) * 512:(ph * 8 + h + 1) * 512, :].rearrange(
                         "(k p) o -> p k o", p=128),
                     waits=gky[ks].need_w(), inc=ksem, incn=16)
                gky[ks].did_w(ksem)
                for qb in range(4):
                    p.op("tensor", "matmul",
                         psim[:, si * 512 + (h % 4) * 128:si * 512 + (h % 4) * 128 + 128]
                         if False else psim[:, (h // 4) * 512 + (h % 4) * 128:
                                            (h // 4) * 512 + (h % 4) * 128 + 128],
                         QB[:, qb * 128:(qb + 1) * 128],
                         KY[:, ks * 512 + qb * 128:ks * 512 + (qb + 1) * 128],
                         start=(qb == 0 and h % 4 == 0) if False else (qb == 0),
                         stop=(qb == 3),
                         waits=(gsim[h // 4].need_w() + [(ksem, ksem.v), (dv, dv.v)])
                         if qb == 0 else (), inc=mm)
                gsim[h // 4].did_w(mm)
                gky[ks].did_r(mm)
                gwq.did_r(mm)
            p.op("vector", "tensor_copy", SIMS[:, ph * 1024:(ph + 1) * 1024],
                 psim[:, 0:1024], waits=[(mm, mm.v)], inc=dv)
            p.drain("vector")
            gsim[0].did_r(dv)
            gsim[1].did_r(dv)
            sv = SIMS[:, ph * 1024:(ph + 1) * 1024].rearrange("p (h k) -> p h k", h=8)
            mxd = SX if ph == 0 else SY
            kxd = KX if ph == 0 else KYc
            p.op("vector", "tensor_reduce", mxd[:, :], sv, AX.X, OP.max, inc=dv)
            p.drain("vector")
            p.op("vector", "tensor_tensor", msks[:, :].rearrange("p (h k) -> p h k", h=8),
                 sv, ap_v(mxd, 0, [[8, 128], [1, 8], [0, 128]]), OP.is_ge, inc=dv)
            p.drain("vector")
            p.op("vector", "scalar_tensor_tensor", msks[:, :], msks[:, :],
                 -1000000.0, iotsg[:, :].rearrange("p f -> p f") if False else iotsg[:, :],
                 OP.mult, OP.add, inc=dv)
            p.drain("vector")
            p.op("vector", "tensor_reduce", kxd[:, :],
                 msks[:, :].rearrange("p (h k) -> p h k", h=8), AX.X, OP.min, inc=dv)
            p.drain("vector")
        p.op("vector", "scalar_tensor_tensor", EXF[:, :], KX[:, :], 128.0, KYc[:, :],
             OP.mult, OP.add, inc=dv)
        p.drain("vector")
        p.op("vector", "tensor_tensor", GT[:, :], SX[:, :], SY[:, :], OP.add, inc=dv)
        p.drain("vector")
        p.op("vector", "tensor_scalar", GT[:, :], GT[:, :], 0.0, None, OP.max, inc=dv)
        p.drain("vector")
        p.op("vector", "tensor_copy", EXI[:, :], EXF[:, :], inc=dv)
        p.drain("vector")
        exdone = dv.v

        for ob in range(4):
            ri = ob % 2
            for kb in range(8):
                p.op("tensor", "matmul", pwo[:, ri * 512:(ri + 1) * 512],
                     ap_v(XT, kb * T + t0, [[8 * T, 128], [1, 128]]),
                     ap_v(WIN, kb * DI + ob * 512, [[8 * DI, 128], [1, 512]]),
                     start=(kb == 0), stop=(kb == 7),
                     waits=(gwo[ri].need_w() + [(lg_, lgv)]) if kb == 0 else (), inc=mm)
            gwo[ri].did_w(mm)
            p.op("vector", "tensor_copy", hid[:, ob * 512:(ob + 1) * 512],
                 pwo[:, ri * 512:(ri + 1) * 512],
                 waits=[(mm, mm.v)] + (ghid.need_w() if ob == 0 else []), inc=dv)
            p.drain("vector")
            gwo[ri].did_r(dv)
        ghid.did_w(dv)
        for h in range(8):
            p.op("gpsimd", "indirect_dma_start", out=AR[:, 0:DM], out_offset=None,
                 in_=in_a[:, :],
                 in_offset=bass.IndirectOffsetOnAxis(ap=EXI[:, h:h + 1], axis=0),
                 waits=gar.need_w() + [(dv, exdone)], inc=ga0, incn=16)
            gar.did_w(ga0)
            p.op("gpsimd", "indirect_dma_start", out=BR[:, :], out_offset=None,
                 in_=in_b[:, :],
                 in_offset=bass.IndirectOffsetOnAxis(ap=EXI[:, h:h + 1], axis=0),
                 waits=gbr.need_w(), inc=gb0, incn=16)
            gbr.did_w(gb0)
            p.op("gpsimd", "tensor_tensor", scr2[:, 0:DM], XM[:, :], AR[:, 0:DM],
                 OP.mult, waits=[(ga0, ga0.v), (lxm, lxm.v), (dv, dv.v)], inc=gp)
            p.drain("gpsimd")
            gar.did_r(gp)
            p.op("vector", "tensor_reduce", lin[:, h:h + 1], scr2[:, 0:DM], AX.X,
                 OP.add, waits=[(gp, gp.v)], inc=dv)
            p.drain("vector")
            p.op("vector", "tensor_tensor", lin[:, h:h + 1], lin[:, h:h + 1],
                 GT[:, h:h + 1], OP.mult, inc=dv)
            p.drain("vector")
            p.op("vector", "scalar_tensor_tensor", hid[:, :], BR[:, :],
                 lin[0:128, h:h + 1], hid[:, :], OP.mult, OP.add,
                 waits=[(gb0, gb0.v)], inc=dv)
            p.drain("vector")
            gbr.did_r(dv)
        p.op("scalar", "activation", hidb[:, :], hid[:, :], AF.Gelu,
             waits=[(dv, dv.v)], inc=sc)
        p.drain("scalar")
        ghid.did_w(sc)
        for kb in range(16):
            dstp = ptp if kb % 2 == 0 else ptp2
            gtj = gtp[kb % 2]
            p.op("tensor", "transpose", dstp[:, 0:128],
                 hidb[:, kb * 128:(kb + 1) * 128], identb[:, :],
                 waits=gtj.need_w() + [(sc, sc.v), (dv, dv.v)], inc=mm)
            gtj.did_w(mm)
            p.op("vector", "tensor_copy", hT[:, kb * 128:(kb + 1) * 128],
                 dstp[:, 0:128], waits=[(mm, mm.v)], inc=dv)
            p.drain("vector")
            gtj.did_r(dv)
        for ob in range(2):
            ri = ob % 2
            for kb in range(16):
                p.op("tensor", "matmul", pwo[:, ri * 512:(ri + 1) * 512],
                     hT[:, kb * 128:(kb + 1) * 128],
                     ap_v(WOU, kb * DM + ob * 512, [[16 * DM, 128], [1, 512]]),
                     start=(kb == 0), stop=(kb == 15),
                     waits=(gwo[ri].need_w() + [(dv, dv.v), (lg_, lgv)])
                     if kb == 0 else (), inc=mm)
            gwo[ri].did_w(mm)
            p.op("vector", "tensor_copy", outt[:, ob * 512:(ob + 1) * 512],
                 pwo[:, ri * 512:(ri + 1) * 512],
                 waits=[(mm, mm.v)] + (gout.need_w() if ob == 0 else []), inc=dv)
            p.drain("vector")
            gwo[ri].did_r(dv)
        gout.did_w(dv)
        for h in range(8):
            p.op("gpsimd", "indirect_dma_start", out=AR[:, :], out_offset=None,
                 in_=out_a[:, :],
                 in_offset=bass.IndirectOffsetOnAxis(ap=EXI[:, h:h + 1], axis=0),
                 waits=gar.need_w() + [(dv, dv.v)], inc=ga0, incn=16)
            gar.did_w(ga0)
            p.op("gpsimd", "indirect_dma_start", out=BR[:, 0:DM], out_offset=None,
                 in_=out_b[:, :],
                 in_offset=bass.IndirectOffsetOnAxis(ap=EXI[:, h:h + 1], axis=0),
                 waits=gbr.need_w(), inc=gb0, incn=16)
            gbr.did_w(gb0)
            p.op("vector", "tensor_tensor", scr2[:, :], hidb[:, :], AR[:, :],
                 OP.mult, waits=[(ga0, ga0.v), (sc, sc.v)], inc=dv)
            p.drain("vector")
            gar.did_r(dv)
            p.op("vector", "tensor_reduce", lin[:, h:h + 1], scr2[:, :], AX.X,
                 OP.add, inc=dv)
            p.drain("vector")
            p.op("vector", "tensor_tensor", lin[:, h:h + 1], lin[:, h:h + 1],
                 GT[:, h:h + 1], OP.mult, inc=dv)
            p.drain("vector")
            p.op("vector", "scalar_tensor_tensor", outt[:, :], BR[:, 0:DM],
                 lin[0:128, h:h + 1], outt[:, :], OP.mult, OP.add,
                 waits=[(gb0, gb0.v)], inc=dv)
            p.drain("vector")
            gbr.did_r(dv)
        gout.did_w(dv)
        p.op("sync", "dma_start", out=pkm_o[t0:t0 + 128, :], in_=outt[:, :],
             waits=[(dv, dv.v)], inc=so, incn=16)
        gout.did_r(so)
        gxm.did_r(dv)
        ghid.did_r(dv)
    for s2 in (so, ld, lg_, lk0, lk1, lwq, lxm, ga0, gb0):
        p.wait("sync", s2, s2.v)


def emit_pkm2(p, io, T=2048):
    NTB = T // 128
    xn1T = io["xn1T"]
    xn1 = io["xn1"]
    wqT = io["wqT"]
    keysr = io["keysr"]
    winT = io["winT"]
    woutT = io["woutT"]
    in_a = io["in_a"]
    in_b = io["in_b"]
    out_a = io["out_a"]
    out_b = io["out_b"]
    pkm_o = io["pkm"]

    XT = p.sb("XT2", [128, 8 * T], BF16)
    WQ = p.sb("WQ2", [128, 2 * 8 * 512], BF16)
    KY = p.sb("KY2", [128, 2 * 4 * 128], BF16)
    QF = p.sb("QF2", [128, 4 * T], BF16)
    SIMT = p.sb("SIMT2", [128, 2 * 128])
    SXa = p.sb("SXa2", [128, 16 * NTB])
    KXa = p.sb("KXa2", [128, 16 * NTB])
    msks = p.sb("msks2", [128, 128])
    iotk = p.sb("iotk2", [128, 128])
    mxc = p.sb("mxc2", [128, 1])
    XM = p.sb("XM2", [128, DM], BF16)
    GT = p.sb("GT2", [128, 8])
    EXI = p.sb("EXI2", [128, 8], U32)
    EXF = p.sb("EXF2", [128, 8])
    AR = p.sb("AR2", [128, DI], BF16)
    BR = p.sb("BR2", [128, DI], BF16)
    lin = p.sb("lin2", [128, 8])
    hid = p.sb("hid2", [128, DI])
    hidb = p.sb("hidb2", [128, DI], BF16)
    hT = p.sb("hT2", [128, 16 * 128], BF16)
    outt = p.sb("outt2", [128, DM])
    scr2 = p.sb("scr22", [128, DI])
    WIN = p.sb("WIN2", [128, 8 * DI], BF16)
    WOU = p.sb("WOU2", [128, 16 * DM], BF16)
    identb = p.sb("identb2p", [128, 128], BF16)
    iotp = p.sb("iotp2", [128, 128])

    pq = p.ps("pq2", [128, 1024])
    psim = p.ps("psim2", [128, 1024])
    pwo = p.ps("pwo2", [128, 1024])
    ptp = p.ps("ptp2a", [128, 1024], BF16)
    ptp2 = p.ps("ptp2b", [128, 1024], BF16)

    ld = p.sem("ld")
    lg_ = p.sem("lg")
    lk0 = p.sem("lk0")
    lk1 = p.sem("lk1")
    lwq0 = p.sem("lwq0")
    lwq1 = p.sem("lwq1")
    lxm = p.sem("lxm")
    ga0 = p.sem("ga0")
    gb0 = p.sem("gb0")
    dv = p.sem("dv")
    sc = p.sem("sc")
    gp = p.sem("gp")
    mm = p.sem("mm")
    so = p.sem("so")

    gq = [Guard(), Guard()]
    gsim = [Guard(), Guard()]
    gsimt = [Guard(), Guard()]
    gwq = [Guard(), Guard()]
    gky = [Guard(), Guard()]
    gqf = Guard()
    gwo = [Guard(), Guard()]
    gtp = [Guard(), Guard()]
    gar = Guard()
    gbr = Guard()
    ghid = Guard()
    gout = Guard()
    gxm = Guard()

    p.op("gpsimd", "iota", iotp[:, :], pattern=[[1, 128]], base=0,
         channel_multiplier=-1, allow_small_or_imprecise_dtypes=True, inc=gp)
    p.op("gpsimd", "iota", iotk[:, :], pattern=[[1, 128]], base=1000000,
         channel_multiplier=0, allow_small_or_imprecise_dtypes=True, inc=gp)
    p.op("vector", "tensor_scalar", identb[:, :], iotp[:, :], 0.0, None, OP.is_equal,
         waits=[(gp, gp.v)], inc=dv)
    p.drain("vector")

    p.op("gpsimd", "dma_start", out=XT[:, :].rearrange("p (k t) -> p k t", k=8),
         in_=xn1T.rearrange("(k p) t -> p k t", p=128), inc=lg_, incn=16)
    p.op("gpsimd", "dma_start", out=WIN[:, :].rearrange("p (k o) -> p k o", k=8),
         in_=winT.rearrange("(k p) o -> p k o", p=128), inc=lg_, incn=16)
    p.op("gpsimd", "dma_start", out=WOU[:, :].rearrange("p (k o) -> p k o", k=16),
         in_=woutT.rearrange("(k p) o -> p k o", p=128), inc=lg_, incn=16)
    lgv = lg_.v

    for phh in range(16):
        ph, h = phh // 8, phh % 8
        s = phh % 2
        wsem = lwq0 if s == 0 else lwq1
        c0 = ph * 4096 + h * 512
        p.op("gpsimd", "dma_start",
             out=WQ[:, s * 4096:(s + 1) * 4096].rearrange("p (k o) -> p k o", k=8),
             in_=wqT[:, c0:c0 + 512].rearrange("(k p) o -> p k o", p=128),
             waits=gwq[s].need_w(), inc=wsem, incn=16)
        gwq[s].did_w(wsem)
        ksem = lk0 if s == 0 else lk1
        p.op("gpsimd", "dma_start",
             out=KY[:, s * 512:(s + 1) * 512].rearrange("p (k o) -> p k o", k=4),
             in_=keysr[phh * 512:(phh + 1) * 512, :].rearrange("(k p) o -> p k o", p=128),
             waits=gky[s].need_w(), inc=ksem, incn=16)
        gky[s].did_w(ksem)
        for md in range(4):
            for t5 in range(T // 512):
                ri = (md * (T // 512) + t5) % 2
                reg = ri * 512
                for kb in range(8):
                    p.op("tensor", "matmul", pq[:, reg:reg + 512],
                         ap_v(WQ, s * 4096 + kb * 512 + md * 128,
                              [[2 * 8 * 512, 128], [1, 128]]),
                         ap_v(XT, kb * T + t5 * 512, [[8 * T, 128], [1, 512]]),
                         start=(kb == 0), stop=(kb == 7),
                         waits=(gq[ri].need_w() + [(wsem, wsem.v), (lg_, lgv)] +
                                (gqf.need_r() if (md == 0 and t5 == 0) else []))
                         if kb == 0 else (), inc=mm)
                gq[ri].did_w(mm)
                p.op("vector", "tensor_copy", QF[:, md * T + t5 * 512:md * T + (t5 + 1) * 512],
                     pq[:, reg:reg + 512], waits=[(mm, mm.v)], inc=dv)
                p.drain("vector")
                gq[ri].did_r(dv)
        gwq[s].did_r(mm)
        gqf.did_w(dv)
        for tb in range(NTB):
            si = tb % 2
            reg = si * 512
            for md in range(4):
                p.op("tensor", "matmul", psim[:, reg:reg + 128],
                     ap_v(QF, md * T + tb * 128, [[4 * T, 128], [1, 128]]),
                     KY[:, s * 512 + md * 128:s * 512 + (md + 1) * 128],
                     start=(md == 0), stop=(md == 3),
                     waits=(gsim[si].need_w() + [(ksem, ksem.v), (dv, dv.v)])
                     if md == 0 else (), inc=mm)
            gsim[si].did_w(mm)
            gky[s].did_r(mm)
            p.op("vector", "tensor_copy", SIMT[:, si * 128:(si + 1) * 128],
                 psim[:, reg:reg + 128],
                 waits=[(mm, mm.v)] + gsimt[si].need_w(), inc=dv)
            p.drain("vector")
            gsim[si].did_r(dv)
            gsimt[si].did_w(dv)
            sv = SIMT[:, si * 128:(si + 1) * 128]
            col = phh * NTB + tb
            p.op("vector", "tensor_reduce", mxc[:, :], sv, AX.X, OP.max, inc=dv)
            p.drain("vector")
            p.op("vector", "tensor_copy", SXa[:, col:col + 1], mxc[:, :], inc=dv)
            p.drain("vector")
            p.op("vector", "tensor_scalar", msks[:, :], sv, mxc[0:128, 0:1], None,
                 OP.is_ge, inc=dv)
            p.drain("vector")
            p.op("vector", "scalar_tensor_tensor", msks[:, :], msks[:, :],
                 -1000000.0, iotk[:, :], OP.mult, OP.add,
                 waits=[(gp, gp.v)], inc=dv)
            p.drain("vector")
            p.op("vector", "tensor_reduce", KXa[:, col:col + 1], msks[:, :],
                 AX.X, OP.min, inc=dv)
            p.drain("vector")
            gsimt[si].did_r(dv)
    phA_dv = dv.v

    for tb in range(NTB):
        t0 = tb * 128
        p.op("gpsimd", "dma_start", out=XM[:, :], in_=xn1[t0:t0 + 128, :],
             waits=gxm.need_w(), inc=lxm, incn=16)
        gxm.did_w(lxm)
        sx0 = ap_v(SXa, 0 * NTB + tb, [[16 * NTB, 128], [NTB, 8]])
        sx1 = ap_v(SXa, 8 * NTB + tb, [[16 * NTB, 128], [NTB, 8]])
        kx0 = ap_v(KXa, 0 * NTB + tb, [[16 * NTB, 128], [NTB, 8]])
        kx1 = ap_v(KXa, 8 * NTB + tb, [[16 * NTB, 128], [NTB, 8]])
        p.op("vector", "tensor_tensor", GT[:, :], sx0, sx1, OP.add,
             waits=[(dv, phA_dv)] + ([(gb0, gb0.v)] if tb > 0 else []), inc=dv)
        p.drain("vector")
        p.op("vector", "tensor_scalar", GT[:, :], GT[:, :], 0.0, None, OP.max, inc=dv)
        p.drain("vector")
        p.op("vector", "scalar_tensor_tensor", EXF[:, :], kx0, 128.0, kx1,
             OP.mult, OP.add, inc=dv)
        p.drain("vector")
        p.op("vector", "tensor_copy", EXI[:, :], EXF[:, :], inc=dv)
        p.drain("vector")
        exdone = dv.v

        for ob in range(4):
            ri = ob % 2
            for kb in range(8):
                p.op("tensor", "matmul", pwo[:, ri * 512:(ri + 1) * 512],
                     ap_v(XT, kb * T + t0, [[8 * T, 128], [1, 128]]),
                     ap_v(WIN, kb * DI + ob * 512, [[8 * DI, 128], [1, 512]]),
                     start=(kb == 0), stop=(kb == 7),
                     waits=(gwo[ri].need_w() + [(lg_, lgv)]) if kb == 0 else (), inc=mm)
            gwo[ri].did_w(mm)
            p.op("vector", "tensor_copy", hid[:, ob * 512:(ob + 1) * 512],
                 pwo[:, ri * 512:(ri + 1) * 512],
                 waits=[(mm, mm.v)] + (ghid.need_w() if ob == 0 else []), inc=dv)
            p.drain("vector")
            gwo[ri].did_r(dv)
        ghid.did_w(dv)
        for h in range(8):
            p.op("gpsimd", "indirect_dma_start", out=AR[:, 0:DM], out_offset=None,
                 in_=in_a[:, :],
                 in_offset=bass.IndirectOffsetOnAxis(ap=EXI[:, h:h + 1], axis=0),
                 waits=gar.need_w() + [(dv, exdone)], inc=ga0, incn=16)
            gar.did_w(ga0)
            p.op("gpsimd", "indirect_dma_start", out=BR[:, :], out_offset=None,
                 in_=in_b[:, :],
                 in_offset=bass.IndirectOffsetOnAxis(ap=EXI[:, h:h + 1], axis=0),
                 waits=gbr.need_w(), inc=gb0, incn=16)
            gbr.did_w(gb0)
            p.op("vector", "tensor_tensor", scr2[:, 0:DM], XM[:, :], AR[:, 0:DM],
                 OP.mult, waits=[(ga0, ga0.v), (lxm, lxm.v)], inc=dv)
            p.drain("vector")
            gar.did_r(dv)
            p.op("vector", "tensor_reduce", lin[:, h:h + 1], scr2[:, 0:DM], AX.X,
                 OP.add, inc=dv)
            p.drain("vector")
            p.op("vector", "tensor_tensor", lin[:, h:h + 1], lin[:, h:h + 1],
                 GT[:, h:h + 1], OP.mult, inc=dv)
            p.drain("vector")
            p.op("vector", "scalar_tensor_tensor", hid[:, :], BR[:, :],
                 lin[0:128, h:h + 1], hid[:, :], OP.mult, OP.add,
                 waits=[(gb0, gb0.v)], inc=dv)
            p.drain("vector")
            gbr.did_r(dv)
        p.op("scalar", "activation", hidb[:, :], hid[:, :], AF.Gelu,
             waits=[(dv, dv.v)], inc=sc)
        p.drain("scalar")
        ghid.did_w(sc)
        for kb in range(16):
            dstp = ptp if kb % 2 == 0 else ptp2
            gtj = gtp[kb % 2]
            p.op("tensor", "transpose", dstp[:, 0:128],
                 hidb[:, kb * 128:(kb + 1) * 128], identb[:, :],
                 waits=gtj.need_w() + [(sc, sc.v), (dv, dv.v)], inc=mm)
            gtj.did_w(mm)
            p.op("vector", "tensor_copy", hT[:, kb * 128:(kb + 1) * 128],
                 dstp[:, 0:128], waits=[(mm, mm.v)], inc=dv)
            p.drain("vector")
            gtj.did_r(dv)
        for ob in range(2):
            ri = ob % 2
            for kb in range(16):
                p.op("tensor", "matmul", pwo[:, ri * 512:(ri + 1) * 512],
                     hT[:, kb * 128:(kb + 1) * 128],
                     ap_v(WOU, kb * DM + ob * 512, [[16 * DM, 128], [1, 512]]),
                     start=(kb == 0), stop=(kb == 15),
                     waits=(gwo[ri].need_w() + [(dv, dv.v), (lg_, lgv)])
                     if kb == 0 else (), inc=mm)
            gwo[ri].did_w(mm)
            p.op("vector", "tensor_copy", outt[:, ob * 512:(ob + 1) * 512],
                 pwo[:, ri * 512:(ri + 1) * 512],
                 waits=[(mm, mm.v)] + (gout.need_w() if ob == 0 else []), inc=dv)
            p.drain("vector")
            gwo[ri].did_r(dv)
        gout.did_w(dv)
        for h in range(8):
            p.op("gpsimd", "indirect_dma_start", out=AR[:, :], out_offset=None,
                 in_=out_a[:, :],
                 in_offset=bass.IndirectOffsetOnAxis(ap=EXI[:, h:h + 1], axis=0),
                 waits=gar.need_w() + [(dv, dv.v)], inc=ga0, incn=16)
            gar.did_w(ga0)
            p.op("gpsimd", "indirect_dma_start", out=BR[:, 0:DM], out_offset=None,
                 in_=out_b[:, :],
                 in_offset=bass.IndirectOffsetOnAxis(ap=EXI[:, h:h + 1], axis=0),
                 waits=gbr.need_w(), inc=gb0, incn=16)
            gbr.did_w(gb0)
            p.op("vector", "tensor_tensor", scr2[:, :], hidb[:, :], AR[:, :],
                 OP.mult, waits=[(ga0, ga0.v), (sc, sc.v)], inc=dv)
            p.drain("vector")
            gar.did_r(dv)
            p.op("vector", "tensor_reduce", lin[:, h:h + 1], scr2[:, :], AX.X,
                 OP.add, inc=dv)
            p.drain("vector")
            p.op("vector", "tensor_tensor", lin[:, h:h + 1], lin[:, h:h + 1],
                 GT[:, h:h + 1], OP.mult, inc=dv)
            p.drain("vector")
            p.op("vector", "scalar_tensor_tensor", outt[:, :], BR[:, 0:DM],
                 lin[0:128, h:h + 1], outt[:, :], OP.mult, OP.add,
                 waits=[(gb0, gb0.v)], inc=dv)
            p.drain("vector")
            gbr.did_r(dv)
        gout.did_w(dv)
        p.op("sync", "dma_start", out=pkm_o[t0:t0 + 128, :], in_=outt[:, :],
             waits=[(dv, dv.v)], inc=so, incn=16)
        gout.did_r(so)
        gxm.did_r(dv)
        ghid.did_r(dv)
    for s2 in (so, ld, lg_, lk0, lk1, lwq0, lwq1, lxm, ga0, gb0):
        p.wait("sync", s2, s2.v)


def pkm_io_dram(p, T=2048):
    return {
        "xn1T": p.dram("xn1T_p", [DM, T]),
        "xn1": p.dram("xn1_p", [T, DM]),
        "wqT": p.dram("wqT", [DM, 8192]),
        "keysr": p.dram("keysr", [2 * 8 * 512, 128]),
        "winT": p.dram("winT", [DM, DI]),
        "woutT": p.dram("woutT", [DI, DM]),
        "in_a": p.dram("in_a", [16384, DM]),
        "in_b": p.dram("in_b", [16384, DI]),
        "out_a": p.dram("out_a", [16384, DI]),
        "out_b": p.dram("out_b", [16384, DM]),
        "pkm": p.dram("pkm", [T, DM], kind="ExternalOutput"),
    }


def build_pkm(T=2048):
    p = P("pkm")
    emit_pkm(p, pkm_io_dram(p), T)
    return p.build()


def build_launchC(T=2048):
    p = P("launchC")
    io_c = combine_io_dram(p, T, with_xn1T=True, internal_xn1=True)
    io_l = {
        "xn1T": io_c["xn1T"],
        "xn1": io_c["xn1"],
        "wg": p.dram("wg", [DM, 1]),
        "bg": p.dram("bg", [1, 1]),
        "H": p.dram("H", [T, DM], kind="ExternalOutput"),
        "clr": p.dram("clr", [1, T], kind="ExternalOutput"),
        "hlast": p.dram("hlast", [1, DM], kind="ExternalOutput"),
    }
    io_p = {
        "xn1T": io_c["xn1T"],
        "xn1": io_c["xn1"],
        "wqT": p.dram("wqT", [DM, 8192]),
        "keysr": p.dram("keysr", [2 * 8 * 512, 128]),
        "winT": p.dram("winT", [DM, DI]),
        "woutT": p.dram("woutT", [DI, DM]),
        "in_a": p.dram("in_a", [16384, DM]),
        "in_b": p.dram("in_b", [16384, DI]),
        "out_a": p.dram("out_a", [16384, DI]),
        "out_b": p.dram("out_b", [16384, DM]),
        "pkm": p.dram("pkm", [T, DM], kind="ExternalOutput"),
    }
    p.push_scope()
    emit_moe_combine(p, io_c, T)
    p.pop_scope()
    p.barrier()
    p.push_scope()
    emit_l1_local(p, io_l, T)
    p.pop_scope()
    p.barrier()
    p.push_scope()
    emit_pkm2(p, io_p, T)
    p.pop_scope()
    p.barrier()
    return p.build()



def emit_l1_fin(p, io, T=2048):
    NTB = T // 128
    H_i = io["H"]
    clr_i = io["clr"]
    clrall_i = io["clrall"]
    hlasts_i = io["hlasts"]
    mrow_i = io["mrow"]
    irow_i = io["irow"]
    woT = io["woT"]
    bout_i = io["bout"]
    pkm_i = io["pkm"]
    x1_i = io["x1"]
    out_o = io["out"]
    clend_i = ap_v_dram(clrall_i, T - 1, [[1, 1], [T, 8]])

    WO = p.sb("WOf", [128, 8 * DM], BF16)
    HL = p.sb("HLf", [8, DM])
    Tac = p.sb("Tacf", [1, DM])
    hinb = p.sb("hinbf", [128, DM])
    clend = p.sb("clendf", [1, 8])
    mrow = p.sb("mrowf", [1, 8])
    irow = p.sb("irowf", [1, 8])
    dec = p.sb("decf", [1, 8])
    Htile = p.sb("Htilef", [128, DM])
    HTt = p.sb("HTtf", [128, 8 * 128], BF16)
    clcol = p.sb("clcolf", [128, 1])
    ecol = p.sb("ecolf", [128, 1])
    ptile = p.sb("ptilef", [128, DM])
    x1t = p.sb("x1tf", [128, DM])
    sout = p.sb("soutf", [128, DM])
    brow_ = p.sb("browf", [1, DM])
    onesr = p.sb("onesrf", [1, 128])
    identb = p.sb("identbf", [128, 128], BF16)
    iot = p.sb("iotf", [128, 128])

    hb0 = p.sb("hb0f", [128, DM], BF16)
    hb1 = p.sb("hb1f", [128, DM], BF16)
    pw = p.ps("pwf", [128, 1024])
    ptA = p.ps("ptAf", [128, 1024], BF16)
    ptB = p.ps("ptBf", [128, 1024], BF16)

    ld = p.sem("ld")
    lg_ = p.sem("lg")
    lt0 = p.sem("lt0")
    dv = p.sem("dv")
    sc = p.sem("sc")
    gp = p.sem("gp")
    mm = p.sem("mm")
    so = p.sem("so")

    gpw = Guard()
    gtp = [Guard(), Guard()]
    gTa = Guard()
    gHt = Guard()
    gso = Guard()

    p.op("gpsimd", "memset", onesr[:, :], 1.0, inc=gp)
    p.op("gpsimd", "memset", Tac[:, :], 0.0, inc=gp)
    p.op("gpsimd", "iota", iot[:, :], pattern=[[1, 128]], base=0,
         channel_multiplier=-1, allow_small_or_imprecise_dtypes=True, inc=gp)
    p.op("vector", "tensor_scalar", identb[:, :], iot[:, :], 0.0, None, OP.is_equal,
         waits=[(gp, gp.v)], inc=dv)
    p.drain("vector")
    p.op("gpsimd", "dma_start", out=WO[:, :].rearrange("p (k o) -> p k o", k=8),
         in_=woT.rearrange("(k p) o -> p k o", p=128), inc=lg_, incn=16)
    p.op("sync", "dma_start", out=HL[:, :], in_=hlasts_i[:, :], inc=ld, incn=16)
    p.op("sync", "dma_start", out=clend[:, :], in_=clend_i, inc=ld, incn=16,
         allow_slow_non_contiguous=True)
    p.op("sync", "dma_start", out=mrow[:, :], in_=mrow_i[:, :], inc=ld, incn=16)
    p.op("sync", "dma_start", out=irow[:, :], in_=irow_i[:, :], inc=ld, incn=16)
    p.op("sync", "dma_start", out=brow_[:, :], in_=bout_i[:, :], inc=ld, incn=16)
    ldv = ld.v

    p.op("scalar", "activation", dec[:, :], clend[:, :], AF.Exp, waits=[(ld, ldv)],
         inc=sc)
    p.drain("scalar")
    p.op("vector", "tensor_tensor", dec[:, :], dec[:, :], mrow[:, :], OP.mult,
         waits=[(sc, sc.v)], inc=dv)
    p.drain("vector")
    p.op("vector", "tensor_tensor", dec[:, :], dec[:, :], irow[:, :], OP.add, inc=dv)
    p.drain("vector")
    wrow = p.sb("wrowf", [1, 8])
    sufp = p.sb("sufpf", [1, 8])
    onesr8 = p.sb("onesr8f", [1, 8])
    p.op("gpsimd", "memset", onesr8[:, :], 1.0, inc=gp)
    zrow8 = p.sb("zrow8f", [1, 8])
    p.op("gpsimd", "memset", zrow8[:, :], 0.0, inc=gp)
    rdec = ap_v(dec, 7, [[8, 1], [-1, 8]])
    rsuf = ap_v(sufp, 7, [[8, 1], [-1, 8]])
    p.op("vector", "tensor_tensor_scan", rsuf, rdec, zrow8[:, :], 1.0,
         OP.mult, OP.add, waits=[(dv, dv.v), (gp, gp.v)], inc=dv)
    p.drain("vector")
    p.op("vector", "tensor_copy", wrow[0:1, 0:7], sufp[0:1, 1:8], inc=dv)
    p.drain("vector")
    p.op("vector", "tensor_copy", wrow[0:1, 7:8], onesr8[0:1, 0:1], inc=dv)
    p.drain("vector")
    p.op("vector", "tensor_tensor", wrow[:, :], wrow[:, :], mrow[:, :], OP.mult,
         inc=dv)
    p.drain("vector")
    id8f = p.sb("id8ff", [8, 8])
    p.op("gpsimd", "iota", id8f[:, :], pattern=[[1, 8]], base=0,
         channel_multiplier=-1, allow_small_or_imprecise_dtypes=True, inc=gp)
    p.op("vector", "tensor_scalar", id8f[:, :], id8f[:, :], 0.0, None, OP.is_equal,
         waits=[(gp, gp.v)], inc=dv)
    p.drain("vector")
    wcol = p.sb("wcolf", [8, 1])
    p.op("tensor", "transpose", pw[0:8, 0:1], wrow[0:1, 0:8], id8f[0:1, 0:1],
         waits=gpw.need_w() + [(dv, dv.v)], inc=mm)
    gpw.did_w(mm)
    p.op("vector", "tensor_copy", wcol[:, :], pw[0:8, 0:1], waits=[(mm, mm.v)], inc=dv)
    p.drain("vector")
    gpw.did_r(dv)
    p.op("tensor", "matmul", pw[0:1, 0:DM - 512] if False else pw[0:1, 0:512],
         wcol[0:8, 0:1], HL[0:8, 0:512], start=True, stop=True,
         waits=gpw.need_w() + [(dv, dv.v), (ld, ldv)], inc=mm)
    p.op("tensor", "matmul", pw[0:1, 512:1024], wcol[0:8, 0:1], HL[0:8, 512:1024],
         start=True, stop=True, inc=mm)
    gpw.did_w(mm)
    p.op("vector", "tensor_copy", Tac[:, :], pw[0:1, 0:1024], waits=[(mm, mm.v)],
         inc=dv)
    p.drain("vector")
    gpw.did_r(dv)
    for i in range(2):
        p.op("tensor", "matmul", pw[:, i * 512:(i + 1) * 512], onesr[0:1, 0:128],
             Tac[0:1, i * 512:(i + 1) * 512], start=True, stop=True,
             waits=gpw.need_w() + [(dv, dv.v)] if i == 0 else (), inc=mm)
    gpw.did_w(mm)
    p.op("vector", "tensor_copy", hinb[:, :], pw[:, 0:1024], waits=[(mm, mm.v)], inc=dv)
    p.drain("vector")
    gpw.did_r(dv)
    hindone = dv.v

    for tb in range(NTB):
        t0 = tb * 128
        p.op("sync", "dma_start", out=Htile[:, :], in_=H_i[t0:t0 + 128, :],
             waits=[(so, so.v), (mm, mm.v)] if tb > 0 else [], inc=lt0, incn=16)
        p.op("sync", "dma_start", out=clcol[:, :],
             in_=ap_v_dram(clr_i, t0, [[1, 128], [1, 1]]),
             inc=lt0, incn=16)
        p.op("sync", "dma_start", out=ptile[:, :], in_=pkm_i[t0:t0 + 128, :],
             inc=lt0, incn=16)
        p.op("sync", "dma_start", out=x1t[:, :], in_=x1_i[t0:t0 + 128, :],
             inc=lt0, incn=16)
        ltv = lt0.v
        p.op("scalar", "activation", ecol[:, :], clcol[:, :], AF.Exp,
             waits=[(lt0, ltv)], inc=sc)
        p.drain("scalar")
        p.op("vector", "scalar_tensor_tensor", Htile[:, :], hinb[:, :],
             ecol[0:128, 0:1], Htile[:, :], OP.mult, OP.add,
             waits=[(sc, sc.v), (dv, hindone)], inc=dv)
        p.drain("vector")
        p.op("scalar", "activation", ptile[:, :], ptile[:, :], AF.Silu, inc=sc)
        p.drain("scalar")
        hb = hb0 if tb % 2 == 0 else hb1
        p.op("vector", "tensor_copy", hb[:, :], Htile[:, :], inc=dv)
        p.drain("vector")
        for kb in range(8):
            dstp = ptA if kb % 2 == 0 else ptB
            gtj = gtp[kb % 2]
            p.op("tensor", "transpose", dstp[:, 0:128],
                 hb[:, kb * 128:(kb + 1) * 128], identb[:, :],
                 waits=gtj.need_w() + [(dv, dv.v)], inc=mm)
            gtj.did_w(mm)
            p.op("vector", "tensor_copy", HTt[:, kb * 128:(kb + 1) * 128],
                 dstp[:, 0:128], waits=[(mm, mm.v)] + (gHt.need_w() if kb == 0 else []),
                 inc=dv)
            p.drain("vector")
            gtj.did_r(dv)
        gHt.did_w(dv)
        for nb in range(2):
            for kb in range(8):
                p.op("tensor", "matmul", pw[:, nb * 512:(nb + 1) * 512],
                     HTt[:, kb * 128:(kb + 1) * 128],
                     ap_v(WO, kb * DM + nb * 512, [[8 * DM, 128], [1, 512]]),
                     start=(kb == 0), stop=False,
                     waits=(gpw.need_w() + [(dv, dv.v), (lg_, lg_.v)]) if (nb == 0 and kb == 0) else (),
                     inc=mm)
            p.op("tensor", "matmul", pw[:, nb * 512:(nb + 1) * 512],
                 onesr[0:1, 0:128], brow_[0:1, nb * 512:(nb + 1) * 512],
                 start=False, stop=True, waits=[(ld, ldv)], inc=mm)
        gpw.did_w(mm)
        gHt.did_r(mm)
        p.op("vector", "tensor_tensor", sout[:, :], pw[:, 0:1024], ptile[:, :],
             OP.mult, waits=[(mm, mm.v), (sc, sc.v)], inc=dv)
        p.drain("vector")
        gpw.did_r(dv)
        p.op("vector", "tensor_tensor", sout[:, :], sout[:, :], x1t[:, :], OP.add,
             inc=dv)
        p.drain("vector")
        p.op("sync", "dma_start", out=out_o[t0:t0 + 128, :], in_=sout[:, :],
             waits=[(dv, dv.v)], inc=so, incn=16)
    for s2 in (so, ld, lg_, lt0):
        p.wait("sync", s2, s2.v)



def build_launchA(T=2048):
    RH = 4096
    N2 = 2 * T
    p = P("launchA")
    io_pre = {
        "x": p.dram("x", [T, DM]),
        "xh": p.dram("xh", [8, DM]),
        "nw": p.dram("nw", [1, DM]),
        "xn": p.dram("xn", [T, DM], kind="ExternalOutput"),
        "xnT": p.dram("xnT", [DM, T], BF16, kind="Internal"),
        "x2Tf": p.dram("x2Tf", [DM, T], BF16, kind="Internal"),
        "x2Tb": p.dram("x2Tb", [DM, T], BF16, kind="Internal"),
        "x2hTf": p.dram("x2hTf", [DM, 4], BF16, kind="Internal"),
        "x2hTb": p.dram("x2hTb", [DM, 4], BF16, kind="Internal"),
    }
    io_m = {}
    for d in ("f", "b"):
        io_m[d] = {
            "x2T": io_pre["x2T" + d],
            "x2hT": io_pre["x2hT" + d],
            "wInT": p.dram("wInT_" + d, [DM, DIP]),
            "cw": p.dram("cw_" + d, [CONVD, 4]),
            "cbr": p.dram("cbr_" + d, [18, 128]),
            "dtb": p.dram("dtb_" + d, [NH, 1]),
            "alog": p.dram("alog_" + d, [NH, 1]),
            "dexpr": p.dram("dexpr_" + d, [1, DI]),
            "yloc": p.dram("yloc_" + d, [T, DI], kind="ExternalOutput"),
            "ztm": p.dram("ztm_" + d, [T, DI], BF16, kind="ExternalOutput"),
            "cfm": p.dram("cfm_" + d, [DST, T], BF16, kind="ExternalOutput"),
            "agT": p.dram("agT_" + d, [T, NH], kind="ExternalOutput"),
            "sout": p.dram("sout_" + d, [DST, DI], BF16, kind="ExternalOutput"),
        }
    io_r = {
        "xnT": io_pre["xnT"],
        "xn": io_pre["xn"],
        "rw1": p.dram("rw1", [DM, RH]),
        "rb1c": p.dram("rb1c", [128, 32]),
        "rw2": p.dram("rw2", [RH, 8]),
        "rb2": p.dram("rb2", [1, 8]),
        "send": p.dram("send", [N2, DM], BF16, kind="ExternalOutput"),
        "counts": p.dram("counts", [8, 1], kind="ExternalOutput"),
        "ids": p.dram("ids", [T, 2], kind="ExternalOutput"),
        "lpos": p.dram("lpos", [T, 2], kind="ExternalOutput"),
        "probs": p.dram("probs", [T, 2], kind="ExternalOutput"),
    }
    p.push_scope()
    emit_l0_pre(p, io_pre, T)
    p.pop_scope()
    p.barrier()
    for d in ("f", "b"):
        p.push_scope()
        emit_mamba_local(p, io_m[d], T, tag=d)
        p.pop_scope()
        p.barrier()
    p.push_scope()
    emit_moe_route(p, io_r, T)
    p.pop_scope()
    p.barrier()
    return p.build()


def build_mega(T=2048):
    RH = 4096
    N2 = 2 * T
    p = P("mega")

    io_pre = {
        "x": p.dram("x", [T, DM]),
        "xh": p.dram("xh", [8, DM]),
        "nw": p.dram("nw", [1, DM]),
        "xn": p.dram("xn", [T, DM], kind="Internal"),
        "xnT": p.dram("xnT", [DM, T], BF16, kind="Internal"),
        "x2Tf": p.dram("x2Tf", [DM, T], BF16, kind="Internal"),
        "x2Tb": p.dram("x2Tb", [DM, T], BF16, kind="Internal"),
        "x2hTf": p.dram("x2hTf", [DM, 4], BF16, kind="Internal"),
        "x2hTb": p.dram("x2hTb", [DM, 4], BF16, kind="Internal"),
    }
    ids = p.dram("ids", [T, 2], kind="Internal")
    lpos = p.dram("lpos", [T, 2], kind="Internal")
    probs = p.dram("probs", [T, 2], kind="Internal")
    send = p.dram("send", [N2, DM], BF16, kind="Internal")
    counts = p.dram("counts", [8, 1], kind="Internal")
    io_r = {
        "xnT": io_pre["xnT"], "xn": io_pre["xn"],
        "rw1": p.dram("rw1", [DM, RH]),
        "rb1c": p.dram("rb1c", [128, 32]),
        "rw2": p.dram("rw2", [RH, 8]),
        "rb2": p.dram("rb2", [1, 8]),
        "send": send, "counts": counts,
        "ids": ids, "lpos": lpos, "probs": probs,
    }
    io_m = {}
    for d in ("f", "b"):
        io_m[d] = {
            "x2T": io_pre["x2T" + d],
            "x2hT": io_pre["x2hT" + d],
            "wInT": p.dram("wInT_" + d, [DM, DIP]),
            "cw": p.dram("cw_" + d, [CONVD, 4]),
            "cbr": p.dram("cbr_" + d, [18, 128]),
            "dtb": p.dram("dtb_" + d, [NH, 1]),
            "alog": p.dram("alog_" + d, [NH, 1]),
            "dexpr": p.dram("dexpr_" + d, [1, DI]),
            "yloc": p.dram("y" + d, [T, DI], kind="Internal"),
            "ztm": p.dram("z" + d, [T, DI], BF16, kind="Internal"),
            "cfm": p.dram("cf" + d, [DST, T], BF16, kind="Internal"),
            "agT": p.dram("ag" + d, [T, NH], kind="Internal"),
            "sout": p.dram("soutloc" + d, [DST, DI], BF16, kind="Internal"),
        }
    sendall = p.dram("sendall", [8 * N2, DM], BF16, kind="Internal")
    countsA = p.dram("countsA", [8, 8], kind="Internal")
    ae = {d: p.dram("ae" + d, [1, 8 * NH], kind="Internal") for d in ("f", "b")}
    so = {d: p.dram("so" + d, [8 * DST, DI], BF16, kind="Internal")
          for d in ("f", "b")}
    io_f = {"projT": p.dram("projT", [2 * DM, DM]),
            "pbias": p.dram("pbias", [1, DM]),
            "xn": io_pre["xn"],
            "s": p.dram("s", [T, DM], kind="Internal")}
    for d in ("f", "b"):
        io_f["y" + d] = io_m[d]["yloc"]
        io_f["z" + d] = io_m[d]["ztm"]
        io_f["cf" + d] = io_m[d]["cfm"]
        io_f["ag" + d] = io_m[d]["agT"]
        io_f["ae" + d] = ae[d]
        io_f["so" + d] = so[d]
        io_f["mrow" + d] = p.dram("mrow" + d, [1, 8])
        io_f["irow" + d] = p.dram("irow" + d, [1, 8])
        io_f["mtil" + d] = p.dram("mtil" + d, [128, 8])
        io_f["wo" + d] = p.dram("wo" + d, [DI, DM])
        io_f["nw" + d] = p.dram("nw" + d, [1, DI])
    ccol = p.dram("ccol", [8, 1], kind="Internal")
    io_e = {
        "sendall": sendall, "countsA": countsA,
        "onehotE": p.dram("onehotE", [8, 8]),
        "ccol": ccol,
        "ew1": p.dram("ew1", [DM, RH]),
        "eb1c": p.dram("eb1c", [128, 32]),
        "ew2": p.dram("ew2", [RH, DM]),
        "eb2": p.dram("eb2", [1, DM]),
        "eo": p.dram("eo", [N2, DM], BF16, kind="Internal"),
    }
    eoall = p.dram("eoall", [8 * N2, DM], BF16, kind="Internal")
    io_c = {
        "eoall": eoall, "countsA": countsA,
        "premask": p.dram("premask", [8, 1]),
        "idsf": ids.rearrange("t k -> (t k)").unsqueeze(0),
        "lposf": lpos.rearrange("t k -> (t k)").unsqueeze(0),
        "probsf": probs.rearrange("t k -> (t k)").unsqueeze(0),
        "s": io_f["s"], "x": io_pre["x"],
        "x1": p.dram("x1", [T, DM], kind="Internal"),
        "xn1": p.dram("xn1", [T, DM], kind="Internal"),
        "xn1T": p.dram("xn1T", [DM, T], BF16, kind="Internal"),
    }
    io_l = {
        "xn1T": io_c["xn1T"], "xn1": io_c["xn1"],
        "wg": p.dram("wg", [DM, 1]),
        "bg": p.dram("bg", [1, 1]),
        "H": p.dram("H", [T, DM], kind="Internal"),
        "clr": p.dram("clr", [1, T], kind="Internal"),
        "hlast": p.dram("hlast", [1, DM], kind="Internal"),
    }
    io_p = {
        "xn1T": io_c["xn1T"], "xn1": io_c["xn1"],
        "wqT": p.dram("wqT", [DM, 8192]),
        "keysr": p.dram("keysr", [2 * 8 * 512, 128]),
        "winT": p.dram("winT", [DM, DI]),
        "woutT": p.dram("woutT", [DI, DM]),
        "in_a": p.dram("in_a", [16384, DM]),
        "in_b": p.dram("in_b", [16384, DI]),
        "out_a": p.dram("out_a", [16384, DI]),
        "out_b": p.dram("out_b", [16384, DM]),
        "pkm": p.dram("pkm", [T, DM], kind="Internal"),
    }
    clrall = p.dram("clrall", [8, T], kind="Internal")
    hlasts = p.dram("hlasts", [8, DM], kind="Internal")
    io_fin = {
        "H": io_l["H"], "clr": io_l["clr"], "clrall": clrall,
        "hlasts": hlasts,
        "mrow": p.dram("mrow1", [1, 8]),
        "irow": p.dram("irow1", [1, 8]),
        "woT": p.dram("woT", [DM, DM]),
        "bout": p.dram("bout", [1, DM]),
        "pkm": io_p["pkm"], "x1": io_c["x1"],
        "out": p.dram("out", [T, DM], kind="ExternalOutput"),
    }

    p.push_scope()
    emit_l0_pre(p, io_pre, T)
    p.pop_scope()
    p.barrier()
    p.push_scope()
    emit_moe_route(p, io_r, T)
    p.pop_scope()
    p.barrier()
    emit_ag(p, send, sendall)
    emit_ag(p, counts, countsA)
    for d, k in (("f", 3), ("b", 5)):
        p.push_scope()
        emit_mamba_local(p, io_m[d], T, tag=d)
        p.pop_scope()
        p.barrier()
        emit_ag(p, io_m[d]["agT"][T - 1:T, :], ae[d])
        emit_ag(p, io_m[d]["sout"], so[d])
    wait_cc(p, 6)
    p.push_scope()
    cA8 = p.sb("ccglue_cA", [8, 8])
    oh8 = p.sb("ccglue_oh", [8, 8])
    cc1 = p.sb("ccglue_c", [8, 1])
    gld = p.sem("ld")
    gdv = p.sem("dv")
    gso = p.sem("so")
    p.op("sync", "dma_start", out=cA8[:, :], in_=countsA[:, :], inc=gld, incn=16)
    p.op("sync", "dma_start", out=oh8[:, :], in_=io_e["onehotE"][:, :],
         inc=gld, incn=16)
    p.op("vector", "tensor_tensor", cA8[:, :], cA8[:, :], oh8[:, :], OP.mult,
         waits=[(gld, gld.v)], inc=gdv)
    p.drain("vector")
    p.op("vector", "tensor_reduce", cc1[:, :], cA8[:, :], AX.X, OP.add, inc=gdv)
    p.drain("vector")
    p.op("sync", "dma_start", out=ccol[:, :], in_=cc1[:, :],
         waits=[(gdv, gdv.v)], inc=gso, incn=16)
    p.wait("sync", gso, gso.v)
    p.pop_scope()
    p.barrier()
    p.push_scope()
    emit_mamba_fin(p, io_f, T)
    p.pop_scope()
    p.barrier()
    p.push_scope()
    emit_moe_expert(p, io_e)
    p.pop_scope()
    p.barrier()
    emit_ag(p, io_e["eo"], eoall)
    wait_cc(p, 7)
    p.push_scope()
    emit_moe_combine(p, io_c, T)
    p.pop_scope()
    p.barrier()
    p.push_scope()
    emit_l1_local(p, io_l, T)
    p.pop_scope()
    p.barrier()
    emit_ag(p, io_l["clr"], clrall)
    emit_ag(p, io_l["hlast"], hlasts)
    p.push_scope()
    emit_pkm2(p, io_p, T)
    p.pop_scope()
    p.barrier()
    wait_cc(p, 9)
    p.push_scope()
    emit_l1_fin(p, io_fin, T)
    p.pop_scope()
    p.barrier()
    return p.build()



def _split_tokens(a, halo=False):
    L = a.shape[0]
    T = L // NCORES
    return [np.ascontiguousarray(a[i * T:(i + 1) * T]) for i in range(NCORES)]


def _halo_rows(x):
    L = x.shape[0]
    T = L // NCORES
    out = []
    for i in range(NCORES):
        h = np.zeros((8, x.shape[1]), np.float32)
        s = i * T
        e = (i + 1) * T
        if s >= 3:
            h[0:3] = x[s - 3:s]
        elif s > 0:
            h[3 - s:3] = x[0:s]
        nb = min(3, L - e)
        if nb > 0:
            h[4:4 + nb] = x[e:e + nb][::-1][-nb:] if False else x[e:e + nb][::-1]
        out.append(h)
    return out


def _runs(nc, in_maps):
    return run_bass_kernel_spmd(nc, in_maps, core_ids=list(range(NCORES))).results


def _t(a):
    return np.ascontiguousarray(np.asarray(a, np.float32).T)


def _c(a):
    return np.ascontiguousarray(np.asarray(a, np.float32))


def kernel(**inputs):
    L = 16384
    T = L // NCORES
    f32 = np.float32
    x = _c(inputs["L0_attn_x"][0])
    xs_ = _split_tokens(x)
    halos = _halo_rows(x)
    nw = _c(inputs["L0_attn_norm_weight"]).reshape(1, DM)

    mprm = {
        "f": dict(W_in=inputs["L0_attn_fwd_W_in"], conv_w=inputs["L0_attn_fwd_conv_w"],
                  conv_b=inputs["L0_attn_fwd_conv_b"], dt_bias=inputs["L0_attn_fwd_dt_bias"],
                  A_log=inputs["L0_attn_fwd_A_log"], D=inputs["L0_attn_fwd_D"],
                  norm_w=inputs["L0_attn_fwd_norm_w"], W_out=inputs["L0_attn_fwd_W_out"]),
        "b": dict(W_in=inputs["L0_attn_bwd_W_in"], conv_w=inputs["L0_attn_bwd_conv_w"],
                  conv_b=inputs["L0_attn_bwd_conv_b"], dt_bias=inputs["L0_attn_bwd_dt_bias"],
                  A_log=inputs["L0_attn_bwd_A_log"], D=inputs["L0_attn_bwd_D"],
                  norm_w=inputs["L0_attn_bwd_norm_w"], W_out=inputs["L0_attn_bwd_W_out"]),
    }
    shared = {
        "nw": nw,
        "rw1": _c(inputs["L0_ffn_rw1"]),
        "rb1c": np.ascontiguousarray(_c(inputs["L0_ffn_rb1"]).reshape(32, 128).T),
        "rw2": _c(inputs["L0_ffn_rw2"]),
        "rb2": _c(inputs["L0_ffn_rb2"]).reshape(1, 8),
        "projT": _t(inputs["L0_attn_proj_W"]),
        "pbias": _c(inputs["L0_attn_proj_b"]).reshape(1, DM),
        "wg": _t(inputs["L1_attn_Wg"]),
        "bg": _c(inputs["L1_attn_bg"]).reshape(1, 1),
        "wqT": _t(inputs["L1_ffn_Wq"]),
        "keysr": np.ascontiguousarray(
            _c(inputs["L1_ffn_keys"]).transpose(2, 0, 3, 1).reshape(2 * 8 * 512, 128)),
        "winT": _t(inputs["L1_ffn_W_in"]),
        "woutT": _t(inputs["L1_ffn_W_out"]),
        "in_a": _c(inputs["L1_ffn_in_a"]),
        "in_b": _c(inputs["L1_ffn_in_b"]),
        "out_a": _c(inputs["L1_ffn_out_a"]),
        "out_b": _c(inputs["L1_ffn_out_b"]),
        "woT": _t(inputs["L1_attn_Wout"]),
        "bout": _c(inputs["L1_attn_bout"]).reshape(1, DM),
    }
    for d in ("f", "b"):
        pp = mprm[d]
        shared["wInT_" + d] = _t(pp["W_in"])
        shared["cw_" + d] = _c(pp["conv_w"])
        shared["cbr_" + d] = _c(pp["conv_b"]).reshape(18, 128)
        shared["dtb_" + d] = _c(pp["dt_bias"]).reshape(NH, 1)
        shared["alog_" + d] = _c(pp["A_log"]).reshape(NH, 1)
        shared["dexpr_" + d] = np.ascontiguousarray(
            np.repeat(_c(pp["D"]), HD).reshape(1, DI))
        shared["wo" + d] = _t(pp["W_out"])
        shared["nw" + d] = _c(pp["norm_w"]).reshape(1, DI)

    nc = build_mega(T)
    ims = []
    for i in range(NCORES):
        im = dict(shared)
        im["x"] = xs_[i]
        im["xh"] = halos[i]
        for d in ("f", "b"):
            mask = (np.arange(8) < i) if d == "f" else (np.arange(8) > i)
            mask = mask.astype(f32)
            im["mrow" + d] = mask.reshape(1, 8)
            im["irow" + d] = (1 - mask).reshape(1, 8)
            im["mtil" + d] = np.tile(mask, (128, 1))
        oh = np.zeros((8, 8), f32)
        oh[:, i] = 1.0
        im["onehotE"] = oh
        im["ew1"] = _c(inputs["L0_ffn_ew1"][i])
        im["eb1c"] = np.ascontiguousarray(_c(inputs["L0_ffn_eb1"][i]).reshape(32, 128).T)
        im["ew2"] = _c(inputs["L0_ffn_ew2"][i])
        im["eb2"] = _c(inputs["L0_ffn_eb2"][i]).reshape(1, DM)
        im["premask"] = (np.arange(8) < i).astype(f32).reshape(8, 1)
        m = (np.arange(8) < i).astype(f32)
        im["mrow1"] = m.reshape(1, 8)
        im["irow1"] = (1 - m).reshape(1, 8)
        ims.append(im)
    res = _runs(nc, ims)
    out = np.concatenate([r["out"] for r in res], 0)
    return out.reshape(1, L, DM)
